# version 10
# baseline (speedup 1.0000x reference)
from contextlib import ExitStack

import numpy as np

import concourse.bass as bass
import concourse.mybir as mybir
from concourse.bass_utils import run_bass_kernel_spmd

F32 = mybir.dt.float32
BF16 = mybir.dt.bfloat16
I32 = mybir.dt.int32
AF = mybir.ActivationFunctionType
ALU = mybir.AluOpType
AX = mybir.AxisListType

NCORES = 8
SEQ, D = 16384, 2048
TCORE = SEQ // NCORES
HALO = 8
TH = TCORE + HALO
DIN, H, NST, DH = 4096, 16, 16, 256
HID = 8192
EPS_LN = 1e-5
EPS_RMS = 1.1920929e-07
NDC = D // 128
NTB = TCORE // 512
NTG = TCORE // 128
WINP = 8960
WIN_SH = WINP // NCORES
N_WCH = 69

_PHASES = ("s0", "s1", "s2", "s3", "s4a", "s4b", "s5")
ACCUM_ON = True

ENGS = ("pe", "act", "dve", "pool", "sp")
DMA_ROT = 8


class Op:
    __slots__ = ("eng", "kind", "emit", "reads", "writes", "deps", "inc_id",
                 "dma_idx", "tag", "hz")

    def __init__(self, eng, kind, emit, reads, writes, tag="", hz=False):
        self.eng, self.kind, self.emit = eng, kind, emit
        self.reads, self.writes, self.tag = tuple(reads), tuple(writes), tag
        self.deps = self.inc_id = self.dma_idx = None
        self.hz = hz


class Prog:

    def __init__(self):
        self.ops = []

    def c(self, eng, emit, reads=(), writes=(), tag="", hz=False):
        self.ops.append(Op(eng, "c", emit, reads, writes, tag, hz))

    def d(self, queue, emit, reads=(), writes=(), tag=""):
        self.ops.append(Op(queue, "d", emit, reads, writes, tag))

    def cc(self, emit, reads=(), writes=(), tag=""):
        self.ops.append(Op("pool", "cc", emit, reads, writes, tag))

    HZ_GAP = 10**9

    def build(self, nc, block, ctx):
        ops = self.ops
        pos = {}
        ecnt = {e: 0 for e in ENGS}
        for i, op in enumerate(ops):
            pos[i] = ecnt[op.eng]
            ecnt[op.eng] += 1
        last_w, readers, need_inc = {}, {}, set()
        for i, op in enumerate(ops):
            dep = set()
            for k in op.reads:
                if k in last_w:
                    dep.add(last_w[k])
            for k in op.writes:
                if k in last_w:
                    dep.add(last_w[k])
                dep.update(readers.get(k, ()))

            def keep(j, i=i, op=op):
                oj = ops[j]
                if oj.eng != op.eng or oj.kind != "c":
                    return True
                if oj.eng == "pe":
                    return False
                if oj.hz:
                    return True
                return pos[i] - pos[j] <= self.HZ_GAP
            real = {j for j in dep if j != i and keep(j)}
            op.deps = real
            need_inc.update(real)
            for k in op.reads:
                readers.setdefault(k, []).append(i)
            for k in op.writes:
                last_w[k] = i
                readers[k] = []
        eng_ctr = {e: 0 for e in ENGS}
        dma_ctr = {e: 0 for e in ENGS}
        cc_ctr = 0
        for i, op in enumerate(ops):
            if op.kind == "d":
                op.dma_idx = dma_ctr[op.eng]
                dma_ctr[op.eng] += 1
            elif op.kind == "cc":
                cc_ctr += 1
                op.inc_id = cc_ctr
            elif i in need_inc:
                eng_ctr[op.eng] += 1
                op.inc_id = eng_ctr[op.eng]
        esem = {e: ctx.enter_context(nc.semaphore(f"e_{e}")) for e in ENGS}
        dsem = {e: [ctx.enter_context(nc.semaphore(f"d_{e}{r}"))
                    for r in range(DMA_ROT)]
                for e in ENGS if dma_ctr[e]}
        ccsem = ctx.enter_context(nc.semaphore("ccs")) if cc_ctr else None

        def tgt(j):
            oj = ops[j]
            if oj.kind == "d":
                return (dsem[oj.eng][oj.dma_idx % DMA_ROT],
                        16 * (oj.dma_idx // DMA_ROT + 1))
            if oj.kind == "cc":
                return (ccsem, oj.inc_id)
            return (esem[oj.eng], oj.inc_id)

        per_eng = {e: [i for i, op in enumerate(ops) if op.eng == e]
                   for e in ENGS}

        def mk(ename):
            idxs = per_eng[ename]

            def emitfn(eng):
                floors = {}
                for i in idxs:
                    op = ops[i]
                    waits = {}
                    for j in sorted(op.deps):
                        sem, val = tgt(j)
                        if waits.get(sem, -1) < val:
                            waits[sem] = val
                    if op.kind == "d" and op.dma_idx >= DMA_ROT:
                        sem = dsem[ename][op.dma_idx % DMA_ROT]
                        val = 16 * (op.dma_idx // DMA_ROT)
                        if waits.get(sem, -1) < val:
                            waits[sem] = val
                    for sem, val in waits.items():
                        if floors.get(sem, 0) < val:
                            eng.wait_ge(sem, val)
                            floors[sem] = val
                    ins = op.emit(eng)
                    if op.kind == "d":
                        ins.then_inc(dsem[ename][op.dma_idx % DMA_ROT], 16)
                    elif op.kind == "cc":
                        ins.then_inc(ccsem, 1)
                    elif op.inc_id is not None:
                        ins.then_inc(esem[ename], 1)
                if ename == "sp":
                    for e, n in dma_ctr.items():
                        for r in range(DMA_ROT):
                            cnt = (n - r + DMA_ROT - 1) // DMA_ROT
                            if cnt > 0:
                                eng.wait_ge(dsem[e][r], 16 * cnt)
            return emitfn

        names = {"pe": "tensor", "act": "scalar", "dve": "vector",
                 "pool": "gpsimd", "sp": "sync"}
        for e in ENGS:
            if per_eng[e]:
                getattr(block, names[e])(mk(e))


def ts(e, out, in0, s1, s2=None, op0=ALU.mult, op1=ALU.bypass):
    return e.tensor_scalar(out, in0, s1, s2, op0, op1)


def host_inputs(inputs):
    x = np.ascontiguousarray(
        np.asarray(inputs["L0_attn_x"]).reshape(SEQ, D), dtype=np.float32)
    W_in = np.asarray(inputs["L0_attn_W_in"], dtype=np.float32)
    win_perm = np.zeros((WINP, D), np.float32)
    win_perm[0:2 * DIN] = W_in[0:2 * DIN]
    win_perm[8192:8208] = W_in[8192:8208]
    win_perm[8320:8576] = W_in[8208:8464]
    win_perm[8576:8832] = W_in[8464:8720]
    gate_w = np.asarray(inputs["L0_ffn_gate_w"], dtype=np.float32)
    val_w = np.asarray(inputs["L0_ffn_val_w"], dtype=np.float32)
    out_w = np.asarray(inputs["L0_ffn_out_w"], dtype=np.float32)
    wout = np.asarray(inputs["L0_attn_W_out"], dtype=np.float32)
    small = dict(
        w_norm=np.ascontiguousarray(
            np.asarray(inputs["L0_attn_w_norm"], np.float32).reshape(1, D)),
        dt_bias=np.ascontiguousarray(
            np.asarray(inputs["L0_attn_dt_bias"], np.float32).reshape(H, 1)),
        a_log=np.ascontiguousarray(
            np.asarray(inputs["L0_attn_A_log"], np.float32).reshape(H, 1)),
        w_conv=np.ascontiguousarray(
            np.asarray(inputs["L0_attn_w_conv"], np.float32).reshape(DIN, 4)),
        w_h2i=np.ascontiguousarray(
            np.asarray(inputs["L0_attn_W_h2i"], np.float32)),
        gate_b=np.ascontiguousarray(
            np.asarray(inputs["L0_ffn_gate_b"], np.float32).reshape(64, 128)),
        val_b=np.ascontiguousarray(
            np.asarray(inputs["L0_ffn_val_b"], np.float32).reshape(64, 128)),
        out_b=np.ascontiguousarray(
            np.asarray(inputs["L0_ffn_out_b"], np.float32).reshape(1, D)),
    )
    maps = []
    for k in range(NCORES):
        lo = TCORE * k - HALO
        xs = np.zeros((TH, D), np.float32)
        s0_ = max(0, lo)
        xs[s0_ - lo:] = x[s0_:TCORE * (k + 1)]
        m = dict(small)
        m["x_sl"] = xs
        m["halo_on"] = np.full((HALO, 1), 0.0 if k == 0 else 1.0, np.float32)
        cm = np.zeros((128, 8), np.float32)
        cm[:, :k] = 1.0
        m["cmask"] = cm
        m["win_sl"] = np.ascontiguousarray(win_perm[WIN_SH * k:WIN_SH * (k + 1)])
        m["gate_sl"] = np.ascontiguousarray(gate_w[1024 * k:1024 * (k + 1)])
        m["val_sl"] = np.ascontiguousarray(val_w[1024 * k:1024 * (k + 1)])
        m["outw_sl"] = np.ascontiguousarray(out_w[256 * k:256 * (k + 1)])
        m["wout_sl"] = np.ascontiguousarray(wout[256 * k:256 * (k + 1)])
        maps.append(m)
    return maps


def build_nc(taps=(), phases=_PHASES):
    taps = set(taps)
    nc = bass.Bass("TRN2", target_bir_lowering=False)
    pr = Prog()
    ctx = ExitStack()
    nc._kernel_ctx = ctx

    def din(name, shape, dt=F32):
        return nc.dram_tensor(name, list(shape), dt, kind="ExternalInput").ap()

    def dout(name, shape, dt=F32):
        return nc.dram_tensor(name, list(shape), dt, kind="ExternalOutput").ap()

    def dint(name, shape, dt=F32, shared=False):
        return nc.dram_tensor(name, list(shape), dt, kind="Internal",
                              addr_space=("Shared" if shared else "Local")).ap()

    x_sl = din("x_sl", (TH, D))
    halo_on = din("halo_on", (HALO, 1))
    win_sl = din("win_sl", (WIN_SH, D))
    gate_sl = din("gate_sl", (1024, D))
    val_sl = din("val_sl", (1024, D))
    outw_sl = din("outw_sl", (256, HID))
    wout_sl = din("wout_sl", (256, DIN))
    w_norm = din("w_norm", (1, D))
    dt_bias = din("dt_bias", (H, 1))
    a_log = din("a_log", (H, 1))
    cmask_in = din("cmask", (128, 8))
    w_conv = din("w_conv", (DIN, 4))
    w_h2i = din("w_h2i", (DIN, H))
    gate_b = din("gate_b", (64, 128))
    val_b = din("val_b", (64, 128))
    out_b = din("out_b", (1, D))
    out_sl = dout("out_sl", (TCORE, D))

    win_shard = dint("win_shard", (WIN_SH, D), BF16)
    win_q = dint("win_q", (WINP, D), BF16, shared=True)
    mg_shard = dint("mg_shard", (1024, D), BF16)
    mg_q = dint("mg_q", (HID, D), BF16, shared=True)
    mv_shard = dint("mv_shard", (1024, D), BF16)
    mv_q = dint("mv_q", (HID, D), BF16, shared=True)
    mo_shard = dint("mo_shard", (256, HID), BF16)
    mo_q = dint("mo_q", (D, HID), BF16, shared=True)
    wo_shard = dint("wo_shard", (256, DIN), BF16)
    wo_q = dint("wo_q", (D, DIN), BF16, shared=True)
    ag1_in = dint("ag1_in", (1, 4))
    ag1_out = dint("ag1_out", (NCORES, 4), F32, shared=True)
    ag3_in = dint("ag3_in", (1, 512))
    ag3_out = dint("ag3_out", (NCORES, 512), F32, shared=True)
    z_scr = dint("z_scr", (32, 128, TCORE), BF16)
    gv_scr = dint("gv_scr", (64, 128, TCORE))
    xq_scr = dint("xq_scr", (NDC, 128, TCORE), BF16)
    row_b = dint("row_b", (1, 4096))
    ffn_scr = dint("ffn_scr", (TCORE, D))
    attn_scr = dint("attn_scr", (TCORE, D))

    tap_outs = {}

    def tapout(name, shape, dt=F32):
        tap_outs[name] = dout("tap_" + name, shape, dt)
        return tap_outs[name]

    ARENA = 202 * 1024
    _arena = nc.alloc_sbuf_tensor("arena", [128, ARENA], mybir.dt.uint8)
    _abase = nc.lookup_mloc(_arena).addr

    def sb_at(name, shape, dt, off):
        assert off + int(np.prod(shape[1:])) * mybir.dt.size(dt) <= ARENA, (
            name, off)
        return nc.alloc_sbuf_tensor_at(name, list(shape), dt,
                                       offset=_abase + off)

    class Bump:
        def __init__(self, base):
            self.cur = base

        def take(self, name, shape, dt):
            per = int(np.prod(shape[1:])) * mybir.dt.size(dt)
            t = sb_at(name, shape, dt, self.cur)
            self.cur += (per + 31) & ~31
            return t

    PB = Bump(0)
    wn_rep = PB.take("wn_rep", [128, D], F32)
    P0 = PB.take("P0", [128, TCORE], F32)
    P1 = PB.take("P1", [128, TCORE], F32)
    ident_f = PB.take("ident_f", [128, 128], F32)
    ident_bf = PB.take("ident_bf", [128, 128], BF16)
    selT_lo = PB.take("selT_lo", [128, 16], F32)
    selT_hi = PB.take("selT_hi", [128, 16], F32)
    sel_loT = PB.take("sel_loT", [16, 128], F32)
    sel_hiT = PB.take("sel_hiT", [16, 128], F32)
    ones_col = PB.take("ones_col", [128, 1], F32)
    ones_col_bf = PB.take("ones_col_bf", [128, 1], BF16)
    ones_row = PB.take("ones_row", [1, 128], F32)
    xbar = PB.take("xbar", [16, TCORE], F32)
    y_loc = PB.take("y_loc", [16, TCORE], F32)
    yful_bf = PB.take("yful_bf", [16, TCORE], BF16)
    misc = PB.take("misc", [128, 64], F32)
    gcols = PB.take("gcols", [128, 8], F32)
    sx_col = PB.take("sx_col", [128, 1], F32)
    sxi_tok = PB.take("sxi_tok", [128, NTG], F32)
    so_tok = PB.take("so_tok", [128, NTG], F32)
    gg_tok = PB.take("gg_tok", [128, NTG], F32)
    dtb_col = PB.take("dtb_col", [16, 1], F32)
    A_col = PB.take("A_col", [16, 1], F32)
    S_colA = PB.take("S_colA", [128, 2], F32)
    D_colA = PB.take("D_colA", [128, 2], F32)
    hinA = PB.take("hinA", [128, 2], F32)
    agS = PB.take("agS", [128, 32], F32)
    wc_sb = PB.take("wc_sb", [128, 128], F32)
    cmask = PB.take("cmask_sb", [128, 8], F32)
    xbar_row = PB.take("xbar_row", [1, TCORE], F32)
    dgs = [PB.take(f"dg{j}", [128, 128], BF16) for j in range(4)]
    gb_sb = PB.take("gb_sb", [128, 64], F32)
    vb_sb = PB.take("vb_sb", [128, 64], F32)
    bload = PB.take("bload", [64, 128], F32)
    _SMALLROW_OFF = PB.cur
    smallrow = PB.take("smallrow", [1, D], F32)
    cst = PB.take("cst", [128, 8], F32)
    PERS_END = PB.cur
    assert PERS_END < 70 * 1024, PERS_END

    AB = PERS_END
    xrT = sb_at("xrT", [128, NDC, TH], BF16, AB)
    xq_res = sb_at("xq_res", [128, NDC, TCORE], BF16, AB)
    kT_str = sb_at("kT_str", [128, 64, 512], BF16, AB)
    yi_str = sb_at("yi_str", [128, 32, 512], BF16, AB)
    A_END = AB + ((NDC * TH * 2 + 31) & ~31)

    BB = A_END
    B2 = Bump(BB)
    wslab = [B2.take(f"wslab{i}", [128, NDC, 256], BF16) for i in range(2)]
    B_END = B2.cur
    S1B = Bump(BB)
    xg = [S1B.take(f"xg{i}", [128, D], F32) for i in range(2)]
    xng = [S1B.take(f"xng{i}", [128, D], F32) for i in range(2)]
    xr_bf = S1B.take("xr_bf", [128, D], BF16)
    kx_bf = S1B.take("kx_bf", [128, D], BF16)
    q1_f = S1B.take("q1_f", [128, D], F32)
    k0_i = S1B.take("k0_i", [128, D], I32)
    dp_f = S1B.take("dp_f", [128, D], F32)
    kq_st = S1B.take("kq_st", [128, NDC, 128], BF16)
    kf_f = q1_f
    S4B = Bump(BB)
    moslab = [S4B.take(f"moslab{i}", [128, 1024], BF16) for i in range(3)]
    gvin = [S4B.take(f"gvin{i}", [128, 512], F32) for i in range(3)]
    woslab = [S4B.take(f"woslab{i}", [128, 1024], BF16) for i in range(3)]
    zin = [S4B.take(f"zin{i}", [128, 512], BF16) for i in range(3)]
    oev = [S4B.take(f"oev{i}", [128, D], F32) for i in range(2)]
    outb_rep = S4B.take("outb_rep", [128, D], F32)
    qs_tile = S4B.take("qs_tile", [128, TCORE], F32)
    wh2iT = S4B.take("wh2iT", [16, DIN], BF16)
    ki_b = S4B.take("ki_b", [128, 512], I32)
    qt1 = S4B.take("qt1", [128, 512], F32)
    qt2 = S4B.take("qt2", [128, 512], F32)
    zf_b = S4B.take("zf_b", [128, 512], F32)

    CB = B_END
    C2 = Bump(CB)
    Bsb = [C2.take(f"Bsb{i}", [128, TCORE], F32) for i in range(2)]
    Csb = [C2.take(f"Csb{i}", [128, TCORE], F32) for i in range(2)]
    pxi_bf = C2.take("pxi_bf", [128, TH], BF16)
    xi_bf = C2.take("xi_bf", [128, TCORE], BF16)
    zev = [C2.take(f"zev{i}", [128, 1024], BF16) for i in range(2)]
    C_END = C2.cur
    S3B = Bump(AB)
    ub = [S3B.take(f"ub{i}", [128, TCORE], F32) for i in range(2)]
    dab = [S3B.take(f"dab{i}", [128, TCORE], F32) for i in range(2)]
    hb = [S3B.take(f"hb{i}", [128, TCORE], F32) for i in range(2)]
    htmp = S3B.take("htmp", [16, TCORE], F32)
    dt_sp = S3B.take("dt_sp", [16, TCORE], F32)
    assert S3B.cur <= A_END + 1, (S3B.cur, A_END)
    cum_t = sb_at("cum_t", [16, TCORE], F32, _SMALLROW_OFF)
    S3_END = max(S3B.cur, C_END)
    S4A = Bump(CB)
    gbuf = [S4A.take("gbuf0", [128, TCORE], F32)]
    gvbuf = [S4A.take("gvbuf0", [128, TCORE], F32)]
    macc = S4A.take("macc", [128, TCORE], F32)
    babs = S4A.take("babs", [128, TCORE], F32)
    sxinv_tile = S4A.take("sxinv_tile", [128, TCORE], F32)
    PR = Bump(BB)
    pw = [PR.take(f"pw{i}", [128, D], F32) for i in range(2)]
    pq = [PR.take(f"pq{i}", [128, D], BF16) for i in range(2)]
    pacc = PR.take("pacc", [128, 4], F32)
    ptmp = PR.take("ptmp", [128, D], F32)
    ptmp2 = PR.take("ptmp2", [128, D], F32)
    LIMIT = ARENA
    for v in (S3_END, S4A.cur, PR.cur, B_END, S1B.cur, S4B.cur):
        assert v < LIMIT, (v, LIMIT)

    psA = ctx.enter_context(nc.psum_tensor("psA", [128, 2048], F32))
    psB = ctx.enter_context(nc.psum_tensor("psB", [128, 1536], F32))
    psT = ctx.enter_context(nc.psum_tensor("psT", [128, 1024], BF16))
    KA = ["psA0", "psA1", "psA2", "psA3"]
    KB = ["psB0", "psB1", "psB2"]

    def bcast_row(srow, dst, dkey, skey, nblk=4):
        for blk in range(nblk):
            pr.c("pe", lambda e, blk=blk: e.matmul(
                psA[:, 512 * blk:512 * (blk + 1)], ones_row[:, :],
                srow[0:1, 512 * blk:512 * (blk + 1)], start=True, stop=True),
                reads=[skey, "onr"], writes=[KA[blk]])
        pr.c("dve", lambda e, nblk=nblk: e.tensor_copy(
            dst[:, 0:512 * nblk], psA[:, 0:512 * nblk]),
            reads=KA[:nblk], writes=[dkey])

    def consts():
        pr.c("pool", lambda e: e.memset(ident_f[:, :], 0.0), writes=["idf"], hz=True)
        pr.c("pool", lambda e: e.affine_select(
            out=ident_f[:, :], in_=ident_f[:, :], compare_op=ALU.not_equal,
            fill=1.0, base=0, pattern=[[-1, 128]], channel_multiplier=1),
            reads=["idf"], writes=["idf"], hz=True)
        pr.c("dve", lambda e: e.tensor_copy(ident_bf[:, :], ident_f[:, :]),
             reads=["idf"], writes=["idb"])
        for t, off, key in ((selT_lo, 0, "stlo"), (selT_hi, 128, "sthi")):
            pr.c("pool", lambda e, t=t: e.memset(t[:, :], 1.0), writes=[key], hz=True)
            pr.c("pool", lambda e, t=t, off=off: e.affine_select(
                out=t[:, :], in_=t[:, :], compare_op=ALU.is_ge, fill=0.0,
                base=off, pattern=[[-16, 16]], channel_multiplier=1),
                reads=[key], writes=[key], hz=True)
            pr.c("pool", lambda e, t=t, off=off: e.affine_select(
                out=t[:, :], in_=t[:, :], compare_op=ALU.is_ge, fill=0.0,
                base=15 - off, pattern=[[16, 16]], channel_multiplier=-1),
                reads=[key], writes=[key], hz=True)
        pr.c("pe", lambda e: e.transpose(psB[0:16, 0:128], selT_lo[:, :],
                                         ident_f[:, :]),
             reads=["stlo", "idf"], writes=["psB0"])
        pr.c("dve", lambda e: e.tensor_copy(sel_loT[:, :], psB[0:16, 0:128]),
             reads=["psB0"], writes=["slT"])
        pr.c("pe", lambda e: e.transpose(psB[0:16, 512:640], selT_hi[:, :],
                                         ident_f[:, :]),
             reads=["sthi", "idf"], writes=["psB1"])
        pr.c("dve", lambda e: e.tensor_copy(sel_hiT[:, :], psB[0:16, 512:640]),
             reads=["psB1"], writes=["shT"])
        pr.c("pool", lambda e: e.memset(ones_col[:, :], 1.0), writes=["onc"], hz=True)
        pr.c("pool", lambda e: e.memset(ones_col_bf[:, :], 1.0),
             writes=["onc"], hz=True)
        pr.c("pool", lambda e: e.memset(cst[:, 0:1], EPS_LN), writes=["cst"], hz=True)
        pr.c("pool", lambda e: e.memset(cst[:, 1:2], EPS_RMS), writes=["cst"], hz=True)
        pr.c("pool", lambda e: e.memset(cst[:, 2:3], 0.0), writes=["cst"], hz=True)
        pr.c("pool", lambda e: e.memset(cst[:, 4:5], 1.0), writes=["cst"],
             hz=True)
        pr.c("pool", lambda e: e.memset(ones_row[:, :], 1.0), writes=["onr"], hz=True)
        pr.d("sp", lambda e: e.dma_start(out=cst[0:HALO, 3:4],
                                         in_=halo_on[:, :]),
             reads=["cst"], writes=["cst"], tag="halo")
        pr.d("sp", lambda e: e.dma_start(out=smallrow[0:1, :],
                                         in_=w_norm[:, :]),
             writes=["wn_row"], tag="wn")
        bcast_row(smallrow, wn_rep, "wn_rep", "wn_row")

    consts()

    def s0():
        pr.c("dve", lambda e: e.memset(pacc[:, :], 0.0), writes=["pacc"], hz=True)
        for i in range(WIN_SH // 112):
            b = i % 2
            pr.d("sp", lambda e, i=i, b=b: e.dma_start(
                out=pw[b][0:112, :], in_=win_sl[112 * i:112 * (i + 1), :]),
                writes=[f"pw{b}"], tag="s0win")
            pr.c("dve", lambda e, b=b: e.tensor_copy(pq[b][0:112, :],
                                                     pw[b][0:112, :]),
                 reads=[f"pw{b}"], writes=[f"pq{b}"])
            pr.d("act", lambda e, i=i, b=b: e.dma_start(
                out=win_shard[112 * i:112 * (i + 1), :], in_=pq[b][0:112, :]),
                reads=[f"pq{b}"], writes=["win_shard"])
        for t in range(4):
            i, ch = divmod(t, 2)
            b = t % 2
            pr.d("sp", lambda e, i=i, ch=ch, b=b: e.dma_start(
                out=pw[b][:, :], in_=wout_sl[128 * i:128 * (i + 1),
                                             D * ch:D * (ch + 1)]),
                writes=[f"pw{b}"], tag="s0wo")
            pr.c("dve", lambda e, b=b: e.tensor_copy(pq[b][:, :], pw[b][:, :]),
                 reads=[f"pw{b}"], writes=[f"pq{b}"])
            pr.d("act", lambda e, i=i, ch=ch, b=b: e.dma_start(
                out=wo_shard[128 * i:128 * (i + 1), D * ch:D * (ch + 1)],
                in_=pq[b][:, :]),
                reads=[f"pq{b}"], writes=["wo_shard"])
        gsrc = []
        for i in range(8):
            gsrc.append((0, gate_sl[128 * i:128 * (i + 1), :]))
        for i in range(8):
            gsrc.append((1, val_sl[128 * i:128 * (i + 1), :]))
        for i in range(2):
            for ch in range(4):
                gsrc.append((2, outw_sl[128 * i:128 * (i + 1),
                                        D * ch:D * (ch + 1)]))
        for n, (col, srcap) in enumerate(gsrc):
            b = n % 2
            pr.d("sp", lambda e, b=b, srcap=srcap:
                 e.dma_start(out=pw[b][:, :], in_=srcap),
                 writes=[f"pw{b}"], tag="s0gam")
            pr.c("dve", lambda e, b=b: e.tensor_reduce(
                misc[:, 40:41], pw[b][:, :], AX.X, ALU.add,
                apply_absolute_value=True),
                reads=[f"pw{b}"], writes=["gred"], hz=True)
            pr.c("dve", lambda e, col=col: e.tensor_tensor(
                pacc[:, col:col + 1], pacc[:, col:col + 1],
                misc[:, 40:41], ALU.add),
                reads=["gred", "pacc"], writes=["pacc"])
        pr.c("pe", lambda e: e.matmul(psB[0:1, 0:4], ones_col[:, :],
                                      pacc[:, 0:4], start=True, stop=True),
             reads=["pacc", "onc"], writes=["psB0"])
        pr.c("dve", lambda e: e.tensor_copy(misc[0:1, 42:46], psB[0:1, 0:4]),
             reads=["psB0"], writes=["gpart"])
        pr.d("sp", lambda e: e.dma_start(out=ag1_in[:, 0:4],
                                         in_=misc[0:1, 42:46]),
             reads=["gpart"], writes=["ag1_in"])
        pr.cc(lambda e: e.collective_compute(
            "AllGather", ALU.bypass, replica_groups=[list(range(NCORES))],
            ins=[ag1_in], outs=[ag1_out]),
            reads=["ag1_in"], writes=["ag1_out"], tag="AG1")
        pr.d("sp", lambda e: e.dma_start(out=misc[0:8, 48:52],
                                         in_=ag1_out[:, 0:4]),
             reads=["ag1_out"], writes=["ag1_sb"])
        pr.c("pe", lambda e: e.matmul(psB[0:1, 512:516], ones_col[0:8, :],
                                      misc[0:8, 48:52], start=True, stop=True),
             reads=["ag1_sb", "onc"], writes=["psB1"])
        for j, cntj in enumerate((HID * D, HID * D, D * HID)):
            pr.c("dve", lambda e, j=j, cntj=cntj: ts(
                e, misc[0:1, 52 + j:53 + j], psB[0:1, 512 + j:513 + j],
                1.0 / cntj, 1e-5, ALU.mult, ALU.max),
                reads=["psB1"], writes=["gfin"])
        pr.c("pe", lambda e: e.matmul(psB[:, 1024:1027], ones_row[:, :],
                                      misc[0:1, 52:55], start=True, stop=True),
             reads=["gfin", "onr"], writes=["psB2"])
        pr.c("dve", lambda e: e.tensor_copy(gcols[:, 0:3], psB[:, 1024:1027]),
             reads=["psB2"], writes=["gcols"])
        pr.c("dve", lambda e: ts(e, gcols[:, 3:6], gcols[:, 0:3], 0.5),
             reads=["gcols"], writes=["gcols"])

        def quant(srcap, dstap, shkey, col, n):
            b = n % 2
            pr.d("sp", lambda e, b=b, srcap=srcap:
                 e.dma_start(out=pw[b][:, :], in_=srcap),
                 writes=[f"pw{b}"], tag="s0q")
            pr.c("dve", lambda e, b=b, col=col:
                 e.tensor_scalar(ptmp[:, :], pw[b][:, :],
                                 gcols[:, 3 + col:4 + col], None, ALU.is_ge),
                 reads=[f"pw{b}", "gcols"], writes=["ptmp"])
            pr.c("dve", lambda e, b=b, col=col:
                 e.tensor_scalar(ptmp2[:, :], pw[b][:, :], -1.0,
                                 gcols[:, 3 + col:4 + col], ALU.mult,
                                 ALU.is_ge),
                 reads=[f"pw{b}", "gcols"], writes=["ptmp2"])
            pr.c("dve", lambda e, b=b:
                 e.tensor_tensor(pq[b][:, :], ptmp[:, :], ptmp2[:, :],
                                 ALU.subtract),
                 reads=["ptmp", "ptmp2"], writes=[f"pq{b}"])
            pr.d("act", lambda e, b=b, dstap=dstap: e.dma_start(
                out=dstap, in_=pq[b][:, :]),
                reads=[f"pq{b}"], writes=[shkey])
        n = 0
        for i in range(8):
            quant(gate_sl[128 * i:128 * (i + 1), :],
                  mg_shard[128 * i:128 * (i + 1), :], "mg_shard", 0, n)
            n += 1
        for i in range(8):
            quant(val_sl[128 * i:128 * (i + 1), :],
                  mv_shard[128 * i:128 * (i + 1), :], "mv_shard", 1, n)
            n += 1
        for i in range(2):
            for ch in range(4):
                quant(outw_sl[128 * i:128 * (i + 1), D * ch:D * (ch + 1)],
                      mo_shard[128 * i:128 * (i + 1), D * ch:D * (ch + 1)],
                      "mo_shard", 2, n)
                n += 1

    if "s0" in phases:
        s0()

    S1KEYS_ = ["xg0", "xg1", "xng0", "xng1", "xr_bf", "kx_bf", "q1", "k0",
               "dp", "kq_st"]

    def s1():
        pr.c("pool", lambda e: e.memset(misc[0:1, 61:62], 0.0),
             writes=["pw0", "pw1", "pq0", "pq1", "pacc", "ptmp",
                     "ptmp2"] + S1KEYS_, hz=True)
        for g in range(NTG + 1):
            b = g % 2
            rows = HALO if g == 0 else 128
            r0 = 0 if g == 0 else HALO + 128 * (g - 1)
            xgk, xngk = f"xg{b}", f"xng{b}"
            prep_keys = (["pw0", "pw1", "pq0", "pq1", "ptmp", "ptmp2", "pacc"]
                         if g <= 1 and "s0" in phases else [])
            pr.d("sp", lambda e, b=b, rows=rows, r0=r0: e.dma_start(
                out=xg[b][0:rows, :], in_=x_sl[r0:r0 + rows, :]),
                reads=prep_keys, writes=[xgk], tag="s1x")
            for q in range(4):
                pr.c("dve", lambda e, b=b, rows=rows, q=q: e.bn_stats(
                    misc[0:rows, 12 + 6 * q:18 + 6 * q],
                    xg[b][0:rows, 512 * q:512 * (q + 1)]),
                    reads=[xgk], writes=["bst"], hz=True)
            pr.c("dve", lambda e, rows=rows: e.bn_aggr(
                misc[0:rows, 36:38], misc[0:rows, 12:36]),
                reads=["bst"], writes=["mv"], hz=True)
            pr.c("act", lambda e, rows=rows: e.activation(
                misc[0:rows, 0:1], misc[0:rows, 37:38], AF.Ln,
                bias=cst[0:rows, 0:1], scale=1.0),
                reads=["mv", "cst"], writes=["lnv"])
            pr.c("act", lambda e, rows=rows: e.activation(
                misc[0:rows, 1:2], misc[0:rows, 0:1], AF.Exp,
                bias=cst[0:rows, 2:3], scale=-0.5),
                reads=["lnv", "cst"], writes=["rs"])
            pr.c("dve", lambda e, b=b, rows=rows: e.tensor_scalar(
                xng[b][0:rows, :], xg[b][0:rows, :], misc[0:rows, 36:37],
                misc[0:rows, 1:2], ALU.subtract, ALU.mult),
                reads=[xgk, "mv", "rs"], writes=[xngk])
            if g == 1 and "st2" in taps:
                t3 = tapout("st2", (128, 16))
                pr.c("dve", lambda e, t3=t3: e.tensor_copy(
                    smallrow[0:1, 0:16] if False else gg_tok[:, 0:16],
                    misc[:, 0:16]), reads=["rr", "mv", "rs", "lnm", "ms"],
                    writes=["st2cap"])
                pr.c("dve", lambda e: e.tensor_copy(
                    so_tok[:, 0:2], misc[:, 36:38]),
                    reads=["mv"], writes=["st2cap2"])
                pr.d("sp", lambda e, t3=t3: e.dma_start(
                    out=t3[:, 0:14], in_=gg_tok[:, 0:14]),
                    reads=["st2cap"], writes=["tap_st2"])
                pr.d("sp", lambda e, t3=t3: e.dma_start(
                    out=t3[:, 14:16], in_=so_tok[:, 0:2]),
                    reads=["st2cap2"], writes=["tap_st2"])
            if g == 1 and "xn1" in taps:
                t = tapout("xn1", (128, D))
                pr.d("sp", lambda e, t=t, b=b: e.dma_start(
                    out=t[:, :], in_=xng[b][:, :]),
                    reads=[xngk], writes=["tap_xn1"])
                t2 = tapout("st1", (128, 64))
                pr.d("sp", lambda e, t2=t2: e.dma_start(
                    out=t2[:, :], in_=misc[:, :]),
                    reads=["rr", "mv", "rs"], writes=["tap_st1"])
            if g > 0:
                pr.c("pool", lambda e, b=b: e.tensor_tensor(
                    xg[b][:, :], xg[b][:, :], xng[b][:, :], ALU.add),
                    reads=[xgk, xngk], writes=[xgk])
                pr.d("act", lambda e, b=b, g=g: e.dma_start(
                    out=out_sl[128 * (g - 1):128 * g, :], in_=xg[b][:, :]),
                    reads=[xgk],
                    writes=[f"outg{g - 1}q{qq}" for qq in range(4)],
                    tag="s1out")
            pr.c("dve", lambda e, rows=rows: ts(
                e, misc[0:rows, 3:4], misc[0:rows, 37:38], 1.0, EPS_LN,
                ALU.mult, ALU.add), reads=["mv"], writes=["vpe"])
            pr.c("dve", lambda e, rows=rows: e.reciprocal(
                misc[0:rows, 4:5], misc[0:rows, 3:4]),
                reads=["vpe"], writes=["r1"], hz=True)
            pr.c("dve", lambda e, rows=rows: e.tensor_tensor(
                misc[0:rows, 5:6], misc[0:rows, 37:38], misc[0:rows, 4:5],
                ALU.mult), reads=["mv", "r1"], writes=["ms"])
            pr.c("act", lambda e, rows=rows: e.activation(
                misc[0:rows, 6:7], misc[0:rows, 5:6], AF.Ln,
                bias=cst[0:rows, 1:2], scale=1.0),
                reads=["ms", "cst"], writes=["lnm"])
            pr.c("act", lambda e, rows=rows: e.activation(
                misc[0:rows, 7:8], misc[0:rows, 6:7], AF.Exp,
                bias=cst[0:rows, 2:3], scale=-0.5),
                reads=["lnm", "cst"], writes=["rr"])
            pr.c("dve", lambda e, b=b, rows=rows: e.scalar_tensor_tensor(
                xr_bf[0:rows, :], xng[b][0:rows, :], misc[0:rows, 7:8],
                wn_rep[0:rows, :], ALU.mult, ALU.mult),
                reads=[xngk, "rr", "wn_rep"], writes=["xr_bf"])
            if g == 0:
                pr.c("dve", lambda e: ts(
                    e, xr_bf[0:HALO, :], xr_bf[0:HALO, :], cst[0:HALO, 3:4],
                    None, ALU.mult),
                    reads=["xr_bf", "cst"], writes=["xr_bf"])
            for half in range(2):
                for c8 in range(8):
                    c = half * 8 + c8
                    pr.c("pe", lambda e, c=c, c8=c8: e.transpose(
                        psT[:, 128 * c8:128 * (c8 + 1)],
                        xr_bf[:, 128 * c:128 * (c + 1)], ident_bf[:, :]),
                        reads=["xr_bf", "idb"], writes=["psT"], tag="s1tr")

                def ev(e, g=g, half=half):
                    c0 = half * 8
                    src = psT[:, :].rearrange("p (c t) -> p c t", c=8)
                    if g == 0:
                        return e.tensor_copy(xrT[:, c0:c0 + 8, 0:HALO],
                                             src[:, :, 0:HALO])
                    return e.tensor_copy(
                        xrT[:, c0:c0 + 8, HALO + 128 * (g - 1):
                            HALO + 128 * g], src)
                pr.c("dve", ev, reads=["psT"], writes=["xrT"])
            if g == 0:
                continue
            pr.c("dve", lambda e, b=b: e.reduce_max(
                misc[:, 8:9], xng[b][:, :], AX.X, apply_absolute_value=True),
                reads=[xngk], writes=["gx"], hz=True)
            pr.c("dve", lambda e: ts(e, misc[:, 9:10], misc[:, 8:9], 1e-5,
                                     None, ALU.max),
                 reads=["gx"], writes=["gxc"])
            pr.c("dve", lambda e: e.reciprocal(misc[:, 10:11], misc[:, 9:10]),
                 reads=["gxc"], writes=["rgx"], hz=True)
            pr.c("dve", lambda e: ts(e, sx_col[:, 0:1], misc[:, 10:11], 127.0),
                 reads=["rgx"], writes=["sxc"])
            pr.c("dve", lambda e, g=g: ts(
                e, sxi_tok[:, g - 1:g], misc[:, 9:10], 1.0 / 127.0),
                reads=["gxc"], writes=["sxi_tok"])
            pr.c("dve", lambda e, b=b: ts(e, q1_f[:, :], xng[b][:, :],
                                          sx_col[:, 0:1]),
                 reads=[xngk, "sxc"], writes=["q1"])
            pr.c("dve", lambda e: ts(e, k0_i[:, :], q1_f[:, :], 256.0, None,
                                     ALU.add),
                 reads=["q1"], writes=["k0"])
            pr.c("dve", lambda e: e.scalar_tensor_tensor(
                dp_f[:, :], k0_i[:, :], 256.0, q1_f[:, :], ALU.subtract,
                ALU.subtract), reads=["k0", "q1"], writes=["dp"])
            pr.c("dve", lambda e: e.scalar_tensor_tensor(
                kf_f[:, :], dp_f[:, :], -0.5, k0_i[:, :], ALU.is_le, ALU.add),
                reads=["dp", "k0", "q1"], writes=["q1"])
            pr.c("dve", lambda e: ts(e, kx_bf[:, :], kf_f[:, :], 256.0, None,
                                     ALU.subtract),
                 reads=["q1"], writes=["kx_bf"])
            if g == 1 and "q11" in taps:
                for nm_, buf_, key_ in (("q11", q1_f, "q1"), ("k01", k0_i, "k0"),
                                        ("dp1", dp_f, "dp"), ("kx1", kx_bf, "kx_bf"),
                                        ("xr1", xr_bf, "xr_bf")):
                    t9 = tapout(nm_, (128, D), F32 if nm_ != "kx1" and nm_ != "xr1" else BF16)
                    if nm_ == "k01":
                        t9 = tap_outs[nm_] = dout("tap_" + nm_ + "b", (128, D), I32)
                    pr.d("sp", lambda e, t9=t9, buf_=buf_: e.dma_start(
                        out=t9[:, :], in_=buf_[:, :]),
                        reads=[key_], writes=["tap_" + nm_])
            for half in range(2):
                for c8 in range(8):
                    c = half * 8 + c8
                    pr.c("pe", lambda e, c=c, c8=c8: e.transpose(
                        psT[:, 128 * c8:128 * (c8 + 1)],
                        kx_bf[:, 128 * c:128 * (c + 1)], ident_bf[:, :]),
                        reads=["kx_bf", "idb"], writes=["psT"], tag="s1trq")

                def evq(e, half=half):
                    c0 = half * 8
                    return e.tensor_copy(
                        kq_st[:, c0:c0 + 8, :],
                        psT[:, :].rearrange("p (c t) -> p c t", c=8))
                pr.c("dve", evq, reads=["psT"], writes=["kq_st"])
            pr.d("act", lambda e, g=g: e.dma_start(
                out=xq_scr.transpose([1, 0, 2])[:, :, 128 * (g - 1):128 * g],
                in_=kq_st[:, :, :]),
                reads=["kq_st"], writes=["xq_scr"], tag="s1xq")

    if "s1" in phases:
        s1()

    def s0_ags():
        prev = ["xrT", "kq_st"]
        for nm, i_, o_, ik, ok in (
                ("AGwin", win_shard, win_q, "win_shard", "win_q"),
                ("AGmg", mg_shard, mg_q, "mg_shard", "mg_q"),
                ("AGmv", mv_shard, mv_q, "mv_shard", "mv_q"),
                ("AGmo", mo_shard, mo_q, "mo_shard", "mo_q"),
                ("AGwo", wo_shard, wo_q, "wo_shard", "wo_q")):
            pr.cc(lambda e, i_=i_, o_=o_: e.collective_compute(
                "AllGather", ALU.bypass, replica_groups=[list(range(NCORES))],
                ins=[i_], outs=[o_]),
                reads=[ik] + list(prev), writes=[ok], tag=nm)
            prev = [ok]

    if "s0" in phases:
        s0_ags()

    NQ = {0: "psA0", 1: "psA1", 2: "psA2", 3: "psA3"}
    NB = {0: "psB0", 1: "psB1", 2: "psB2"}

    def wslab_load(b, srcdram, c0, ncols, wkey):
        for dc in range(NDC):
            q = "sp" if dc % 2 == 0 else "act"
            pr.d(q, lambda e, b=b, dc=dc, c0=c0, ncols=ncols, srcdram=srcdram:
                 e.dma_start_transpose(
                     wslab[b][:, dc, 0:ncols],
                     srcdram[c0:c0 + ncols, 128 * dc:128 * (dc + 1)]),
                 writes=[wkey, "tch"], reads=["s1fence"], tag="wslab")

    S1KEYS = ["xg0", "xg1", "xng0", "xng1", "xr_bf", "kx_bf", "q1", "k0",
              "dp", "kq_st"]

    def super_mms(c0, nch, wkey, b, toks_xi):
        for ci in range(nch):
            c = c0 + ci
            for tb in range(4):
                lo = (0 if toks_xi else 8) + 512 * tb
                for dc in range(NDC):
                    pr.c("pe", lambda e, b=b, ci=ci, dc=dc, tb=tb, lo=lo:
                         e.matmul(psA[:, 512 * tb:512 * (tb + 1)],
                                  wslab[b][:, dc, 128 * ci:128 * (ci + 1)],
                                  xrT[:, dc, lo:lo + 512],
                                  start=(dc == 0), stop=(dc == NDC - 1)),
                         reads=[wkey, "xrT"], writes=[NQ[tb]],
                         tag=f"mm_c{c}")
            if toks_xi:
                for dc in range(NDC):
                    pr.c("pe", lambda e, b=b, ci=ci, dc=dc:
                         e.matmul(psB[:, 0:8],
                                  wslab[b][:, dc, 128 * ci:128 * (ci + 1)],
                                  xrT[:, dc, 2048:2056],
                                  start=(dc == 0), stop=(dc == NDC - 1)),
                         reads=[wkey, "xrT"], writes=["psB0"],
                         tag=f"mmtail_c{c}")
            yield c

    def s2():
        pr.c("pool", lambda e: e.memset(misc[0:1, 63:64], 0.0),
             reads=[], writes=["s1fence"] + S1KEYS, hz=True)
        pr.d("sp", lambda e: e.dma_start(
            out=wc_sb[:, :].rearrange("p (c j) -> p c j", j=4),
            in_=w_conv.rearrange("(c p) j -> p c j", p=128)),
            writes=["wc"], tag="wc")
        pr.d("sp", lambda e: e.dma_start(out=dtb_col[:, :], in_=dt_bias[:, :]),
             writes=["dtb"], tag="dtb")
        pr.d("sp", lambda e: e.dma_start(out=A_col[:, :], in_=a_log[:, :]),
             writes=["A_log_sb"], tag="alog")
        pr.c("act", lambda e: e.activation(A_col[:, :], A_col[:, :], AF.Exp,
                                           bias=cst[0:16, 2:3], scale=1.0),
             reads=["A_log_sb", "cst"], writes=["A_log_sb"], hz=True)
        pr.c("dve", lambda e: ts(e, A_col[:, :], A_col[:, :], -1.0),
             reads=["A_log_sb"], writes=["A_col"], hz=True)
        pr.d("sp", lambda e: e.dma_start(out=cmask[:, :], in_=cmask_in[:, :]),
             writes=["cmask"], tag="cmaskl")

        for sup in range(16):
            b = sup % 2
            wkey = f"wslab{b}"
            wslab_load(b, win_q, (32 + 2 * sup) * 128, 256, wkey)
            for c in super_mms(32 + 2 * sup, 2, wkey, b, True):
                cc = c - 32
                head = cc // 2
                for tb in range(4):
                    pr.c("dve", lambda e, tb=tb: e.tensor_copy(
                        pxi_bf[:, 512 * tb:512 * (tb + 1)],
                        psA[:, 512 * tb:512 * (tb + 1)]),
                        reads=[NQ[tb]], writes=["pxi"])
                pr.c("dve", lambda e: e.tensor_copy(
                    pxi_bf[:, 2048:2056], psB[:, 0:8]),
                    reads=["psB0"], writes=["pxi"])
                if cc == 0 and "pxi0" in taps:
                    t = tapout("pxi0", (128, TH), BF16)
                    pr.d("sp", lambda e, t=t: e.dma_start(
                        out=t[:, :], in_=pxi_bf[:, :]),
                        reads=["pxi"], writes=["tap_pxi0"])
                for j in range(4):
                    pr.c("dve", lambda e, j=j, cc=cc: ts(
                        e, dgs[j][:, :], ident_f[:, :],
                        wc_sb[:, 4 * cc + j:4 * cc + j + 1]),
                        reads=["idf", "wc"], writes=[f"dg{j}"], hz=True)
                ctgt = [(psB, 0, "psB0"), (psB, 512, "psB1"),
                        (psB, 1024, "psB2"), (psA, 0, "psA0")]
                for tb in range(4):
                    tgt, off, key = ctgt[tb]
                    for j in range(4):
                        pr.c("pe", lambda e, tb=tb, j=j, tgt=tgt, off=off:
                             e.matmul(tgt[:, off:off + 512], dgs[j][:, :],
                                      pxi_bf[:, 512 * tb + 5 + j:
                                             512 * tb + 5 + j + 512],
                                      start=(j == 0), stop=(j == 3)),
                             reads=[f"dg{j}", "pxi"], writes=[key],
                             tag=f"conv{cc}")
                    pr.c("act", lambda e, tb=tb, tgt=tgt, off=off, key=key:
                         e.activation(xi_bf[:, 512 * tb:512 * (tb + 1)],
                                      tgt[:, off:off + 512], AF.Silu,
                                      bias=cst[:, 2:3], scale=1.0),
                         reads=[key, "cst"], writes=["xi_bf"], hz=True)
                if cc == 0 and "xi0" in taps:
                    t = tapout("xi0", (128, TCORE), BF16)
                    pr.d("sp", lambda e, t=t: e.dma_start(
                        out=t[:, :], in_=xi_bf[:, :]),
                        reads=["xi_bf"], writes=["tap_xi0"])
                if cc % 2 == 0:
                    pr.c("pool", lambda e: e.memset(xbar_row[0:1, :], 0.0),
                         writes=["xbar_row"], hz=True)
                for tb in range(4):
                    pr.c("pe", lambda e, tb=tb: e.matmul(
                        psB[0:1, 0:512], ones_col_bf[:, :],
                        xi_bf[:, 512 * tb:512 * (tb + 1)],
                        start=True, stop=True),
                        reads=["onc", "xi_bf"], writes=["psB0"],
                        tag="xbarmm")
                    pr.c("dve", lambda e, tb=tb: e.tensor_tensor(
                        xbar_row[0:1, 512 * tb:512 * (tb + 1)],
                        xbar_row[0:1, 512 * tb:512 * (tb + 1)],
                        psB[0:1, 0:512], ALU.add),
                        reads=["psB0", "xbar_row"], writes=["xbar_row"])
                if cc % 2 == 1:
                    pr.d("act", lambda e: e.dma_start(
                        out=row_b[0:1, 0:TCORE], in_=xbar_row[0:1, :]),
                        reads=["xbar_row", "row_b"], writes=["row_b"],
                        tag="xbb1")
                    pr.d("sp", lambda e, head=head: e.dma_start(
                        out=xbar[head:head + 1, :], in_=row_b[0:1, 0:TCORE]),
                        reads=["row_b"], writes=["xbar"], tag="xbb2")

    def s2dt():
        for sup, (c0, nch) in enumerate(((65, 2), (67, 2), (64, 1))):
            b = sup % 2
            wkey = f"wslab{b}"
            wslab_load(b, win_q, c0 * 128, 128 * nch, wkey)
            for c in super_mms(c0, nch, wkey, b, False):
                if c == 64:
                    for tb in range(4):
                        pr.c("dve", lambda e, tb=tb: e.tensor_copy(
                            dt_sp[0:16, 512 * tb:512 * (tb + 1)],
                            psA[0:16, 512 * tb:512 * (tb + 1)]),
                            reads=[NQ[tb]],
                            writes=["dt_sp", "xrT"])
                    if "dtraw" in taps:
                        t = tapout("dtraw", (16, TCORE))
                        pr.d("sp", lambda e, t=t: e.dma_start(
                            out=t[:, :], in_=dt_sp[0:16, :]),
                            reads=["dt_sp"], writes=["tap_dtraw"])
                else:
                    dst, key = ((Bsb[c - 65], f"Bsb{c - 65}") if c < 67 else
                                (Csb[c - 67], f"Csb{c - 67}"))
                    for tb in range(4):
                        pr.c("dve", lambda e, tb=tb, dst=dst, key=key:
                             e.tensor_copy(
                                 dst[:, 512 * tb:512 * (tb + 1)],
                                 psA[:, 512 * tb:512 * (tb + 1)]),
                             reads=[NQ[tb]], writes=[key])

    def s3a():
        pr.c("dve", lambda e: ts(e, dt_sp[0:16, :], dt_sp[0:16, :],
                                 dtb_col[:, 0:1], None, ALU.add),
             reads=["dt_sp", "dtb"], writes=["dt_sp"], hz=True)
        pr.c("act", lambda e: e.activation(htmp[0:16, :], dt_sp[0:16, :],
                                           AF.Abs, bias=cst[0:16, 2:3],
                                           scale=1.0),
             reads=["dt_sp", "cst"], writes=["htmp", "xrT"], hz=True)
        pr.c("act", lambda e: e.activation(htmp[0:16, :], htmp[0:16, :],
                                           AF.Exp, bias=cst[0:16, 2:3],
                                           scale=-1.0),
             reads=["htmp", "cst"], writes=["htmp"], hz=True)
        pr.c("act", lambda e: e.activation(htmp[0:16, :], htmp[0:16, :],
                                           AF.Ln, bias=cst[0:16, 4:5],
                                           scale=1.0),
             reads=["htmp", "cst"], writes=["htmp"], hz=True)
        pr.c("dve", lambda e: ts(e, dt_sp[0:16, :], dt_sp[0:16, :], 0.0,
                                 None, ALU.max),
             reads=["dt_sp"], writes=["dt_sp"], hz=True)
        pr.c("dve", lambda e: e.tensor_tensor(dt_sp[0:16, :], dt_sp[0:16, :],
                                              htmp[0:16, :], ALU.add),
             reads=["dt_sp", "htmp"], writes=["dt_sp"], hz=True)
        pr.c("dve", lambda e: ts(e, htmp[0:16, :], dt_sp[0:16, :],
                                 A_col[:, 0:1]),
             reads=["dt_sp", "A_col"], writes=["htmp"], hz=True)
        pr.c("dve", lambda e: e.tensor_tensor_scan(
            cum_t[0:16, :], htmp[0:16, :], htmp[0:16, :], 0.0,
            ALU.add, ALU.bypass),
            reads=["htmp"], writes=["cum", "smallrow"], hz=True)
        for bt, sel, selk in ((0, sel_loT, "slT"), (1, sel_hiT, "shT")):
            for tb in range(4):
                pr.c("pe", lambda e, tb=tb, sel=sel, selk=selk: e.matmul(
                    psA[:, 512 * tb:512 * (tb + 1)], sel[:, :],
                    htmp[0:16, 512 * tb:512 * (tb + 1)],
                    start=True, stop=True),
                    reads=[selk, "htmp"], writes=[NQ[tb]], tag="bca")
            pr.c("act", lambda e, bt=bt: e.activation(
                dab[bt][:, :], psA[:, :], AF.Exp, bias=cst[:, 2:3], scale=1.0),
                reads=["psA0", "psA1", "psA2", "psA3", "cst"],
                writes=[f"dab{bt}", "xrT"], hz=True)
        pr.c("dve", lambda e: e.scalar_tensor_tensor(
            htmp[0:16, :], xbar[0:16, :], 1.0 / DH, dt_sp[0:16, :],
            ALU.mult, ALU.mult),
            reads=["xbar", "dt_sp", "cum"], writes=["htmp"], hz=True)
        for bt, sel, selk in ((0, sel_loT, "slT"), (1, sel_hiT, "shT")):
            for tb in range(4):
                pr.c("pe", lambda e, tb=tb, sel=sel, selk=selk: e.matmul(
                    psA[:, 512 * tb:512 * (tb + 1)], sel[:, :],
                    htmp[0:16, 512 * tb:512 * (tb + 1)],
                    start=True, stop=True),
                    reads=[selk, "htmp"], writes=[NQ[tb]], tag="bcw")
            pr.c("dve", lambda e, bt=bt: e.tensor_tensor(
                ub[bt][:, :], Bsb[bt][:, :], psA[:, :], ALU.mult),
                reads=["psA0", "psA1", "psA2", "psA3", f"Bsb{bt}"],
                writes=[f"ub{bt}", "xrT"], hz=True)
        if "us3" in taps:
            t = tapout("us3", (128, TCORE))
            pr.d("sp", lambda e, t=t: e.dma_start(out=t[:, :],
                                                  in_=ub[0][:, :]),
                 reads=["ub0"], writes=["tap_us3"])
            t2 = tapout("das3", (128, TCORE))
            pr.d("sp", lambda e, t2=t2: e.dma_start(out=t2[:, :],
                                                    in_=dab[0][:, :]),
                 reads=["dab0"], writes=["tap_das3"])
        for bt in range(2):
            pr.c("dve", lambda e, bt=bt: e.tensor_tensor_scan(
                hb[bt][:, :], dab[bt][:, :], ub[bt][:, :], 0.0,
                ALU.mult, ALU.add),
                reads=[f"dab{bt}", f"ub{bt}"],
                writes=[f"hb{bt}", "xrT"], hz=True)
            pr.c("dve", lambda e, bt=bt: e.tensor_copy(
                S_colA[:, bt:bt + 1], hb[bt][:, TCORE - 1:TCORE]),
                reads=[f"hb{bt}"], writes=["S_col"], hz=True)
            pr.c("dve", lambda e, bt=bt: e.tensor_tensor(
                ub[bt][:, :], hb[bt][:, :], Csb[bt][:, :], ALU.mult),
                reads=[f"hb{bt}", f"Csb{bt}"], writes=[f"ub{bt}"], hz=True)

    def s3b():
        for tb in range(4):
            pr.c("pe", lambda e, tb=tb: e.matmul(
                psA[0:16, 512 * tb:512 * (tb + 1)], selT_lo[:, :],
                ub[0][:, 512 * tb:512 * (tb + 1)], start=True, stop=False),
                reads=["stlo", "ub0"], writes=[NQ[tb]], tag="ylo")
            pr.c("pe", lambda e, tb=tb: e.matmul(
                psA[0:16, 512 * tb:512 * (tb + 1)], selT_hi[:, :],
                ub[1][:, 512 * tb:512 * (tb + 1)], start=False, stop=True),
                reads=["sthi", "ub1"], writes=[NQ[tb]], tag="yhi")
        pr.c("dve", lambda e: e.tensor_copy(y_loc[0:16, :], psA[0:16, :]),
             reads=["psA0", "psA1", "psA2", "psA3"], writes=["y_loc"],
             hz=True)
        pr.c("act", lambda e: e.activation(htmp[0:16, :], cum_t[0:16, :],
                                           AF.Exp, bias=cst[0:16, 2:3],
                                           scale=1.0),
             reads=["cum", "cst"], writes=["htmp"], hz=True)
        for bt, sel, selk, Pt, Pk in ((0, sel_loT, "slT", P0, "P0"),
                                      (1, sel_hiT, "shT", P1, "P1")):
            for tb in range(4):
                pr.c("pe", lambda e, tb=tb, sel=sel, selk=selk: e.matmul(
                    psA[:, 512 * tb:512 * (tb + 1)], sel[:, :],
                    htmp[0:16, 512 * tb:512 * (tb + 1)],
                    start=True, stop=True),
                    reads=[selk, "htmp"], writes=[NQ[tb]], tag="bcE")
            pr.c("dve", lambda e, bt=bt, Pt=Pt, Pk=Pk: e.tensor_tensor(
                Pt[:, :], Csb[bt][:, :], psA[:, :], ALU.mult),
                reads=["psA0", "psA1", "psA2", "psA3", f"Csb{bt}"],
                writes=[Pk], hz=True)
            pr.c("dve", lambda e, bt=bt: e.tensor_copy(
                D_colA[:, bt:bt + 1], psA[:, TCORE - 1:TCORE]),
                reads=["psA3"], writes=["D_col"], hz=True)
        pr.d("sp", lambda e: e.dma_start(out=ag3_in[0:1, 0:256],
                                         in_=S_colA[:, :]),
             reads=["S_col"], writes=["ag3_in"], tag="ag3s")
        pr.d("sp", lambda e: e.dma_start(out=ag3_in[0:1, 256:512],
                                         in_=D_colA[:, :]),
             reads=["D_col"], writes=["ag3_in"], tag="ag3d")
        pr.cc(lambda e: e.collective_compute(
            "AllGather", ALU.bypass, replica_groups=[list(range(NCORES))],
            ins=[ag3_in], outs=[ag3_out]),
            reads=["ag3_in", "wo_q"], writes=["ag3_out"], tag="AG3")
        pr.d("sp", lambda e: e.dma_start(
            out=agS[:, 0:16].rearrange("p (r b) -> p r b", b=2),
            in_=ag3_out[:, 0:256].rearrange("r (p b) -> p r b", p=128)),
            reads=["ag3_out"], writes=["agS"], tag="ag3u1")
        pr.d("sp", lambda e: e.dma_start(
            out=agS[:, 16:32].rearrange("p (r b) -> p r b", b=2),
            in_=ag3_out[:, 256:512].rearrange("r (p b) -> p r b", p=128)),
            reads=["ag3_out"], writes=["agS"], tag="ag3u2")
        pr.c("dve", lambda e: e.memset(hinA[:, :], 0.0), writes=["hin"],
             hz=True)
        for r in range(NCORES):
            for bt in range(2):
                pr.c("dve", lambda e, r=r, bt=bt: e.scalar_tensor_tensor(
                    misc[:, 40 + bt:41 + bt], hinA[:, bt:bt + 1],
                    agS[:, 16 + 2 * r + bt:17 + 2 * r + bt],
                    agS[:, 2 * r + bt:1 + 2 * r + bt],
                    ALU.mult, ALU.add),
                    reads=["hin", "agS"], writes=["hin_t"], hz=True)
                pr.c("dve", lambda e, bt=bt: e.tensor_tensor(
                    misc[:, 42 + bt:43 + bt], misc[:, 40 + bt:41 + bt],
                    hinA[:, bt:bt + 1], ALU.subtract),
                    reads=["hin_t", "hin"], writes=["hin_d"], hz=True)
                pr.c("dve", lambda e, r=r, bt=bt: e.scalar_tensor_tensor(
                    hinA[:, bt:bt + 1], misc[:, 42 + bt:43 + bt],
                    cmask[:, r:r + 1], hinA[:, bt:bt + 1],
                    ALU.mult, ALU.add),
                    reads=["hin_d", "cmask", "hin"], writes=["hin"], hz=True)
        for bt, Pt, Pk in ((0, P0, "P0"), (1, P1, "P1")):
            pr.c("dve", lambda e, bt=bt, Pt=Pt, Pk=Pk: ts(
                e, dab[bt][:, :], Pt[:, :], hinA[:, bt:bt + 1]),
                reads=[Pk, "hin"], writes=[f"dab{bt}"], hz=True)
        for tb in range(4):
            pr.c("pe", lambda e, tb=tb: e.matmul(
                psA[0:16, 512 * tb:512 * (tb + 1)], selT_lo[:, :],
                dab[0][:, 512 * tb:512 * (tb + 1)], start=True, stop=False),
                reads=["stlo", "dab0"], writes=[NQ[tb]], tag="yc0")
            pr.c("pe", lambda e, tb=tb: e.matmul(
                psA[0:16, 512 * tb:512 * (tb + 1)], selT_hi[:, :],
                dab[1][:, 512 * tb:512 * (tb + 1)], start=False, stop=True),
                reads=["sthi", "dab1"], writes=[NQ[tb]], tag="yc1")
        pr.c("dve", lambda e: e.tensor_tensor(
            y_loc[0:16, :], y_loc[0:16, :], psA[0:16, :], ALU.add),
            reads=["psA0", "psA1", "psA2", "psA3", "y_loc"],
            writes=["y_loc"], hz=True)
        pr.c("dve", lambda e: e.tensor_copy(yful_bf[0:16, :], y_loc[0:16, :]),
             reads=["y_loc"], writes=["yful"], hz=True)

    def s2z():
        for sup in range(16):
            b = sup % 2
            wkey = f"wslab{b}"
            wslab_load(b, win_q, 2 * sup * 128, 256, wkey)
            for ci in range(2):
                c = 2 * sup + ci
                for tb in range(4):
                    lo = 8 + 512 * tb
                    for dc in range(NDC):
                        pr.c("pe", lambda e, b=b, ci=ci, dc=dc, tb=tb, lo=lo:
                             e.matmul(psA[:, 512 * tb:512 * (tb + 1)],
                                      wslab[b][:, dc, 128 * ci:128 * (ci + 1)],
                                      xrT[:, dc, lo:lo + 512],
                                      start=(dc == 0), stop=(dc == NDC - 1)),
                             reads=[wkey, "xrT"], writes=[NQ[tb]],
                             tag=f"zmm{c}")
                for half in range(2):
                    zk = f"zev{half}"
                    for q in range(2):
                        tb = 2 * half + q
                        pr.c("dve", lambda e, tb=tb, half=half, q=q:
                             e.tensor_copy(zev[half][:, 512 * q:512 * (q + 1)],
                                           psA[:, 512 * tb:512 * (tb + 1)]),
                             reads=[NQ[tb]], writes=[zk])
                    pr.d("act", lambda e, c=c, half=half, zk=zk: e.dma_start(
                        out=z_scr[c, :, 1024 * half:1024 * (half + 1)],
                        in_=zev[half][:, :]),
                        reads=[zk], writes=["z_scr"], tag="zspill")

    if "s2" in phases:
        s2()
        s2z()
        s2dt()
    if "s3" in phases:
        s3a()
        s3b()

    def s4a():
        for srcb, dstb, kk in ((gate_b, gb_sb, "gb"), (val_b, vb_sb, "vb")):
            pr.d("sp", lambda e, srcb=srcb: e.dma_start(out=bload[:, :],
                                                        in_=srcb[:, :]),
                 reads=["bload"], writes=["bload"], tag="bld")
            pr.c("pe", lambda e: e.transpose(psB[:, 0:64], bload[:, :],
                                             ident_f[0:64, 0:64]),
                 reads=["bload", "idf"], writes=["psB0"])
            pr.c("dve", lambda e, dstb=dstb, kk=kk: e.tensor_copy(
                dstb[:, :], psB[:, 0:64]),
                reads=["psB0"], writes=[kk], hz=True)
        pr.c("pe", lambda e: e.transpose(psB[0:16, 512:640], sxi_tok[:, :],
                                         ident_f[:, :]),
             reads=["sxi_tok", "idf"], writes=["psB1"])
        pr.c("dve", lambda e: e.tensor_copy(bload[0:16, 0:128],
                                            psB[0:16, 512:640]),
             reads=["psB1", "bload"], writes=["bload"], hz=True)
        pr.d("act", lambda e: e.dma_start(
            out=row_b.rearrange("a (g p) -> (a g) p", p=128)[0:NTG, :],
            in_=bload[0:16, 0:128]),
            reads=["bload", "row_b"], writes=["row_b"], tag="sxf")
        pr.d("sp", lambda e: e.dma_start(out=smallrow[0:1, 0:TCORE],
                                         in_=row_b[0:1, 0:TCORE]),
             reads=["row_b"], writes=["smallrow", "cum"], tag="sxl")
        pr.c("pool", lambda e: e.memset(misc[0:1, 62:63], 0.0),
             writes=["pxi", "xi_bf", "zev0", "zev1", "sxinv_tile"], hz=True)
        bcast_row(smallrow, sxinv_tile, "sxinv_tile", "smallrow")
        pr.c("pool", lambda e: e.memset(macc[:, :], 0.0),
             writes=["macc", "Csb0"], hz=True)
        for dc in range(NDC):
            pr.d("sp", lambda e, dc=dc: e.dma_start(
                out=xq_res[:, dc, :], in_=xq_scr[dc, :, :]),
                writes=(["xq_res", "xrT", "ub0", "ub1", "hb0", "hb1",
                         "dab0", "dab1", "htmp", "dt_sp"] if dc == 0
                        else ["xq_res"]), tag="xqload")
        for c in range(64):
            for mat, mq, gv in ((0, mg_q, "g"), (1, mv_q, "v")):
                b = (2 * c + mat) % 2
                wkey = f"wslab{b}"
                wslab_load(b, mq, 128 * c, 128, wkey)
                for tb in range(4):
                    for dc in range(NDC):
                        pr.c("pe", lambda e, b=b, dc=dc, tb=tb:
                             e.matmul(psA[:, 512 * tb:512 * (tb + 1)],
                                      wslab[b][:, dc, 0:128],
                                      xq_res[:, dc, 512 * tb:512 * (tb + 1)],
                                      start=(dc == 0), stop=(dc == NDC - 1)),
                             reads=[wkey, "xq_res"], writes=[NQ[tb]],
                             tag=f"s4mm{gv}{c}")
                bb = 0
                if mat == 0:
                    for tb in range(4):
                        pr.c("dve", lambda e, bb=bb, tb=tb:
                             e.scalar_tensor_tensor(
                                 gbuf[bb][:, 512 * tb:512 * (tb + 1)],
                                 psA[:, 512 * tb:512 * (tb + 1)],
                                 gcols[:, 0:1],
                                 sxinv_tile[:, 512 * tb:512 * (tb + 1)],
                                 ALU.mult, ALU.mult),
                             reads=[NQ[tb], "gcols", "sxinv_tile"],
                             writes=([f"gbuf{bb}", "Bsb0"] if tb == 0 else
                                     [f"gbuf{bb}"]))
                    pr.c("act", lambda e, bb=bb, c=c: e.activation(
                        gbuf[bb][:, :], gbuf[bb][:, :], AF.Silu,
                        bias=gb_sb[:, c:c + 1], scale=1.0),
                        reads=[f"gbuf{bb}", "gb"], writes=[f"gbuf{bb}"],
                        hz=True)
                else:
                    for tb in range(4):
                        pr.c("dve", lambda e, bb=bb, tb=tb:
                             e.scalar_tensor_tensor(
                                 gvbuf[bb][:, 512 * tb:512 * (tb + 1)],
                                 psA[:, 512 * tb:512 * (tb + 1)],
                                 gcols[:, 1:2],
                                 sxinv_tile[:, 512 * tb:512 * (tb + 1)],
                                 ALU.mult, ALU.mult),
                             reads=[NQ[tb], "gcols", "sxinv_tile"],
                             writes=([f"gvbuf{bb}", "Bsb1"] if tb == 0 else
                                     [f"gvbuf{bb}"]))
                    pr.c("dve", lambda e, bb=bb, c=c: e.scalar_tensor_tensor(
                        gvbuf[bb][:, :], gvbuf[bb][:, :], vb_sb[:, c:c + 1],
                        gbuf[bb][:, :], ALU.add, ALU.mult),
                        reads=[f"gvbuf{bb}", "vb", f"gbuf{bb}"],
                        writes=[f"gvbuf{bb}"], hz=True)
                    pr.c("act", lambda e, bb=bb: e.activation(
                        babs[:, :], gvbuf[bb][:, :], AF.Abs,
                        bias=cst[:, 2:3], scale=1.0),
                        reads=[f"gvbuf{bb}", "cst"],
                        writes=["babs", "Csb1"], hz=True)
                    pr.c("dve", lambda e: e.tensor_tensor(
                        macc[:, :], macc[:, :], babs[:, :], ALU.max),
                        reads=["babs", "macc"], writes=["macc"], hz=True)
                    pr.d("act", lambda e, bb=bb, c=c: e.dma_start(
                        out=gv_scr[c, :, :], in_=gvbuf[bb][:, :]),
                        reads=[f"gvbuf{bb}"], writes=["gv_scr"], tag="gvsp")
        for blk in range(NTG):
            pr.c("pe", lambda e, blk=blk: e.transpose(
                psA[:, 0:128], macc[:, 128 * blk:128 * (blk + 1)],
                ident_f[:, :]),
                reads=["macc", "idf"], writes=["psA0"], tag="ggtr")
            pr.c("dve", lambda e, blk=blk: e.reduce_max(
                gg_tok[:, blk:blk + 1], psA[:, 0:128], AX.X),
                reads=["psA0"], writes=["gg_tok"], hz=True)
        pr.c("dve", lambda e: ts(e, gg_tok[:, :], gg_tok[:, :], 1e-5, None,
                                 ALU.max),
             reads=["gg_tok"], writes=["gg_tok"], hz=True)
        pr.c("dve", lambda e: ts(e, so_tok[:, :], gg_tok[:, :],
                                 gcols[:, 2:3], 1.0 / 127.0,
                                 ALU.mult, ALU.mult),
             reads=["gg_tok", "gcols"], writes=["so_tok"], hz=True)
        pr.c("dve", lambda e: e.reciprocal(misc[:, 44:60], gg_tok[:, :]),
             reads=["gg_tok"], writes=["qsr"], hz=True)
        pr.c("dve", lambda e: ts(e, misc[:, 44:60], misc[:, 44:60], 127.0),
             reads=["qsr"], writes=["qsr"], hz=True)
        pr.c("pe", lambda e: e.transpose(psB[0:16, 512:640], misc[:, 44:60],
                                         ident_f[:, :]),
             reads=["qsr", "idf"], writes=["psB1"])
        pr.c("dve", lambda e: e.tensor_copy(bload[0:16, 0:128],
                                            psB[0:16, 512:640]),
             reads=["psB1", "bload"], writes=["bload"], hz=True)
        pr.d("act", lambda e: e.dma_start(
            out=row_b.rearrange("a (g p) -> (a g) p", p=128)[0:NTG, :],
            in_=bload[0:16, 0:128]),
            reads=["bload", "row_b"], writes=["row_b"], tag="qsf")
        pr.d("sp", lambda e: e.dma_start(out=smallrow[0:1, 0:TCORE],
                                         in_=row_b[0:1, 0:TCORE]),
             reads=["row_b", "sxinv_tile"], writes=["smallrow"], tag="qsl")

    if "s4a" in phases:
        s4a()

    def s4b():
        pr.c("pool", lambda e: e.memset(misc[0:1, 60:61], 0.0),
             writes=["gbuf0", "gvbuf0", "macc", "babs", "sxinv_tile",
                     "wslab0", "wslab1", "xq_res",
                     "outb_rep", "qs_tile", "wh2iT", "ki_b", "qt1", "qt2",
                     "zf", "kT", "yi",
                     "moslab0", "moslab1", "moslab2",
                     "woslab0", "woslab1", "woslab2",
                     "zin0", "zin1", "zin2", "gvin0", "gvin1", "gvin2"]
             + [f"oev{ro}q{qq}" for ro in range(2) for qq in range(4)],
             hz=True)
        bcast_row(smallrow, qs_tile, "qs_tile", "smallrow")
        pr.d("sp", lambda e: e.dma_start(out=smallrow[0:1, 0:D],
                                         in_=out_b[:, :]),
             reads=["smallrow", "qs_tile"], writes=["smallrow"], tag="obl")
        bcast_row(smallrow, outb_rep, "outb_rep", "smallrow")
        for icb in range(8):
            pr.d("sp", lambda e, icb=icb: e.dma_start(
                out=zf_b[0:16, 0:512].rearrange("p (c f) -> p c f", c=4),
                in_=w_h2i.rearrange("(c f) h -> h c f", f=128)[
                    :, 4 * icb:4 * icb + 4, :],
                allow_slow_non_contiguous=True),
                reads=["zf"], writes=["zf"], tag="wh2il")
            pr.c("dve", lambda e, icb=icb: e.tensor_copy(
                wh2iT[:, 512 * icb:512 * (icb + 1)], zf_b[0:16, 0:512]),
                reads=["zf"], writes=["wh2iT"], hz=True)
        for s_ in range(4):
            for hc in range(64):
                r = hc % 3
                pr.d("sp", lambda e, s_=s_, hc=hc, r=r: e.dma_start(
                    out=gvin[r][:, :],
                    in_=gv_scr[hc, :, 512 * s_:512 * (s_ + 1)]),
                    writes=([f"gvin{r}", "wslab0", "wslab1"]
                            if hc < 3 and s_ == 0 else [f"gvin{r}"]),
                    tag="gvld")
                pr.c("pool", lambda e, s_=s_, r=r: e.tensor_tensor(
                    qt1[:, :], gvin[r][:, :],
                    qs_tile[:, 512 * s_:512 * (s_ + 1)], ALU.mult),
                    reads=[f"gvin{r}", "qs_tile"], writes=["qt1"], hz=True)
                pr.c("dve", lambda e: ts(e, ki_b[:, :], qt1[:, :], 256.0,
                                         None, ALU.add),
                     reads=["qt1"], writes=["ki"], hz=True)
                pr.c("dve", lambda e: e.scalar_tensor_tensor(
                    qt2[:, :], ki_b[:, :], 256.0, qt1[:, :],
                    ALU.subtract, ALU.subtract),
                    reads=["ki", "qt1"], writes=["qt2"], hz=True)
                pr.c("dve", lambda e: e.scalar_tensor_tensor(
                    qt1[:, :], qt2[:, :], -0.5, ki_b[:, :],
                    ALU.is_le, ALU.add),
                    reads=["qt2", "ki"], writes=["qt1"], hz=True)
                pr.c("act", lambda e, hc=hc: e.activation(
                    kT_str[:, hc, :], qt1[:, :], AF.Copy, bias=-256.0,
                    scale=1.0),
                    reads=["qt1"],
                    writes=(["kT", "xq_res"] if hc == 0 and s_ == 0
                            else ["kT"]), hz=True)
            for qd in range(4):
                for hc in range(64):
                    r = hc % 3
                    for hf in range(2):
                        pr.d("sp" if hf == 0 else "act",
                             lambda e, qd=qd, hc=hc, r=r, hf=hf:
                             e.dma_start_transpose(
                                 moslab[r][:, 256 * hf:256 * (hf + 1)],
                                 mo_q[512 * qd + 256 * hf:
                                      512 * qd + 256 * (hf + 1),
                                      128 * hc:128 * (hc + 1)]),
                             writes=([f"moslab{r}", "tch", "wslab0",
                                      "wslab1"]
                                     if s_ == 0 and qd == 0 and hc < 3
                                     else [f"moslab{r}", "tch"]), tag="mold")
                    for tsub in range(4):
                        pr.c("pe", lambda e, hc=hc, tsub=tsub, r=r: e.matmul(
                            psA[:, 512 * tsub:512 * (tsub + 1)],
                            kT_str[:, hc, 128 * tsub:128 * (tsub + 1)],
                            moslab[r][:, 0:512],
                            start=(hc == 0), stop=(hc == 63)),
                            reads=[f"moslab{r}", "kT"], writes=[NQ[tsub]],
                            tag=f"s4bmm{qd}")
                for tsub in range(4):
                    ro = tsub % 2
                    pr.c("dve", lambda e, s_=s_, qd=qd, tsub=tsub, ro=ro:
                         e.scalar_tensor_tensor(
                             oev[ro][:, 512 * qd:512 * (qd + 1)],
                             psA[:, 512 * tsub:512 * (tsub + 1)],
                             so_tok[:, 4 * s_ + tsub:4 * s_ + tsub + 1],
                             outb_rep[:, 512 * qd:512 * (qd + 1)],
                             ALU.mult, ALU.add),
                         reads=[NQ[tsub], "so_tok", "outb_rep"],
                         writes=[f"oev{ro}q{qd}"])
                    if s_ == 0 and qd == 0 and tsub == 0 and "oevf" in taps:
                        t = tapout("oevf", (128, 512))
                        pr.d("sp", lambda e, t=t: e.dma_start(
                            out=t[:, :], in_=oev[0][:, 0:512]),
                            reads=["oev0q0"], writes=["tap_oevf"])
                    pr.d("pool", lambda e, s_=s_, qd=qd, tsub=tsub, ro=ro:
                         e.dma_start(
                             out=ffn_scr[512 * s_ + 128 * tsub:
                                         512 * s_ + 128 * (tsub + 1),
                                         512 * qd:512 * (qd + 1)],
                             in_=oev[ro][:, 512 * qd:512 * (qd + 1)]),
                         reads=[f"oev{ro}q{qd}"],
                         writes=[f"ffng{4 * s_ + tsub}q{qd}"], tag="oacc")

    if "s4b" in phases:
        s4b()

    def s5():
        for s_ in range(4):
            for ic in range(32):
                r = ic % 3
                pr.d("sp", lambda e, s_=s_, ic=ic, r=r: e.dma_start(
                    out=zin[r][:, :],
                    in_=z_scr[ic, :, 512 * s_:512 * (s_ + 1)]),
                    writes=[f"zin{r}"], tag="zld")
                pr.c("pe", lambda e, s_=s_, ic=ic: e.matmul(
                    psB[:, 512:1024], wh2iT[:, 128 * ic:128 * (ic + 1)],
                    yful_bf[0:16, 512 * s_:512 * (s_ + 1)],
                    start=True, stop=True),
                    reads=["wh2iT", "yful"], writes=["psB1"], tag="y1mm")
                pr.c("act", lambda e, r=r: e.activation(
                    zf_b[:, :], zin[r][:, :], AF.Silu, bias=cst[:, 2:3],
                    scale=1.0),
                    reads=[f"zin{r}", "cst"], writes=["zf"], hz=True)
                pr.c("dve", lambda e, ic=ic: e.tensor_tensor(
                    yi_str[:, ic, :], zf_b[:, :], psB[:, 512:1024],
                    ALU.mult),
                    reads=["zf", "psB1"],
                    writes=(["yi", "kT"] if ic == 0 else ["yi"]), hz=True)
            for qd in range(4):
                for ic in range(32):
                    r = ic % 3
                    for hf in range(2):
                        pr.d("act" if hf == 0 else "sp",
                             lambda e, qd=qd, ic=ic, r=r, hf=hf:
                             e.dma_start_transpose(
                                 woslab[r][:, 256 * hf:256 * (hf + 1)],
                                 wo_q[512 * qd + 256 * hf:
                                      512 * qd + 256 * (hf + 1),
                                      128 * ic:128 * (ic + 1)]),
                             writes=([f"woslab{r}", "tch", "wslab1"]
                                     if qd == 0 and ic < 3
                                     else [f"woslab{r}", "tch"]), tag="wold")
                    for tsub in range(4):
                        pr.c("pe", lambda e, ic=ic, tsub=tsub, r=r: e.matmul(
                            psA[:, 512 * tsub:512 * (tsub + 1)],
                            yi_str[:, ic, 128 * tsub:128 * (tsub + 1)],
                            woslab[r][:, 0:512],
                            start=(ic == 0), stop=(ic == 31)),
                            reads=[f"woslab{r}", "yi"], writes=[NQ[tsub]],
                            tag=f"s5mm{qd}")
                for tsub in range(4):
                    ro = tsub % 2
                    g = 4 * s_ + tsub
                    ob, obk = (qt1, "qt1") if ro == 0 else (gvin[0], "gvin0")
                    fb, fbk = (qt2, "qt2") if ro == 0 else (gvin[1], "gvin1")
                    rows = slice(512 * s_ + 128 * tsub,
                                 512 * s_ + 128 * (tsub + 1))
                    cols = slice(512 * qd, 512 * (qd + 1))
                    pr.d("sp", lambda e, ob=ob, rows=rows, cols=cols:
                         e.dma_start(out=ob[:, :], in_=out_sl[rows, cols]),
                         reads=[f"outg{g}q{qd}"], writes=[obk], tag="obld")
                    pr.d("act", lambda e, fb=fb, rows=rows, cols=cols:
                         e.dma_start(out=fb[:, :], in_=ffn_scr[rows, cols]),
                         reads=[f"ffng{g}q{qd}"], writes=[fbk], tag="fbld")
                    pr.c("dve", lambda e, qd=qd, tsub=tsub, ro=ro, ob=ob:
                         e.tensor_tensor(
                             oev[ro][:, 512 * qd:512 * (qd + 1)],
                             psA[:, 512 * tsub:512 * (tsub + 1)],
                             ob[:, :], ALU.add),
                         reads=[NQ[tsub], obk], writes=[f"oev{ro}q{qd}"])
                    pr.c("dve", lambda e, qd=qd, ro=ro, fb=fb:
                         e.tensor_tensor(
                             oev[ro][:, 512 * qd:512 * (qd + 1)],
                             oev[ro][:, 512 * qd:512 * (qd + 1)],
                             fb[:, :], ALU.add),
                         reads=[f"oev{ro}q{qd}", fbk],
                         writes=[f"oev{ro}q{qd}"], hz=True)
                    pr.d("pool", lambda e, qd=qd, ro=ro, rows=rows, cols=cols:
                         e.dma_start(out=out_sl[rows, cols],
                                     in_=oev[ro][:, 512 * qd:512 * (qd + 1)]),
                         reads=[f"oev{ro}q{qd}"],
                         writes=[f"outg{g}q{qd}"], tag="ost")

    if "s5" in phases:
        s5()

    if "xrT" in taps:
        t = tapout("xrT", (128, NDC, TH), BF16)
        pr.d("sp", lambda e, t=t: e.dma_start(out=t[:, :, :], in_=xrT[:, :, :]),
             reads=["xrT"], writes=["tap_xrT"])
    if "xq" in taps:
        t = tapout("xq", (NDC, 128, TCORE), BF16)
        pr.d("sp", lambda e, t=t: e.dma_start(out=t[:, :, :], in_=xq_scr[:, :, :]),
             reads=["xq_scr"], writes=["tap_xq"])
    if "sxinv" in taps:
        t = tapout("sxinv", (128, NTG))
        pr.d("sp", lambda e, t=t: e.dma_start(out=t[:, :], in_=sxi_tok[:, :]),
             reads=["sxi_tok"], writes=["tap_sxinv"])
    if "winq" in taps:
        t = tapout("winq", (WINP, D), BF16)
        pr.d("sp", lambda e, t=t: e.dma_start(out=t[:, :], in_=win_q[:, :]),
             reads=["win_q"], writes=["tap_winq"])
    if "mgq" in taps:
        t = tapout("mgq", (HID, D), BF16)
        pr.d("sp", lambda e, t=t: e.dma_start(out=t[:, :], in_=mg_q[:, :]),
             reads=["mg_q"], writes=["tap_mgq"])
    if "moq" in taps:
        t = tapout("moq", (D, HID), BF16)
        pr.d("sp", lambda e, t=t: e.dma_start(out=t[:, :], in_=mo_q[:, :]),
             reads=["mo_q"], writes=["tap_moq"])
    if "ffnscr" in taps:
        t = tapout("ffnscr", (TCORE, D))
        pr.d("sp", lambda e, t=t: e.dma_start(out=t[:, :], in_=ffn_scr[:, :]),
             reads=[f"ffng{gg}q{qq}" for gg in range(NTG)
                    for qq in range(4)],
             writes=["tap_ffnscr"])
    if "yl" in taps:
        t = tapout("yl", (16, TCORE))
        pr.d("sp", lambda e, t=t: e.dma_start(out=t[:, :], in_=y_loc[0:16, :]),
             reads=["y_loc"], writes=["tap_yl"])
        t2 = tapout("cum", (16, TCORE))
        pr.d("sp", lambda e, t2=t2: e.dma_start(out=t2[:, :],
                                                in_=cum_t[0:16, :]),
             reads=["cum"], writes=["tap_cum"])
        t3 = tapout("dtsp", (16, TCORE))
        pr.d("sp", lambda e, t3=t3: e.dma_start(out=t3[:, :],
                                                in_=dt_sp[0:16, :]),
             reads=["dt_sp"], writes=["tap_dtsp"])
        t4 = tapout("xbar", (16, TCORE))
        pr.d("sp", lambda e, t4=t4: e.dma_start(out=t4[:, :],
                                                in_=xbar[0:16, :]),
             reads=["xbar"], writes=["tap_xbar"])
        t5 = tapout("acol", (16, 1))
        pr.d("sp", lambda e, t5=t5: e.dma_start(out=t5[:, :], in_=A_col[:, :]),
             reads=["A_col"], writes=["tap_acol"])
        t6 = tapout("hin", (128, 2))
        pr.d("sp", lambda e, t6=t6: e.dma_start(out=t6[:, :], in_=hinA[:, :]),
             reads=["hin"], writes=["tap_hin"])
        t7 = tapout("ggtok", (128, NTG))
        pr.d("sp", lambda e, t7=t7: e.dma_start(out=t7[:, :],
                                                in_=gg_tok[:, :]),
             reads=["gg_tok"], writes=["tap_ggtok"])
    if "wht" in taps:
        t = tapout("wht", (16, DIN), BF16)
        pr.d("sp", lambda e, t=t: e.dma_start(out=t[:, :], in_=wh2iT[:, :]),
             reads=["wh2iT"], writes=["tap_wht"])
        t2 = tapout("yfd", (16, TCORE), BF16)
        pr.d("sp", lambda e, t2=t2: e.dma_start(out=t2[:, :],
                                                in_=yful_bf[0:16, :]),
             reads=["yful"], writes=["tap_yfd"])
    if "outbr" in taps:
        t = tapout("outbr", (128, D))
        pr.d("sp", lambda e, t=t: e.dma_start(out=t[:, :], in_=outb_rep[:, :]),
             reads=["outb_rep"], writes=["tap_outbr"])
        t2 = tapout("oev0", (128, D))
        pr.d("sp", lambda e, t2=t2: e.dma_start(out=t2[:, :], in_=oev[0][:, :]),
             reads=["oev0q0", "oev0q1", "oev0q2", "oev0q3"],
             writes=["tap_oev0"])
    if "ktd" in taps:
        t = tapout("ktd", (128, 64, 512), BF16)
        pr.d("sp", lambda e, t=t: e.dma_start(out=t[:, :, :],
                                              in_=kT_str[:, :, :]),
             reads=["kT"], writes=["tap_ktd"])
        t2 = tapout("qst", (128, TCORE))
        pr.d("sp", lambda e, t2=t2: e.dma_start(out=t2[:, :],
                                                in_=qs_tile[:, :]),
             reads=["qs_tile"], writes=["tap_qst"])
        t3 = tapout("sot", (128, NTG))
        pr.d("sp", lambda e, t3=t3: e.dma_start(out=t3[:, :],
                                                in_=so_tok[:, :]),
             reads=["so_tok"], writes=["tap_sot"])
    if "csb0" in taps:
        t = tapout("csb0", (128, TCORE))
        pr.d("sp", lambda e, t=t: e.dma_start(out=t[:, :], in_=Csb[0][:, :]),
             reads=["Csb0"], writes=["tap_csb0"])
        t2 = tapout("bsb0", (128, TCORE))
        pr.d("sp", lambda e, t2=t2: e.dma_start(out=t2[:, :],
                                                in_=Bsb[0][:, :]),
             reads=["Bsb0"], writes=["tap_bsb0"])
        t3 = tapout("p0", (128, TCORE))
        pr.d("sp", lambda e, t3=t3: e.dma_start(out=t3[:, :], in_=P0[:, :]),
             reads=["P0"], writes=["tap_p0"])
    if "hb0" in taps:
        t = tapout("hb0", (128, TCORE))
        pr.d("sp", lambda e, t=t: e.dma_start(out=t[:, :], in_=hb[0][:, :]),
             reads=["hb0"], writes=["tap_hb0"])
    if "gvd" in taps:
        t = tapout("gvd", (64, 128, TCORE))
        pr.d("sp", lambda e, t=t: e.dma_start(out=t[:, :, :],
                                              in_=gv_scr[:, :, :]),
             reads=["gv_scr"], writes=["tap_gvd"])
    if "zd" in taps:
        t = tapout("zd", (32, 128, TCORE), BF16)
        pr.d("sp", lambda e, t=t: e.dma_start(out=t[:, :, :],
                                              in_=z_scr[:, :, :]),
             reads=["z_scr"], writes=["tap_zd"])
    if "gammas" in taps:
        t = tapout("gammas", (128, 8))
        pr.d("sp", lambda e, t=t: e.dma_start(out=t[:, :], in_=gcols[:, :]),
             reads=["gcols"], writes=["tap_gammas"])
    if "pacc" in taps:
        t = tapout("pacc", (128, 4))
        pr.d("sp", lambda e, t=t: e.dma_start(out=t[:, :], in_=pacc[:, :]),
             reads=["pacc"], writes=["tap_pacc"])
        t2 = tapout("ag1", (8, 4))
        pr.d("sp", lambda e, t2=t2: e.dma_start(out=t2[:, :],
                                                in_=misc[0:8, 48:52]),
             reads=["ag1_sb"], writes=["tap_ag1"])
    if "mgsh" in taps:
        t = tapout("mgsh", (1024, D), BF16)
        pr.d("sp", lambda e, t=t: e.dma_start(out=t[:, :], in_=mg_shard[:, :]),
             reads=["mg_shard"], writes=["tap_mgsh"])

    with nc.Block() as block:
        pr.build(nc, block, ctx)
    return nc, tap_outs


_CACHE = {}


def kernel(**inputs):
    if "nc" not in _CACHE:
        _CACHE["nc"], _ = build_nc()
    nc = _CACHE["nc"]
    in_maps = host_inputs(inputs)
    res = run_bass_kernel_spmd(nc, in_maps, core_ids=list(range(NCORES)))
    out = np.concatenate([r["out_sl"] for r in res.results], axis=0)
    return np.ascontiguousarray(out.reshape(1, SEQ, D))


# version 11
# speedup vs baseline: 1.1893x; 1.1893x over previous
from contextlib import ExitStack

import numpy as np

import concourse.bass as bass
import concourse.mybir as mybir
from concourse.bass_utils import run_bass_kernel_spmd

F32 = mybir.dt.float32
BF16 = mybir.dt.bfloat16
I32 = mybir.dt.int32
AF = mybir.ActivationFunctionType
ALU = mybir.AluOpType
AX = mybir.AxisListType

NCORES = 8
SEQ, D = 16384, 2048
TCORE = SEQ // NCORES
HALO = 8
TH = TCORE + HALO
DIN, H, NST, DH = 4096, 16, 16, 256
HID = 8192
EPS_LN = 1e-5
EPS_RMS = 1.1920929e-07
NDC = D // 128
NTB = TCORE // 512
NTG = TCORE // 128
WINP = 8960
WIN_SH = WINP // NCORES
N_WCH = 69

_PHASES = ("s0", "s1", "s2", "s3", "s4a", "s4b", "s5")
ACCUM_ON = True

ENGS = ("pe", "act", "dve", "pool", "sp")
DMA_ROT = 8


class Op:
    __slots__ = ("eng", "kind", "emit", "reads", "writes", "deps", "inc_id",
                 "dma_idx", "tag", "hz")

    def __init__(self, eng, kind, emit, reads, writes, tag="", hz=False):
        self.eng, self.kind, self.emit = eng, kind, emit
        self.reads, self.writes, self.tag = tuple(reads), tuple(writes), tag
        self.deps = self.inc_id = self.dma_idx = None
        self.hz = hz


class Prog:

    def __init__(self):
        self.ops = []

    def c(self, eng, emit, reads=(), writes=(), tag="", hz=False):
        self.ops.append(Op(eng, "c", emit, reads, writes, tag, hz))

    def d(self, queue, emit, reads=(), writes=(), tag=""):
        self.ops.append(Op(queue, "d", emit, reads, writes, tag))

    def cc(self, emit, reads=(), writes=(), tag=""):
        self.ops.append(Op("pool", "cc", emit, reads, writes, tag))

    HZ_GAP = 10**9

    def build(self, nc, block, ctx):
        ops = self.ops
        pos = {}
        ecnt = {e: 0 for e in ENGS}
        for i, op in enumerate(ops):
            pos[i] = ecnt[op.eng]
            ecnt[op.eng] += 1
        last_w, readers, need_inc = {}, {}, set()
        for i, op in enumerate(ops):
            dep = set()
            for k in op.reads:
                if k in last_w:
                    dep.add(last_w[k])
            for k in op.writes:
                if k in last_w:
                    dep.add(last_w[k])
                dep.update(readers.get(k, ()))

            def keep(j, i=i, op=op):
                oj = ops[j]
                if oj.eng != op.eng or oj.kind != "c":
                    return True
                if oj.eng == "pe":
                    return False
                if oj.hz:
                    return True
                return pos[i] - pos[j] <= self.HZ_GAP
            real = {j for j in dep if j != i and keep(j)}
            op.deps = real
            need_inc.update(real)
            for k in op.reads:
                readers.setdefault(k, []).append(i)
            for k in op.writes:
                last_w[k] = i
                readers[k] = []
        eng_ctr = {e: 0 for e in ENGS}
        dma_ctr = {e: 0 for e in ENGS}
        cc_ctr = 0
        for i, op in enumerate(ops):
            if op.kind == "d":
                op.dma_idx = dma_ctr[op.eng]
                dma_ctr[op.eng] += 1
            elif op.kind == "cc":
                cc_ctr += 1
                op.inc_id = cc_ctr
            elif i in need_inc:
                eng_ctr[op.eng] += 1
                op.inc_id = eng_ctr[op.eng]
        esem = {e: ctx.enter_context(nc.semaphore(f"e_{e}")) for e in ENGS}
        dsem = {e: [ctx.enter_context(nc.semaphore(f"d_{e}{r}"))
                    for r in range(DMA_ROT)]
                for e in ENGS if dma_ctr[e]}
        ccsem = ctx.enter_context(nc.semaphore("ccs")) if cc_ctr else None

        def tgt(j):
            oj = ops[j]
            if oj.kind == "d":
                return (dsem[oj.eng][oj.dma_idx % DMA_ROT],
                        16 * (oj.dma_idx // DMA_ROT + 1))
            if oj.kind == "cc":
                return (ccsem, oj.inc_id)
            return (esem[oj.eng], oj.inc_id)

        per_eng = {e: [i for i, op in enumerate(ops) if op.eng == e]
                   for e in ENGS}

        def mk(ename):
            idxs = per_eng[ename]

            def emitfn(eng):
                floors = {}
                for i in idxs:
                    op = ops[i]
                    waits = {}
                    for j in sorted(op.deps):
                        sem, val = tgt(j)
                        if waits.get(sem, -1) < val:
                            waits[sem] = val
                    if op.kind == "d" and op.dma_idx >= DMA_ROT:
                        sem = dsem[ename][op.dma_idx % DMA_ROT]
                        val = 16 * (op.dma_idx // DMA_ROT)
                        if waits.get(sem, -1) < val:
                            waits[sem] = val
                    for sem, val in waits.items():
                        if floors.get(sem, 0) < val:
                            eng.wait_ge(sem, val)
                            floors[sem] = val
                    ins = op.emit(eng)
                    if op.kind == "d":
                        ins.then_inc(dsem[ename][op.dma_idx % DMA_ROT], 16)
                    elif op.kind == "cc":
                        ins.then_inc(ccsem, 1)
                    elif op.inc_id is not None:
                        ins.then_inc(esem[ename], 1)
                if ename == "sp":
                    for e, n in dma_ctr.items():
                        for r in range(DMA_ROT):
                            cnt = (n - r + DMA_ROT - 1) // DMA_ROT
                            if cnt > 0:
                                eng.wait_ge(dsem[e][r], 16 * cnt)
            return emitfn

        names = {"pe": "tensor", "act": "scalar", "dve": "vector",
                 "pool": "gpsimd", "sp": "sync"}
        for e in ENGS:
            if per_eng[e]:
                getattr(block, names[e])(mk(e))


def ts(e, out, in0, s1, s2=None, op0=ALU.mult, op1=ALU.bypass):
    return e.tensor_scalar(out, in0, s1, s2, op0, op1)


def host_inputs(inputs):
    x = np.ascontiguousarray(
        np.asarray(inputs["L0_attn_x"]).reshape(SEQ, D), dtype=np.float32)
    W_in = np.asarray(inputs["L0_attn_W_in"], dtype=np.float32)
    win_perm = np.zeros((WINP, D), np.float32)
    win_perm[0:2 * DIN] = W_in[0:2 * DIN]
    win_perm[8192:8208] = W_in[8192:8208]
    win_perm[8320:8576] = W_in[8208:8464]
    win_perm[8576:8832] = W_in[8464:8720]
    gate_w = np.asarray(inputs["L0_ffn_gate_w"], dtype=np.float32)
    val_w = np.asarray(inputs["L0_ffn_val_w"], dtype=np.float32)
    out_w = np.asarray(inputs["L0_ffn_out_w"], dtype=np.float32)
    wout = np.asarray(inputs["L0_attn_W_out"], dtype=np.float32)
    small = dict(
        w_norm=np.ascontiguousarray(
            np.asarray(inputs["L0_attn_w_norm"], np.float32).reshape(1, D)),
        dt_bias=np.ascontiguousarray(
            np.asarray(inputs["L0_attn_dt_bias"], np.float32).reshape(H, 1)),
        a_log=np.ascontiguousarray(
            np.asarray(inputs["L0_attn_A_log"], np.float32).reshape(H, 1)),
        w_conv=np.ascontiguousarray(
            np.asarray(inputs["L0_attn_w_conv"], np.float32).reshape(DIN, 4)),
        w_h2i=np.ascontiguousarray(
            np.asarray(inputs["L0_attn_W_h2i"], np.float32)),
        gate_b=np.ascontiguousarray(
            np.asarray(inputs["L0_ffn_gate_b"], np.float32).reshape(64, 128)),
        val_b=np.ascontiguousarray(
            np.asarray(inputs["L0_ffn_val_b"], np.float32).reshape(64, 128)),
        out_b=np.ascontiguousarray(
            np.asarray(inputs["L0_ffn_out_b"], np.float32).reshape(1, D)),
    )
    maps = []
    for k in range(NCORES):
        lo = TCORE * k - HALO
        xs = np.zeros((TH, D), np.float32)
        s0_ = max(0, lo)
        xs[s0_ - lo:] = x[s0_:TCORE * (k + 1)]
        m = dict(small)
        m["x_sl"] = xs
        m["halo_on"] = np.full((HALO, 1), 0.0 if k == 0 else 1.0, np.float32)
        cm = np.zeros((128, 8), np.float32)
        cm[:, :k] = 1.0
        m["cmask"] = cm
        m["win_sl"] = np.ascontiguousarray(win_perm[WIN_SH * k:WIN_SH * (k + 1)])
        m["gate_sl"] = np.ascontiguousarray(gate_w[1024 * k:1024 * (k + 1)])
        m["val_sl"] = np.ascontiguousarray(val_w[1024 * k:1024 * (k + 1)])
        m["outw_sl"] = np.ascontiguousarray(out_w[256 * k:256 * (k + 1)])
        m["wout_sl"] = np.ascontiguousarray(wout[256 * k:256 * (k + 1)])
        maps.append(m)
    return maps


def build_nc(taps=(), phases=_PHASES):
    taps = set(taps)
    nc = bass.Bass("TRN2", target_bir_lowering=False)
    pr = Prog()
    ctx = ExitStack()
    nc._kernel_ctx = ctx

    def din(name, shape, dt=F32):
        return nc.dram_tensor(name, list(shape), dt, kind="ExternalInput").ap()

    def dout(name, shape, dt=F32):
        return nc.dram_tensor(name, list(shape), dt, kind="ExternalOutput").ap()

    def dint(name, shape, dt=F32, shared=False):
        return nc.dram_tensor(name, list(shape), dt, kind="Internal",
                              addr_space=("Shared" if shared else "Local")).ap()

    x_sl = din("x_sl", (TH, D))
    halo_on = din("halo_on", (HALO, 1))
    win_sl = din("win_sl", (WIN_SH, D))
    gate_sl = din("gate_sl", (1024, D))
    val_sl = din("val_sl", (1024, D))
    outw_sl = din("outw_sl", (256, HID))
    wout_sl = din("wout_sl", (256, DIN))
    w_norm = din("w_norm", (1, D))
    dt_bias = din("dt_bias", (H, 1))
    a_log = din("a_log", (H, 1))
    cmask_in = din("cmask", (128, 8))
    w_conv = din("w_conv", (DIN, 4))
    w_h2i = din("w_h2i", (DIN, H))
    gate_b = din("gate_b", (64, 128))
    val_b = din("val_b", (64, 128))
    out_b = din("out_b", (1, D))
    out_sl = dout("out_sl", (TCORE, D))

    win_shard = dint("win_shard", (WIN_SH, D), BF16)
    win_q = dint("win_q", (WINP, D), BF16, shared=True)
    mg_shard = dint("mg_shard", (1024, D), BF16)
    mg_q = dint("mg_q", (HID, D), BF16, shared=True)
    mv_shard = dint("mv_shard", (1024, D), BF16)
    mv_q = dint("mv_q", (HID, D), BF16, shared=True)
    mo_shard = dint("mo_shard", (256, HID), BF16)
    mo_q = dint("mo_q", (D, HID), BF16, shared=True)
    wo_shard = dint("wo_shard", (256, DIN), BF16)
    wo_q = dint("wo_q", (D, DIN), BF16, shared=True)
    ag1_in = dint("ag1_in", (1, 4))
    ag1_out = dint("ag1_out", (NCORES, 4), F32, shared=True)
    ag3_in = dint("ag3_in", (1, 512))
    ag3_out = dint("ag3_out", (NCORES, 512), F32, shared=True)
    z_scr = dint("z_scr", (32, 128, TCORE), BF16)
    gv_scr = dint("gv_scr", (64, 128, TCORE))
    xq_scr = dint("xq_scr", (NDC, 128, TCORE), BF16)
    row_b = dint("row_b", (1, 4096))
    ffn_scr = dint("ffn_scr", (TCORE, D))
    attn_scr = dint("attn_scr", (TCORE, D))

    tap_outs = {}

    def tapout(name, shape, dt=F32):
        tap_outs[name] = dout("tap_" + name, shape, dt)
        return tap_outs[name]

    ARENA = 202 * 1024
    _arena = nc.alloc_sbuf_tensor("arena", [128, ARENA], mybir.dt.uint8)
    _abase = nc.lookup_mloc(_arena).addr

    def sb_at(name, shape, dt, off):
        assert off + int(np.prod(shape[1:])) * mybir.dt.size(dt) <= ARENA, (
            name, off)
        return nc.alloc_sbuf_tensor_at(name, list(shape), dt,
                                       offset=_abase + off)

    class Bump:
        def __init__(self, base):
            self.cur = base

        def take(self, name, shape, dt):
            per = int(np.prod(shape[1:])) * mybir.dt.size(dt)
            t = sb_at(name, shape, dt, self.cur)
            self.cur += (per + 31) & ~31
            return t

    PB = Bump(0)
    wn_rep = PB.take("wn_rep", [128, D], F32)
    P0 = PB.take("P0", [128, TCORE], F32)
    P1 = PB.take("P1", [128, TCORE], F32)
    ident_f = PB.take("ident_f", [128, 128], F32)
    ident_bf = PB.take("ident_bf", [128, 128], BF16)
    selT_lo = PB.take("selT_lo", [128, 16], F32)
    selT_hi = PB.take("selT_hi", [128, 16], F32)
    sel_loT = PB.take("sel_loT", [16, 128], F32)
    sel_hiT = PB.take("sel_hiT", [16, 128], F32)
    ones_col = PB.take("ones_col", [128, 1], F32)
    ones_col_bf = PB.take("ones_col_bf", [128, 1], BF16)
    ones_row = PB.take("ones_row", [1, 128], F32)
    xbar = PB.take("xbar", [16, TCORE], F32)
    y_loc = PB.take("y_loc", [16, TCORE], F32)
    yful_bf = PB.take("yful_bf", [16, TCORE], BF16)
    misc = PB.take("misc", [128, 64], F32)
    gcols = PB.take("gcols", [128, 8], F32)
    sx_col = PB.take("sx_col", [128, 1], F32)
    sxi_tok = PB.take("sxi_tok", [128, NTG], F32)
    so_tok = PB.take("so_tok", [128, NTG], F32)
    gg_tok = PB.take("gg_tok", [128, NTG], F32)
    dtb_col = PB.take("dtb_col", [16, 1], F32)
    A_col = PB.take("A_col", [16, 1], F32)
    S_colA = PB.take("S_colA", [128, 2], F32)
    D_colA = PB.take("D_colA", [128, 2], F32)
    hinA = PB.take("hinA", [128, 2], F32)
    agS = PB.take("agS", [128, 32], F32)
    wc_sb = PB.take("wc_sb", [128, 128], F32)
    cmask = PB.take("cmask_sb", [128, 8], F32)
    xbar_row = PB.take("xbar_row", [1, TCORE], F32)
    dgs = [PB.take(f"dg{j}", [128, 128], BF16) for j in range(4)]
    gb_sb = PB.take("gb_sb", [128, 64], F32)
    vb_sb = PB.take("vb_sb", [128, 64], F32)
    bload = PB.take("bload", [64, 128], F32)
    _SMALLROW_OFF = PB.cur
    smallrow = PB.take("smallrow", [1, D], F32)
    cst = PB.take("cst", [128, 8], F32)
    PERS_END = PB.cur
    assert PERS_END < 70 * 1024, PERS_END

    AB = PERS_END
    xrT = sb_at("xrT", [128, NDC, TH], BF16, AB)
    xq_res = sb_at("xq_res", [128, NDC, TCORE], BF16, AB)
    kT_str = sb_at("kT_str", [128, 64, 512], BF16, AB)
    yi_str = sb_at("yi_str", [128, 32, 512], BF16, AB)
    A_END = AB + ((NDC * TH * 2 + 31) & ~31)

    BB = A_END
    B2 = Bump(BB)
    wslab = [B2.take(f"wslab{i}", [128, NDC, 256], BF16) for i in range(2)]
    B_END = B2.cur
    S1B = Bump(BB)
    xg = [S1B.take(f"xg{i}", [128, D], F32) for i in range(2)]
    xng = [S1B.take(f"xng{i}", [128, D], F32) for i in range(2)]
    xr_bf = S1B.take("xr_bf", [128, D], BF16)
    kx_bf = S1B.take("kx_bf", [128, D], BF16)
    q1_f = S1B.take("q1_f", [128, D], F32)
    k0_i = S1B.take("k0_i", [128, D], I32)
    dp_f = S1B.take("dp_f", [128, D], F32)
    kq_st = S1B.take("kq_st", [128, NDC, 128], BF16)
    kf_f = q1_f
    S4B = Bump(BB)
    moslab = [S4B.take(f"moslab{i}", [128, 1024], BF16) for i in range(3)]
    gvin = [S4B.take(f"gvin{i}", [128, 512], F32) for i in range(3)]
    woslab = [S4B.take(f"woslab{i}", [128, 1024], BF16) for i in range(3)]
    zin = [S4B.take(f"zin{i}", [128, 512], BF16) for i in range(3)]
    oev = [S4B.take(f"oev{i}", [128, D], F32) for i in range(2)]
    outb_rep = S4B.take("outb_rep", [128, D], F32)
    qs_tile = S4B.take("qs_tile", [128, TCORE], F32)
    wh2iT = S4B.take("wh2iT", [16, DIN], BF16)
    ki_b = S4B.take("ki_b", [128, 512], I32)
    qt1 = S4B.take("qt1", [128, 512], F32)
    qt2 = S4B.take("qt2", [128, 512], F32)
    zf_b = S4B.take("zf_b", [128, 512], F32)

    CB = B_END
    C2 = Bump(CB)
    Bsb = [C2.take(f"Bsb{i}", [128, TCORE], F32) for i in range(2)]
    Csb = [C2.take(f"Csb{i}", [128, TCORE], F32) for i in range(2)]
    pxi_bf = C2.take("pxi_bf", [128, TH], BF16)
    xi_bf = C2.take("xi_bf", [128, TCORE], BF16)
    zev = [C2.take(f"zev{i}", [128, 1024], BF16) for i in range(2)]
    C_END = C2.cur
    S3B = Bump(AB)
    ub = [S3B.take(f"ub{i}", [128, TCORE], F32) for i in range(2)]
    dab = [S3B.take(f"dab{i}", [128, TCORE], F32) for i in range(2)]
    hb = [S3B.take(f"hb{i}", [128, TCORE], F32) for i in range(2)]
    htmp = S3B.take("htmp", [16, TCORE], F32)
    dt_sp = S3B.take("dt_sp", [16, TCORE], F32)
    assert S3B.cur <= A_END + 1, (S3B.cur, A_END)
    cum_t = sb_at("cum_t", [16, TCORE], F32, _SMALLROW_OFF)
    S3_END = max(S3B.cur, C_END)
    S4A = Bump(CB)
    gbuf = [S4A.take("gbuf0", [128, TCORE], F32)]
    gvbuf = [S4A.take("gvbuf0", [128, TCORE], F32)]
    macc = S4A.take("macc", [128, TCORE], F32)
    babs = S4A.take("babs", [128, TCORE], F32)
    sxinv_tile = S4A.take("sxinv_tile", [128, TCORE], F32)
    PR = Bump(BB)
    pw = [PR.take(f"pw{i}", [128, D], F32) for i in range(2)]
    pq = [PR.take(f"pq{i}", [128, D], BF16) for i in range(2)]
    pacc = PR.take("pacc", [128, 4], F32)
    ptmp = PR.take("ptmp", [128, D], F32)
    ptmp2 = PR.take("ptmp2", [128, D], F32)
    LIMIT = ARENA
    for v in (S3_END, S4A.cur, PR.cur, B_END, S1B.cur, S4B.cur):
        assert v < LIMIT, (v, LIMIT)

    psA = ctx.enter_context(nc.psum_tensor("psA", [128, 2048], F32))
    psB = ctx.enter_context(nc.psum_tensor("psB", [128, 1536], F32))
    psT = ctx.enter_context(nc.psum_tensor("psT", [128, 1024], BF16))
    KA = ["psA0", "psA1", "psA2", "psA3"]
    KB = ["psB0", "psB1", "psB2"]

    def bcast_row(srow, dst, dkey, skey, nblk=4):
        for blk in range(nblk):
            pr.c("pe", lambda e, blk=blk: e.matmul(
                psA[:, 512 * blk:512 * (blk + 1)], ones_row[:, :],
                srow[0:1, 512 * blk:512 * (blk + 1)], start=True, stop=True),
                reads=[skey, "onr"], writes=[KA[blk]])
        pr.c("dve", lambda e, nblk=nblk: e.tensor_copy(
            dst[:, 0:512 * nblk], psA[:, 0:512 * nblk]),
            reads=KA[:nblk], writes=[dkey])

    def consts():
        pr.c("pool", lambda e: e.memset(ident_f[:, :], 0.0), writes=["idf"], hz=True)
        pr.c("pool", lambda e: e.affine_select(
            out=ident_f[:, :], in_=ident_f[:, :], compare_op=ALU.not_equal,
            fill=1.0, base=0, pattern=[[-1, 128]], channel_multiplier=1),
            reads=["idf"], writes=["idf"], hz=True)
        pr.c("dve", lambda e: e.tensor_copy(ident_bf[:, :], ident_f[:, :]),
             reads=["idf"], writes=["idb"])
        for t, off, key in ((selT_lo, 0, "stlo"), (selT_hi, 128, "sthi")):
            pr.c("pool", lambda e, t=t: e.memset(t[:, :], 1.0), writes=[key], hz=True)
            pr.c("pool", lambda e, t=t, off=off: e.affine_select(
                out=t[:, :], in_=t[:, :], compare_op=ALU.is_ge, fill=0.0,
                base=off, pattern=[[-16, 16]], channel_multiplier=1),
                reads=[key], writes=[key], hz=True)
            pr.c("pool", lambda e, t=t, off=off: e.affine_select(
                out=t[:, :], in_=t[:, :], compare_op=ALU.is_ge, fill=0.0,
                base=15 - off, pattern=[[16, 16]], channel_multiplier=-1),
                reads=[key], writes=[key], hz=True)
        pr.c("pe", lambda e: e.transpose(psB[0:16, 0:128], selT_lo[:, :],
                                         ident_f[:, :]),
             reads=["stlo", "idf"], writes=["psB0"])
        pr.c("dve", lambda e: e.tensor_copy(sel_loT[:, :], psB[0:16, 0:128]),
             reads=["psB0"], writes=["slT"])
        pr.c("pe", lambda e: e.transpose(psB[0:16, 512:640], selT_hi[:, :],
                                         ident_f[:, :]),
             reads=["sthi", "idf"], writes=["psB1"])
        pr.c("dve", lambda e: e.tensor_copy(sel_hiT[:, :], psB[0:16, 512:640]),
             reads=["psB1"], writes=["shT"])
        pr.c("pool", lambda e: e.memset(ones_col[:, :], 1.0), writes=["onc"], hz=True)
        pr.c("pool", lambda e: e.memset(ones_col_bf[:, :], 1.0),
             writes=["onc"], hz=True)
        pr.c("pool", lambda e: e.memset(cst[:, 0:1], EPS_LN), writes=["cst"], hz=True)
        pr.c("pool", lambda e: e.memset(cst[:, 1:2], EPS_RMS), writes=["cst"], hz=True)
        pr.c("pool", lambda e: e.memset(cst[:, 2:3], 0.0), writes=["cst"], hz=True)
        pr.c("pool", lambda e: e.memset(cst[:, 4:5], 1.0), writes=["cst"],
             hz=True)
        pr.c("pool", lambda e: e.memset(ones_row[:, :], 1.0), writes=["onr"], hz=True)
        pr.d("sp", lambda e: e.dma_start(out=cst[0:HALO, 3:4],
                                         in_=halo_on[:, :]),
             reads=["cst"], writes=["cst"], tag="halo")
        pr.d("sp", lambda e: e.dma_start(out=smallrow[0:1, :],
                                         in_=w_norm[:, :]),
             writes=["wn_row"], tag="wn")
        bcast_row(smallrow, wn_rep, "wn_rep", "wn_row")

    consts()

    def s0():
        pr.c("dve", lambda e: e.memset(pacc[:, :], 0.0), writes=["pacc"], hz=True)
        for i in range(WIN_SH // 112):
            b = i % 2
            pr.d("sp", lambda e, i=i, b=b: e.dma_start(
                out=pw[b][0:112, :], in_=win_sl[112 * i:112 * (i + 1), :]),
                writes=[f"pw{b}"], tag="s0win")
            pr.c("dve", lambda e, b=b: e.tensor_copy(pq[b][0:112, :],
                                                     pw[b][0:112, :]),
                 reads=[f"pw{b}"], writes=[f"pq{b}"])
            pr.d("act", lambda e, i=i, b=b: e.dma_start(
                out=win_shard[112 * i:112 * (i + 1), :], in_=pq[b][0:112, :]),
                reads=[f"pq{b}"], writes=["win_shard"])
        for t in range(4):
            i, ch = divmod(t, 2)
            b = t % 2
            pr.d("sp", lambda e, i=i, ch=ch, b=b: e.dma_start(
                out=pw[b][:, :], in_=wout_sl[128 * i:128 * (i + 1),
                                             D * ch:D * (ch + 1)]),
                writes=[f"pw{b}"], tag="s0wo")
            pr.c("dve", lambda e, b=b: e.tensor_copy(pq[b][:, :], pw[b][:, :]),
                 reads=[f"pw{b}"], writes=[f"pq{b}"])
            pr.d("act", lambda e, i=i, ch=ch, b=b: e.dma_start(
                out=wo_shard[128 * i:128 * (i + 1), D * ch:D * (ch + 1)],
                in_=pq[b][:, :]),
                reads=[f"pq{b}"], writes=["wo_shard"])
        gsrc = []
        for i in range(8):
            gsrc.append((0, gate_sl[128 * i:128 * (i + 1), :]))
        for i in range(8):
            gsrc.append((1, val_sl[128 * i:128 * (i + 1), :]))
        for i in range(2):
            for ch in range(4):
                gsrc.append((2, outw_sl[128 * i:128 * (i + 1),
                                        D * ch:D * (ch + 1)]))
        for n, (col, srcap) in enumerate(gsrc):
            b = n % 2
            pr.d("sp", lambda e, b=b, srcap=srcap:
                 e.dma_start(out=pw[b][:, :], in_=srcap),
                 writes=[f"pw{b}"], tag="s0gam")
            pr.c("dve", lambda e, b=b: e.tensor_reduce(
                misc[:, 40:41], pw[b][:, :], AX.X, ALU.add,
                apply_absolute_value=True),
                reads=[f"pw{b}"], writes=["gred"], hz=True)
            pr.c("dve", lambda e, col=col: e.tensor_tensor(
                pacc[:, col:col + 1], pacc[:, col:col + 1],
                misc[:, 40:41], ALU.add),
                reads=["gred", "pacc"], writes=["pacc"])
        pr.c("pe", lambda e: e.matmul(psB[0:1, 0:4], ones_col[:, :],
                                      pacc[:, 0:4], start=True, stop=True),
             reads=["pacc", "onc"], writes=["psB0"])
        pr.c("dve", lambda e: e.tensor_copy(misc[0:1, 42:46], psB[0:1, 0:4]),
             reads=["psB0"], writes=["gpart"])
        pr.d("sp", lambda e: e.dma_start(out=ag1_in[:, 0:4],
                                         in_=misc[0:1, 42:46]),
             reads=["gpart"], writes=["ag1_in"])
        pr.cc(lambda e: e.collective_compute(
            "AllGather", ALU.bypass, replica_groups=[list(range(NCORES))],
            ins=[ag1_in], outs=[ag1_out]),
            reads=["ag1_in"], writes=["ag1_out"], tag="AG1")
        pr.d("sp", lambda e: e.dma_start(out=misc[0:8, 48:52],
                                         in_=ag1_out[:, 0:4]),
             reads=["ag1_out"], writes=["ag1_sb"])
        pr.c("pe", lambda e: e.matmul(psB[0:1, 512:516], ones_col[0:8, :],
                                      misc[0:8, 48:52], start=True, stop=True),
             reads=["ag1_sb", "onc"], writes=["psB1"])
        for j, cntj in enumerate((HID * D, HID * D, D * HID)):
            pr.c("dve", lambda e, j=j, cntj=cntj: ts(
                e, misc[0:1, 52 + j:53 + j], psB[0:1, 512 + j:513 + j],
                1.0 / cntj, 1e-5, ALU.mult, ALU.max),
                reads=["psB1"], writes=["gfin"])
        pr.c("pe", lambda e: e.matmul(psB[:, 1024:1027], ones_row[:, :],
                                      misc[0:1, 52:55], start=True, stop=True),
             reads=["gfin", "onr"], writes=["psB2"])
        pr.c("dve", lambda e: e.tensor_copy(gcols[:, 0:3], psB[:, 1024:1027]),
             reads=["psB2"], writes=["gcols"])
        pr.c("dve", lambda e: ts(e, gcols[:, 3:6], gcols[:, 0:3], 0.5),
             reads=["gcols"], writes=["gcols"])

        def quant(srcap, dstap, shkey, col, n):
            b = n % 2
            pr.d("sp", lambda e, b=b, srcap=srcap:
                 e.dma_start(out=pw[b][:, :], in_=srcap),
                 writes=[f"pw{b}"], tag="s0q")
            pr.c("dve", lambda e, b=b, col=col:
                 e.tensor_scalar(ptmp[:, :], pw[b][:, :],
                                 gcols[:, 3 + col:4 + col], None, ALU.is_ge),
                 reads=[f"pw{b}", "gcols"], writes=["ptmp"])
            pr.c("dve", lambda e, b=b, col=col:
                 e.tensor_scalar(ptmp2[:, :], pw[b][:, :], -1.0,
                                 gcols[:, 3 + col:4 + col], ALU.mult,
                                 ALU.is_ge),
                 reads=[f"pw{b}", "gcols"], writes=["ptmp2"])
            pr.c("dve", lambda e, b=b:
                 e.tensor_tensor(pq[b][:, :], ptmp[:, :], ptmp2[:, :],
                                 ALU.subtract),
                 reads=["ptmp", "ptmp2"], writes=[f"pq{b}"])
            pr.d("act", lambda e, b=b, dstap=dstap: e.dma_start(
                out=dstap, in_=pq[b][:, :]),
                reads=[f"pq{b}"], writes=[shkey])
        n = 0
        for i in range(8):
            quant(gate_sl[128 * i:128 * (i + 1), :],
                  mg_shard[128 * i:128 * (i + 1), :], "mg_shard", 0, n)
            n += 1
        for i in range(8):
            quant(val_sl[128 * i:128 * (i + 1), :],
                  mv_shard[128 * i:128 * (i + 1), :], "mv_shard", 1, n)
            n += 1
        for i in range(2):
            for ch in range(4):
                quant(outw_sl[128 * i:128 * (i + 1), D * ch:D * (ch + 1)],
                      mo_shard[128 * i:128 * (i + 1), D * ch:D * (ch + 1)],
                      "mo_shard", 2, n)
                n += 1

    if "s0" in phases:
        s0()

    S1KEYS_ = ["xg0", "xg1", "xng0", "xng1", "xr_bf", "kx_bf", "q1", "k0",
               "dp", "kq_st"]

    def s1():
        pr.c("pool", lambda e: e.memset(misc[0:1, 61:62], 0.0),
             writes=["pw0", "pw1", "pq0", "pq1", "pacc", "ptmp",
                     "ptmp2"] + S1KEYS_, hz=True)
        for g in range(NTG + 1):
            b = g % 2
            rows = HALO if g == 0 else 128
            r0 = 0 if g == 0 else HALO + 128 * (g - 1)
            xgk, xngk = f"xg{b}", f"xng{b}"
            prep_keys = (["pw0", "pw1", "pq0", "pq1", "ptmp", "ptmp2", "pacc"]
                         if g <= 1 and "s0" in phases else [])
            pr.d("sp", lambda e, b=b, rows=rows, r0=r0: e.dma_start(
                out=xg[b][0:rows, :], in_=x_sl[r0:r0 + rows, :]),
                reads=prep_keys, writes=[xgk], tag="s1x")
            for q in range(4):
                pr.c("dve", lambda e, b=b, rows=rows, q=q: e.bn_stats(
                    misc[0:rows, 12 + 6 * q:18 + 6 * q],
                    xg[b][0:rows, 512 * q:512 * (q + 1)]),
                    reads=[xgk], writes=["bst"], hz=True)
            pr.c("dve", lambda e, rows=rows: e.bn_aggr(
                misc[0:rows, 36:38], misc[0:rows, 12:36]),
                reads=["bst"], writes=["mv"], hz=True)
            pr.c("act", lambda e, rows=rows: e.activation(
                misc[0:rows, 0:1], misc[0:rows, 37:38], AF.Ln,
                bias=cst[0:rows, 0:1], scale=1.0),
                reads=["mv", "cst"], writes=["lnv"])
            pr.c("act", lambda e, rows=rows: e.activation(
                misc[0:rows, 1:2], misc[0:rows, 0:1], AF.Exp,
                bias=cst[0:rows, 2:3], scale=-0.5),
                reads=["lnv", "cst"], writes=["rs"])
            pr.c("dve", lambda e, b=b, rows=rows: e.tensor_scalar(
                xng[b][0:rows, :], xg[b][0:rows, :], misc[0:rows, 36:37],
                misc[0:rows, 1:2], ALU.subtract, ALU.mult),
                reads=[xgk, "mv", "rs"], writes=[xngk])
            if g == 1 and "st2" in taps:
                t3 = tapout("st2", (128, 16))
                pr.c("dve", lambda e, t3=t3: e.tensor_copy(
                    smallrow[0:1, 0:16] if False else gg_tok[:, 0:16],
                    misc[:, 0:16]), reads=["rr", "mv", "rs", "lnm", "ms"],
                    writes=["st2cap"])
                pr.c("dve", lambda e: e.tensor_copy(
                    so_tok[:, 0:2], misc[:, 36:38]),
                    reads=["mv"], writes=["st2cap2"])
                pr.d("sp", lambda e, t3=t3: e.dma_start(
                    out=t3[:, 0:14], in_=gg_tok[:, 0:14]),
                    reads=["st2cap"], writes=["tap_st2"])
                pr.d("sp", lambda e, t3=t3: e.dma_start(
                    out=t3[:, 14:16], in_=so_tok[:, 0:2]),
                    reads=["st2cap2"], writes=["tap_st2"])
            if g == 1 and "xn1" in taps:
                t = tapout("xn1", (128, D))
                pr.d("sp", lambda e, t=t, b=b: e.dma_start(
                    out=t[:, :], in_=xng[b][:, :]),
                    reads=[xngk], writes=["tap_xn1"])
                t2 = tapout("st1", (128, 64))
                pr.d("sp", lambda e, t2=t2: e.dma_start(
                    out=t2[:, :], in_=misc[:, :]),
                    reads=["rr", "mv", "rs"], writes=["tap_st1"])
            if g > 0:
                pr.c("pool", lambda e, b=b: e.tensor_tensor(
                    xg[b][:, :], xg[b][:, :], xng[b][:, :], ALU.add),
                    reads=[xgk, xngk], writes=[xgk])
                pr.d("act", lambda e, b=b, g=g: e.dma_start(
                    out=out_sl[128 * (g - 1):128 * g, :], in_=xg[b][:, :]),
                    reads=[xgk],
                    writes=[f"outg{g - 1}q{qq}" for qq in range(4)],
                    tag="s1out")
            pr.c("dve", lambda e, rows=rows: ts(
                e, misc[0:rows, 3:4], misc[0:rows, 37:38], 1.0, EPS_LN,
                ALU.mult, ALU.add), reads=["mv"], writes=["vpe"])
            pr.c("dve", lambda e, rows=rows: e.reciprocal(
                misc[0:rows, 4:5], misc[0:rows, 3:4]),
                reads=["vpe"], writes=["r1"], hz=True)
            pr.c("dve", lambda e, rows=rows: e.tensor_tensor(
                misc[0:rows, 5:6], misc[0:rows, 37:38], misc[0:rows, 4:5],
                ALU.mult), reads=["mv", "r1"], writes=["ms"])
            pr.c("act", lambda e, rows=rows: e.activation(
                misc[0:rows, 6:7], misc[0:rows, 5:6], AF.Ln,
                bias=cst[0:rows, 1:2], scale=1.0),
                reads=["ms", "cst"], writes=["lnm"])
            pr.c("act", lambda e, rows=rows: e.activation(
                misc[0:rows, 7:8], misc[0:rows, 6:7], AF.Exp,
                bias=cst[0:rows, 2:3], scale=-0.5),
                reads=["lnm", "cst"], writes=["rr"])
            pr.c("dve", lambda e, b=b, rows=rows: e.scalar_tensor_tensor(
                xr_bf[0:rows, :], xng[b][0:rows, :], misc[0:rows, 7:8],
                wn_rep[0:rows, :], ALU.mult, ALU.mult),
                reads=[xngk, "rr", "wn_rep"], writes=["xr_bf"])
            if g == 0:
                pr.c("dve", lambda e: ts(
                    e, xr_bf[0:HALO, :], xr_bf[0:HALO, :], cst[0:HALO, 3:4],
                    None, ALU.mult),
                    reads=["xr_bf", "cst"], writes=["xr_bf"])
            for half in range(2):
                for c8 in range(8):
                    c = half * 8 + c8
                    pr.c("pe", lambda e, c=c, c8=c8: e.transpose(
                        psT[:, 128 * c8:128 * (c8 + 1)],
                        xr_bf[:, 128 * c:128 * (c + 1)], ident_bf[:, :]),
                        reads=["xr_bf", "idb"], writes=["psT"], tag="s1tr")

                def ev(e, g=g, half=half):
                    c0 = half * 8
                    src = psT[:, :].rearrange("p (c t) -> p c t", c=8)
                    if g == 0:
                        return e.tensor_copy(xrT[:, c0:c0 + 8, 0:HALO],
                                             src[:, :, 0:HALO])
                    return e.tensor_copy(
                        xrT[:, c0:c0 + 8, HALO + 128 * (g - 1):
                            HALO + 128 * g], src)
                pr.c("dve", ev, reads=["psT"], writes=["xrT"])
            if g == 0:
                continue
            pr.c("dve", lambda e, b=b: e.reduce_max(
                misc[:, 8:9], xng[b][:, :], AX.X, apply_absolute_value=True),
                reads=[xngk], writes=["gx"], hz=True)
            pr.c("dve", lambda e: ts(e, misc[:, 9:10], misc[:, 8:9], 1e-5,
                                     None, ALU.max),
                 reads=["gx"], writes=["gxc"])
            pr.c("dve", lambda e: e.reciprocal(misc[:, 10:11], misc[:, 9:10]),
                 reads=["gxc"], writes=["rgx"], hz=True)
            pr.c("dve", lambda e: ts(e, sx_col[:, 0:1], misc[:, 10:11], 127.0),
                 reads=["rgx"], writes=["sxc"])
            pr.c("dve", lambda e, g=g: ts(
                e, sxi_tok[:, g - 1:g], misc[:, 9:10], 1.0 / 127.0),
                reads=["gxc"], writes=["sxi_tok"])
            pr.c("dve", lambda e, b=b: ts(e, q1_f[:, :], xng[b][:, :],
                                          sx_col[:, 0:1]),
                 reads=[xngk, "sxc"], writes=["q1"])
            pr.c("dve", lambda e: ts(e, k0_i[:, :], q1_f[:, :], 256.0, None,
                                     ALU.add),
                 reads=["q1"], writes=["k0"])
            pr.c("dve", lambda e: e.scalar_tensor_tensor(
                dp_f[:, :], k0_i[:, :], 256.0, q1_f[:, :], ALU.subtract,
                ALU.subtract), reads=["k0", "q1"], writes=["dp"])
            pr.c("dve", lambda e: e.scalar_tensor_tensor(
                kf_f[:, :], dp_f[:, :], -0.5, k0_i[:, :], ALU.is_le, ALU.add),
                reads=["dp", "k0", "q1"], writes=["q1"])
            pr.c("dve", lambda e: ts(e, kx_bf[:, :], kf_f[:, :], 256.0, None,
                                     ALU.subtract),
                 reads=["q1"], writes=["kx_bf"])
            if g == 1 and "q11" in taps:
                for nm_, buf_, key_ in (("q11", q1_f, "q1"), ("k01", k0_i, "k0"),
                                        ("dp1", dp_f, "dp"), ("kx1", kx_bf, "kx_bf"),
                                        ("xr1", xr_bf, "xr_bf")):
                    t9 = tapout(nm_, (128, D), F32 if nm_ != "kx1" and nm_ != "xr1" else BF16)
                    if nm_ == "k01":
                        t9 = tap_outs[nm_] = dout("tap_" + nm_ + "b", (128, D), I32)
                    pr.d("sp", lambda e, t9=t9, buf_=buf_: e.dma_start(
                        out=t9[:, :], in_=buf_[:, :]),
                        reads=[key_], writes=["tap_" + nm_])
            for half in range(2):
                for c8 in range(8):
                    c = half * 8 + c8
                    pr.c("pe", lambda e, c=c, c8=c8: e.transpose(
                        psT[:, 128 * c8:128 * (c8 + 1)],
                        kx_bf[:, 128 * c:128 * (c + 1)], ident_bf[:, :]),
                        reads=["kx_bf", "idb"], writes=["psT"], tag="s1trq")

                def evq(e, half=half):
                    c0 = half * 8
                    return e.tensor_copy(
                        kq_st[:, c0:c0 + 8, :],
                        psT[:, :].rearrange("p (c t) -> p c t", c=8))
                pr.c("dve", evq, reads=["psT"], writes=["kq_st"])
            pr.d("act", lambda e, g=g: e.dma_start(
                out=xq_scr.transpose([1, 0, 2])[:, :, 128 * (g - 1):128 * g],
                in_=kq_st[:, :, :]),
                reads=["kq_st"], writes=["xq_scr"], tag="s1xq")

    if "s1" in phases:
        s1()

    def s0_ags():
        prev = ["xrT", "kq_st"]
        for nm, i_, o_, ik, ok in (
                ("AGwin", win_shard, win_q, "win_shard", "win_q"),
                ("AGmg", mg_shard, mg_q, "mg_shard", "mg_q"),
                ("AGmv", mv_shard, mv_q, "mv_shard", "mv_q"),
                ("AGmo", mo_shard, mo_q, "mo_shard", "mo_q"),
                ("AGwo", wo_shard, wo_q, "wo_shard", "wo_q")):
            pr.cc(lambda e, i_=i_, o_=o_: e.collective_compute(
                "AllGather", ALU.bypass, replica_groups=[list(range(NCORES))],
                ins=[i_], outs=[o_]),
                reads=[ik] + list(prev), writes=[ok], tag=nm)
            prev = [ok]

    if "s0" in phases:
        s0_ags()

    NQ = {0: "psA0", 1: "psA1", 2: "psA2", 3: "psA3"}
    NB = {0: "psB0", 1: "psB1", 2: "psB2"}

    def wslab_load(b, srcdram, c0, ncols, wkey):
        for dc in range(NDC):
            q = "sp" if dc % 2 == 0 else "act"
            pr.d(q, lambda e, b=b, dc=dc, c0=c0, ncols=ncols, srcdram=srcdram:
                 e.dma_start_transpose(
                     wslab[b][:, dc, 0:ncols],
                     srcdram[c0:c0 + ncols, 128 * dc:128 * (dc + 1)]),
                 writes=[wkey, "tch"], reads=["s1fence"], tag="wslab")

    S1KEYS = ["xg0", "xg1", "xng0", "xng1", "xr_bf", "kx_bf", "q1", "k0",
              "dp", "kq_st"]

    def super_mms(c0, nch, wkey, b, toks_xi):
        for ci in range(nch):
            c = c0 + ci
            for tb in range(4):
                lo = (0 if toks_xi else 8) + 512 * tb
                for dc in range(NDC):
                    pr.c("pe", lambda e, b=b, ci=ci, dc=dc, tb=tb, lo=lo:
                         e.matmul(psA[:, 512 * tb:512 * (tb + 1)],
                                  wslab[b][:, dc, 128 * ci:128 * (ci + 1)],
                                  xrT[:, dc, lo:lo + 512],
                                  start=(dc == 0), stop=(dc == NDC - 1)),
                         reads=[wkey, "xrT"], writes=[NQ[tb]],
                         tag=f"mm_c{c}")
            if toks_xi:
                for dc in range(NDC):
                    pr.c("pe", lambda e, b=b, ci=ci, dc=dc:
                         e.matmul(psB[:, 0:8],
                                  wslab[b][:, dc, 128 * ci:128 * (ci + 1)],
                                  xrT[:, dc, 2048:2056],
                                  start=(dc == 0), stop=(dc == NDC - 1)),
                         reads=[wkey, "xrT"], writes=["psB0"],
                         tag=f"mmtail_c{c}")
            yield c

    def s2():
        pr.c("pool", lambda e: e.memset(misc[0:1, 63:64], 0.0),
             reads=[], writes=["s1fence"] + S1KEYS, hz=True)
        pr.d("sp", lambda e: e.dma_start(
            out=wc_sb[:, :].rearrange("p (c j) -> p c j", j=4),
            in_=w_conv.rearrange("(c p) j -> p c j", p=128)),
            writes=["wc"], tag="wc")
        pr.d("sp", lambda e: e.dma_start(out=dtb_col[:, :], in_=dt_bias[:, :]),
             writes=["dtb"], tag="dtb")
        pr.d("sp", lambda e: e.dma_start(out=A_col[:, :], in_=a_log[:, :]),
             writes=["A_log_sb"], tag="alog")
        pr.c("act", lambda e: e.activation(A_col[:, :], A_col[:, :], AF.Exp,
                                           bias=cst[0:16, 2:3], scale=1.0),
             reads=["A_log_sb", "cst"], writes=["A_log_sb"], hz=True)
        pr.c("dve", lambda e: ts(e, A_col[:, :], A_col[:, :], -1.0),
             reads=["A_log_sb"], writes=["A_col"], hz=True)
        pr.d("sp", lambda e: e.dma_start(out=cmask[:, :], in_=cmask_in[:, :]),
             writes=["cmask"], tag="cmaskl")

        for sup in range(16):
            b = sup % 2
            wkey = f"wslab{b}"
            wslab_load(b, win_q, (32 + 2 * sup) * 128, 256, wkey)
            for c in super_mms(32 + 2 * sup, 2, wkey, b, True):
                cc = c - 32
                head = cc // 2
                for tb in range(4):
                    pr.c("dve", lambda e, tb=tb: e.tensor_copy(
                        pxi_bf[:, 512 * tb:512 * (tb + 1)],
                        psA[:, 512 * tb:512 * (tb + 1)]),
                        reads=[NQ[tb]], writes=["pxi"])
                pr.c("dve", lambda e: e.tensor_copy(
                    pxi_bf[:, 2048:2056], psB[:, 0:8]),
                    reads=["psB0"], writes=["pxi"])
                if cc == 0 and "pxi0" in taps:
                    t = tapout("pxi0", (128, TH), BF16)
                    pr.d("sp", lambda e, t=t: e.dma_start(
                        out=t[:, :], in_=pxi_bf[:, :]),
                        reads=["pxi"], writes=["tap_pxi0"])
                for j in range(4):
                    pr.c("dve", lambda e, j=j, cc=cc: ts(
                        e, dgs[j][:, :], ident_f[:, :],
                        wc_sb[:, 4 * cc + j:4 * cc + j + 1]),
                        reads=["idf", "wc"], writes=[f"dg{j}"], hz=True)
                ctgt = [(psB, 0, "psB0"), (psB, 512, "psB1"),
                        (psB, 1024, "psB2"), (psA, 0, "psA0")]
                for tb in range(4):
                    tgt, off, key = ctgt[tb]
                    for j in range(4):
                        pr.c("pe", lambda e, tb=tb, j=j, tgt=tgt, off=off:
                             e.matmul(tgt[:, off:off + 512], dgs[j][:, :],
                                      pxi_bf[:, 512 * tb + 5 + j:
                                             512 * tb + 5 + j + 512],
                                      start=(j == 0), stop=(j == 3)),
                             reads=[f"dg{j}", "pxi"], writes=[key],
                             tag=f"conv{cc}")
                    pr.c("act", lambda e, tb=tb, tgt=tgt, off=off, key=key:
                         e.activation(xi_bf[:, 512 * tb:512 * (tb + 1)],
                                      tgt[:, off:off + 512], AF.Silu,
                                      bias=cst[:, 2:3], scale=1.0),
                         reads=[key, "cst"], writes=["xi_bf"], hz=True)
                if cc == 0 and "xi0" in taps:
                    t = tapout("xi0", (128, TCORE), BF16)
                    pr.d("sp", lambda e, t=t: e.dma_start(
                        out=t[:, :], in_=xi_bf[:, :]),
                        reads=["xi_bf"], writes=["tap_xi0"])
                if cc % 2 == 0:
                    pr.c("pool", lambda e: e.memset(xbar_row[0:1, :], 0.0),
                         writes=["xbar_row"], hz=True)
                for tb in range(4):
                    pr.c("pe", lambda e, tb=tb: e.matmul(
                        psB[0:1, 0:512], ones_col_bf[:, :],
                        xi_bf[:, 512 * tb:512 * (tb + 1)],
                        start=True, stop=True),
                        reads=["onc", "xi_bf"], writes=["psB0"],
                        tag="xbarmm")
                    pr.c("dve", lambda e, tb=tb: e.tensor_tensor(
                        xbar_row[0:1, 512 * tb:512 * (tb + 1)],
                        xbar_row[0:1, 512 * tb:512 * (tb + 1)],
                        psB[0:1, 0:512], ALU.add),
                        reads=["psB0", "xbar_row"], writes=["xbar_row"])
                if cc % 2 == 1:
                    pr.d("act", lambda e: e.dma_start(
                        out=row_b[0:1, 0:TCORE], in_=xbar_row[0:1, :]),
                        reads=["xbar_row", "row_b"], writes=["row_b"],
                        tag="xbb1")
                    pr.d("sp", lambda e, head=head: e.dma_start(
                        out=xbar[head:head + 1, :], in_=row_b[0:1, 0:TCORE]),
                        reads=["row_b"], writes=["xbar"], tag="xbb2")

    def s2dt():
        for sup, (c0, nch) in enumerate(((65, 2), (67, 2), (64, 1))):
            b = sup % 2
            wkey = f"wslab{b}"
            wslab_load(b, win_q, c0 * 128, 128 * nch, wkey)
            for c in super_mms(c0, nch, wkey, b, False):
                if c == 64:
                    for tb in range(4):
                        pr.c("dve", lambda e, tb=tb: e.tensor_copy(
                            dt_sp[0:16, 512 * tb:512 * (tb + 1)],
                            psA[0:16, 512 * tb:512 * (tb + 1)]),
                            reads=[NQ[tb]],
                            writes=["dt_sp", "xrT"])
                    if "dtraw" in taps:
                        t = tapout("dtraw", (16, TCORE))
                        pr.d("sp", lambda e, t=t: e.dma_start(
                            out=t[:, :], in_=dt_sp[0:16, :]),
                            reads=["dt_sp"], writes=["tap_dtraw"])
                else:
                    dst, key = ((Bsb[c - 65], f"Bsb{c - 65}") if c < 67 else
                                (Csb[c - 67], f"Csb{c - 67}"))
                    for tb in range(4):
                        pr.c("dve", lambda e, tb=tb, dst=dst, key=key:
                             e.tensor_copy(
                                 dst[:, 512 * tb:512 * (tb + 1)],
                                 psA[:, 512 * tb:512 * (tb + 1)]),
                             reads=[NQ[tb]], writes=[key])

    def s3a():
        pr.c("dve", lambda e: ts(e, dt_sp[0:16, :], dt_sp[0:16, :],
                                 dtb_col[:, 0:1], None, ALU.add),
             reads=["dt_sp", "dtb"], writes=["dt_sp"], hz=True)
        pr.c("act", lambda e: e.activation(htmp[0:16, :], dt_sp[0:16, :],
                                           AF.Abs, bias=cst[0:16, 2:3],
                                           scale=1.0),
             reads=["dt_sp", "cst"], writes=["htmp", "xrT"], hz=True)
        pr.c("act", lambda e: e.activation(htmp[0:16, :], htmp[0:16, :],
                                           AF.Exp, bias=cst[0:16, 2:3],
                                           scale=-1.0),
             reads=["htmp", "cst"], writes=["htmp"], hz=True)
        pr.c("act", lambda e: e.activation(htmp[0:16, :], htmp[0:16, :],
                                           AF.Ln, bias=cst[0:16, 4:5],
                                           scale=1.0),
             reads=["htmp", "cst"], writes=["htmp"], hz=True)
        pr.c("dve", lambda e: ts(e, dt_sp[0:16, :], dt_sp[0:16, :], 0.0,
                                 None, ALU.max),
             reads=["dt_sp"], writes=["dt_sp"], hz=True)
        pr.c("dve", lambda e: e.tensor_tensor(dt_sp[0:16, :], dt_sp[0:16, :],
                                              htmp[0:16, :], ALU.add),
             reads=["dt_sp", "htmp"], writes=["dt_sp"], hz=True)
        pr.c("dve", lambda e: ts(e, htmp[0:16, :], dt_sp[0:16, :],
                                 A_col[:, 0:1]),
             reads=["dt_sp", "A_col"], writes=["htmp"], hz=True)
        pr.c("dve", lambda e: e.tensor_tensor_scan(
            cum_t[0:16, :], htmp[0:16, :], htmp[0:16, :], 0.0,
            ALU.add, ALU.bypass),
            reads=["htmp"], writes=["cum", "smallrow"], hz=True)
        for bt, sel, selk in ((0, sel_loT, "slT"), (1, sel_hiT, "shT")):
            for tb in range(4):
                pr.c("pe", lambda e, tb=tb, sel=sel, selk=selk: e.matmul(
                    psA[:, 512 * tb:512 * (tb + 1)], sel[:, :],
                    htmp[0:16, 512 * tb:512 * (tb + 1)],
                    start=True, stop=True),
                    reads=[selk, "htmp"], writes=[NQ[tb]], tag="bca")
            pr.c("act", lambda e, bt=bt: e.activation(
                dab[bt][:, :], psA[:, :], AF.Exp, bias=cst[:, 2:3], scale=1.0),
                reads=["psA0", "psA1", "psA2", "psA3", "cst"],
                writes=[f"dab{bt}", "xrT"], hz=True)
        pr.c("dve", lambda e: e.scalar_tensor_tensor(
            htmp[0:16, :], xbar[0:16, :], 1.0 / DH, dt_sp[0:16, :],
            ALU.mult, ALU.mult),
            reads=["xbar", "dt_sp", "cum"], writes=["htmp"], hz=True)
        for bt, sel, selk in ((0, sel_loT, "slT"), (1, sel_hiT, "shT")):
            for tb in range(4):
                pr.c("pe", lambda e, tb=tb, sel=sel, selk=selk: e.matmul(
                    psA[:, 512 * tb:512 * (tb + 1)], sel[:, :],
                    htmp[0:16, 512 * tb:512 * (tb + 1)],
                    start=True, stop=True),
                    reads=[selk, "htmp"], writes=[NQ[tb]], tag="bcw")
            pr.c("dve", lambda e, bt=bt: e.tensor_tensor(
                ub[bt][:, :], Bsb[bt][:, :], psA[:, :], ALU.mult),
                reads=["psA0", "psA1", "psA2", "psA3", f"Bsb{bt}"],
                writes=[f"ub{bt}", "xrT"], hz=True)
        if "us3" in taps:
            t = tapout("us3", (128, TCORE))
            pr.d("sp", lambda e, t=t: e.dma_start(out=t[:, :],
                                                  in_=ub[0][:, :]),
                 reads=["ub0"], writes=["tap_us3"])
            t2 = tapout("das3", (128, TCORE))
            pr.d("sp", lambda e, t2=t2: e.dma_start(out=t2[:, :],
                                                    in_=dab[0][:, :]),
                 reads=["dab0"], writes=["tap_das3"])
        for bt in range(2):
            pr.c("dve", lambda e, bt=bt: e.tensor_tensor_scan(
                hb[bt][:, :], dab[bt][:, :], ub[bt][:, :], 0.0,
                ALU.mult, ALU.add),
                reads=[f"dab{bt}", f"ub{bt}"],
                writes=[f"hb{bt}", "xrT"], hz=True)
            pr.c("dve", lambda e, bt=bt: e.tensor_copy(
                S_colA[:, bt:bt + 1], hb[bt][:, TCORE - 1:TCORE]),
                reads=[f"hb{bt}"], writes=["S_col"], hz=True)
            pr.c("dve", lambda e, bt=bt: e.tensor_tensor(
                ub[bt][:, :], hb[bt][:, :], Csb[bt][:, :], ALU.mult),
                reads=[f"hb{bt}", f"Csb{bt}"], writes=[f"ub{bt}"], hz=True)

    def s3b():
        for tb in range(4):
            pr.c("pe", lambda e, tb=tb: e.matmul(
                psA[0:16, 512 * tb:512 * (tb + 1)], selT_lo[:, :],
                ub[0][:, 512 * tb:512 * (tb + 1)], start=True, stop=False),
                reads=["stlo", "ub0"], writes=[NQ[tb]], tag="ylo")
            pr.c("pe", lambda e, tb=tb: e.matmul(
                psA[0:16, 512 * tb:512 * (tb + 1)], selT_hi[:, :],
                ub[1][:, 512 * tb:512 * (tb + 1)], start=False, stop=True),
                reads=["sthi", "ub1"], writes=[NQ[tb]], tag="yhi")
        pr.c("dve", lambda e: e.tensor_copy(y_loc[0:16, :], psA[0:16, :]),
             reads=["psA0", "psA1", "psA2", "psA3"], writes=["y_loc"],
             hz=True)
        pr.c("act", lambda e: e.activation(htmp[0:16, :], cum_t[0:16, :],
                                           AF.Exp, bias=cst[0:16, 2:3],
                                           scale=1.0),
             reads=["cum", "cst"], writes=["htmp"], hz=True)
        for bt, sel, selk, Pt, Pk in ((0, sel_loT, "slT", P0, "P0"),
                                      (1, sel_hiT, "shT", P1, "P1")):
            for tb in range(4):
                pr.c("pe", lambda e, tb=tb, sel=sel, selk=selk: e.matmul(
                    psA[:, 512 * tb:512 * (tb + 1)], sel[:, :],
                    htmp[0:16, 512 * tb:512 * (tb + 1)],
                    start=True, stop=True),
                    reads=[selk, "htmp"], writes=[NQ[tb]], tag="bcE")
            pr.c("dve", lambda e, bt=bt, Pt=Pt, Pk=Pk: e.tensor_tensor(
                Pt[:, :], Csb[bt][:, :], psA[:, :], ALU.mult),
                reads=["psA0", "psA1", "psA2", "psA3", f"Csb{bt}"],
                writes=[Pk], hz=True)
            pr.c("dve", lambda e, bt=bt: e.tensor_copy(
                D_colA[:, bt:bt + 1], psA[:, TCORE - 1:TCORE]),
                reads=["psA3"], writes=["D_col"], hz=True)
        pr.d("sp", lambda e: e.dma_start(out=ag3_in[0:1, 0:256],
                                         in_=S_colA[:, :]),
             reads=["S_col"], writes=["ag3_in"], tag="ag3s")
        pr.d("sp", lambda e: e.dma_start(out=ag3_in[0:1, 256:512],
                                         in_=D_colA[:, :]),
             reads=["D_col"], writes=["ag3_in"], tag="ag3d")
        pr.cc(lambda e: e.collective_compute(
            "AllGather", ALU.bypass, replica_groups=[list(range(NCORES))],
            ins=[ag3_in], outs=[ag3_out]),
            reads=["ag3_in", "wo_q"], writes=["ag3_out"], tag="AG3")
        pr.d("sp", lambda e: e.dma_start(
            out=agS[:, 0:16].rearrange("p (r b) -> p r b", b=2),
            in_=ag3_out[:, 0:256].rearrange("r (p b) -> p r b", p=128)),
            reads=["ag3_out"], writes=["agS"], tag="ag3u1")
        pr.d("sp", lambda e: e.dma_start(
            out=agS[:, 16:32].rearrange("p (r b) -> p r b", b=2),
            in_=ag3_out[:, 256:512].rearrange("r (p b) -> p r b", p=128)),
            reads=["ag3_out"], writes=["agS"], tag="ag3u2")
        pr.c("dve", lambda e: e.memset(hinA[:, :], 0.0), writes=["hin"],
             hz=True)
        for r in range(NCORES):
            for bt in range(2):
                pr.c("dve", lambda e, r=r, bt=bt: e.scalar_tensor_tensor(
                    misc[:, 40 + bt:41 + bt], hinA[:, bt:bt + 1],
                    agS[:, 16 + 2 * r + bt:17 + 2 * r + bt],
                    agS[:, 2 * r + bt:1 + 2 * r + bt],
                    ALU.mult, ALU.add),
                    reads=["hin", "agS"], writes=["hin_t"], hz=True)
                pr.c("dve", lambda e, bt=bt: e.tensor_tensor(
                    misc[:, 42 + bt:43 + bt], misc[:, 40 + bt:41 + bt],
                    hinA[:, bt:bt + 1], ALU.subtract),
                    reads=["hin_t", "hin"], writes=["hin_d"], hz=True)
                pr.c("dve", lambda e, r=r, bt=bt: e.scalar_tensor_tensor(
                    hinA[:, bt:bt + 1], misc[:, 42 + bt:43 + bt],
                    cmask[:, r:r + 1], hinA[:, bt:bt + 1],
                    ALU.mult, ALU.add),
                    reads=["hin_d", "cmask", "hin"], writes=["hin"], hz=True)
        for bt, Pt, Pk in ((0, P0, "P0"), (1, P1, "P1")):
            pr.c("dve", lambda e, bt=bt, Pt=Pt, Pk=Pk: ts(
                e, dab[bt][:, :], Pt[:, :], hinA[:, bt:bt + 1]),
                reads=[Pk, "hin"], writes=[f"dab{bt}"], hz=True)
        for tb in range(4):
            pr.c("pe", lambda e, tb=tb: e.matmul(
                psA[0:16, 512 * tb:512 * (tb + 1)], selT_lo[:, :],
                dab[0][:, 512 * tb:512 * (tb + 1)], start=True, stop=False),
                reads=["stlo", "dab0"], writes=[NQ[tb]], tag="yc0")
            pr.c("pe", lambda e, tb=tb: e.matmul(
                psA[0:16, 512 * tb:512 * (tb + 1)], selT_hi[:, :],
                dab[1][:, 512 * tb:512 * (tb + 1)], start=False, stop=True),
                reads=["sthi", "dab1"], writes=[NQ[tb]], tag="yc1")
        pr.c("dve", lambda e: e.tensor_tensor(
            y_loc[0:16, :], y_loc[0:16, :], psA[0:16, :], ALU.add),
            reads=["psA0", "psA1", "psA2", "psA3", "y_loc"],
            writes=["y_loc"], hz=True)
        pr.c("dve", lambda e: e.tensor_copy(yful_bf[0:16, :], y_loc[0:16, :]),
             reads=["y_loc"], writes=["yful"], hz=True)

    def s2z():
        for sup in range(16):
            b = sup % 2
            wkey = f"wslab{b}"
            wslab_load(b, win_q, 2 * sup * 128, 256, wkey)
            for ci in range(2):
                c = 2 * sup + ci
                for tb in range(4):
                    lo = 8 + 512 * tb
                    for dc in range(NDC):
                        pr.c("pe", lambda e, b=b, ci=ci, dc=dc, tb=tb, lo=lo:
                             e.matmul(psA[:, 512 * tb:512 * (tb + 1)],
                                      wslab[b][:, dc, 128 * ci:128 * (ci + 1)],
                                      xrT[:, dc, lo:lo + 512],
                                      start=(dc == 0), stop=(dc == NDC - 1)),
                             reads=[wkey, "xrT"], writes=[NQ[tb]],
                             tag=f"zmm{c}")
                for half in range(2):
                    zk = f"zev{half}"
                    for q in range(2):
                        tb = 2 * half + q
                        pr.c("dve", lambda e, tb=tb, half=half, q=q:
                             e.tensor_copy(zev[half][:, 512 * q:512 * (q + 1)],
                                           psA[:, 512 * tb:512 * (tb + 1)]),
                             reads=[NQ[tb]], writes=[zk])
                    pr.d("act", lambda e, c=c, half=half, zk=zk: e.dma_start(
                        out=z_scr[c, :, 1024 * half:1024 * (half + 1)],
                        in_=zev[half][:, :]),
                        reads=[zk], writes=["z_scr"], tag="zspill")

    if "s2" in phases:
        s2()
        s2z()
        s2dt()
    if "s3" in phases:
        s3a()
        s3b()

    def s4a():
        for srcb, dstb, kk in ((gate_b, gb_sb, "gb"), (val_b, vb_sb, "vb")):
            pr.d("sp", lambda e, srcb=srcb: e.dma_start(out=bload[:, :],
                                                        in_=srcb[:, :]),
                 reads=["bload"], writes=["bload"], tag="bld")
            pr.c("pe", lambda e: e.transpose(psB[:, 0:64], bload[:, :],
                                             ident_f[0:64, 0:64]),
                 reads=["bload", "idf"], writes=["psB0"])
            pr.c("dve", lambda e, dstb=dstb, kk=kk: e.tensor_copy(
                dstb[:, :], psB[:, 0:64]),
                reads=["psB0"], writes=[kk], hz=True)
        pr.c("pe", lambda e: e.transpose(psB[0:16, 512:640], sxi_tok[:, :],
                                         ident_f[:, :]),
             reads=["sxi_tok", "idf"], writes=["psB1"])
        pr.c("dve", lambda e: e.tensor_copy(bload[0:16, 0:128],
                                            psB[0:16, 512:640]),
             reads=["psB1", "bload"], writes=["bload"], hz=True)
        pr.d("act", lambda e: e.dma_start(
            out=row_b.rearrange("a (g p) -> (a g) p", p=128)[0:NTG, :],
            in_=bload[0:16, 0:128]),
            reads=["bload", "row_b"], writes=["row_b"], tag="sxf")
        pr.d("sp", lambda e: e.dma_start(out=smallrow[0:1, 0:TCORE],
                                         in_=row_b[0:1, 0:TCORE]),
             reads=["row_b"], writes=["smallrow", "cum"], tag="sxl")
        pr.c("pool", lambda e: e.memset(misc[0:1, 62:63], 0.0),
             writes=["pxi", "xi_bf", "zev0", "zev1", "sxinv_tile"], hz=True)
        bcast_row(smallrow, sxinv_tile, "sxinv_tile", "smallrow")
        pr.c("pool", lambda e: e.memset(macc[:, :], 0.0),
             writes=["macc", "Csb0"], hz=True)
        for dc in range(NDC):
            pr.d("sp", lambda e, dc=dc: e.dma_start(
                out=xq_res[:, dc, :], in_=xq_scr[dc, :, :]),
                writes=(["xq_res", "xrT", "ub0", "ub1", "hb0", "hb1",
                         "dab0", "dab1", "htmp", "dt_sp"] if dc == 0
                        else ["xq_res"]), tag="xqload")
        for c in range(64):
            for mat, mq, gv in ((0, mg_q, "g"), (1, mv_q, "v")):
                b = (2 * c + mat) % 2
                wkey = f"wslab{b}"
                wslab_load(b, mq, 128 * c, 128, wkey)
                for tb in range(4):
                    for dc in range(NDC):
                        pr.c("pe", lambda e, b=b, dc=dc, tb=tb:
                             e.matmul(psA[:, 512 * tb:512 * (tb + 1)],
                                      wslab[b][:, dc, 0:128],
                                      xq_res[:, dc, 512 * tb:512 * (tb + 1)],
                                      start=(dc == 0), stop=(dc == NDC - 1)),
                             reads=[wkey, "xq_res"], writes=[NQ[tb]],
                             tag=f"s4mm{gv}{c}")
                bb = 0
                if mat == 0:
                    for tb in range(4):
                        pr.c("dve", lambda e, bb=bb, tb=tb:
                             e.scalar_tensor_tensor(
                                 gbuf[bb][:, 512 * tb:512 * (tb + 1)],
                                 psA[:, 512 * tb:512 * (tb + 1)],
                                 gcols[:, 0:1],
                                 sxinv_tile[:, 512 * tb:512 * (tb + 1)],
                                 ALU.mult, ALU.mult),
                             reads=[NQ[tb], "gcols", "sxinv_tile"],
                             writes=([f"gbuf{bb}", "Bsb0"] if tb == 0 else
                                     [f"gbuf{bb}"]))
                    pr.c("act", lambda e, bb=bb, c=c: e.activation(
                        gbuf[bb][:, :], gbuf[bb][:, :], AF.Silu,
                        bias=gb_sb[:, c:c + 1], scale=1.0),
                        reads=[f"gbuf{bb}", "gb"], writes=[f"gbuf{bb}"],
                        hz=True)
                else:
                    for tb in range(4):
                        pr.c("dve", lambda e, bb=bb, tb=tb:
                             e.scalar_tensor_tensor(
                                 gvbuf[bb][:, 512 * tb:512 * (tb + 1)],
                                 psA[:, 512 * tb:512 * (tb + 1)],
                                 gcols[:, 1:2],
                                 sxinv_tile[:, 512 * tb:512 * (tb + 1)],
                                 ALU.mult, ALU.mult),
                             reads=[NQ[tb], "gcols", "sxinv_tile"],
                             writes=([f"gvbuf{bb}", "Bsb1"] if tb == 0 else
                                     [f"gvbuf{bb}"]))
                    pr.c("dve", lambda e, bb=bb, c=c: e.scalar_tensor_tensor(
                        gvbuf[bb][:, :], gvbuf[bb][:, :], vb_sb[:, c:c + 1],
                        gbuf[bb][:, :], ALU.add, ALU.mult),
                        reads=[f"gvbuf{bb}", "vb", f"gbuf{bb}"],
                        writes=[f"gvbuf{bb}"], hz=True)
                    pr.c("act", lambda e, bb=bb: e.activation(
                        babs[:, :], gvbuf[bb][:, :], AF.Abs,
                        bias=cst[:, 2:3], scale=1.0),
                        reads=[f"gvbuf{bb}", "cst"],
                        writes=["babs", "Csb1"], hz=True)
                    pr.c("dve", lambda e: e.tensor_tensor(
                        macc[:, :], macc[:, :], babs[:, :], ALU.max),
                        reads=["babs", "macc"], writes=["macc"], hz=True)
                    pr.d("act", lambda e, bb=bb, c=c: e.dma_start(
                        out=gv_scr[c, :, :], in_=gvbuf[bb][:, :]),
                        reads=[f"gvbuf{bb}"], writes=["gv_scr"], tag="gvsp")
        for blk in range(NTG):
            pr.c("pe", lambda e, blk=blk: e.transpose(
                psA[:, 0:128], macc[:, 128 * blk:128 * (blk + 1)],
                ident_f[:, :]),
                reads=["macc", "idf"], writes=["psA0"], tag="ggtr")
            pr.c("dve", lambda e, blk=blk: e.reduce_max(
                gg_tok[:, blk:blk + 1], psA[:, 0:128], AX.X),
                reads=["psA0"], writes=["gg_tok"], hz=True)
        pr.c("dve", lambda e: ts(e, gg_tok[:, :], gg_tok[:, :], 1e-5, None,
                                 ALU.max),
             reads=["gg_tok"], writes=["gg_tok"], hz=True)
        pr.c("dve", lambda e: ts(e, so_tok[:, :], gg_tok[:, :],
                                 gcols[:, 2:3], 1.0 / 127.0,
                                 ALU.mult, ALU.mult),
             reads=["gg_tok", "gcols"], writes=["so_tok"], hz=True)
        pr.c("dve", lambda e: e.reciprocal(misc[:, 44:60], gg_tok[:, :]),
             reads=["gg_tok"], writes=["qsr"], hz=True)
        pr.c("dve", lambda e: ts(e, misc[:, 44:60], misc[:, 44:60], 127.0),
             reads=["qsr"], writes=["qsr"], hz=True)
        pr.c("pe", lambda e: e.transpose(psB[0:16, 512:640], misc[:, 44:60],
                                         ident_f[:, :]),
             reads=["qsr", "idf"], writes=["psB1"])
        pr.c("dve", lambda e: e.tensor_copy(bload[0:16, 0:128],
                                            psB[0:16, 512:640]),
             reads=["psB1", "bload"], writes=["bload"], hz=True)
        pr.d("act", lambda e: e.dma_start(
            out=row_b.rearrange("a (g p) -> (a g) p", p=128)[0:NTG, :],
            in_=bload[0:16, 0:128]),
            reads=["bload", "row_b"], writes=["row_b"], tag="qsf")
        pr.d("sp", lambda e: e.dma_start(out=smallrow[0:1, 0:TCORE],
                                         in_=row_b[0:1, 0:TCORE]),
             reads=["row_b", "sxinv_tile"], writes=["smallrow"], tag="qsl")

    if "s4a" in phases:
        s4a()

    def s4b():
        pr.c("pool", lambda e: e.memset(misc[0:1, 60:61], 0.0),
             writes=["gbuf0", "gvbuf0", "macc", "babs", "sxinv_tile",
                     "wslab0", "wslab1", "xq_res",
                     "outb_rep", "qs_tile", "wh2iT", "ki_b", "qt1", "qt2",
                     "zf", "kT", "yi",
                     "moslab0", "moslab1", "moslab2",
                     "woslab0", "woslab1", "woslab2",
                     "zin0", "zin1", "zin2", "gvin0", "gvin1", "gvin2"]
             + [f"oev{ro}q{qq}" for ro in range(2) for qq in range(4)],
             hz=True)
        bcast_row(smallrow, qs_tile, "qs_tile", "smallrow")
        pr.d("sp", lambda e: e.dma_start(out=smallrow[0:1, 0:D],
                                         in_=out_b[:, :]),
             reads=["smallrow", "qs_tile"], writes=["smallrow"], tag="obl")
        bcast_row(smallrow, outb_rep, "outb_rep", "smallrow")
        for icb in range(8):
            pr.d("sp", lambda e, icb=icb: e.dma_start(
                out=zf_b[0:16, 0:512].rearrange("p (c f) -> p c f", c=4),
                in_=w_h2i.rearrange("(c f) h -> h c f", f=128)[
                    :, 4 * icb:4 * icb + 4, :],
                allow_slow_non_contiguous=True),
                reads=["zf"], writes=["zf"], tag="wh2il")
            pr.c("dve", lambda e, icb=icb: e.tensor_copy(
                wh2iT[:, 512 * icb:512 * (icb + 1)], zf_b[0:16, 0:512]),
                reads=["zf"], writes=["wh2iT"], hz=True)
        for s_ in range(4):
            for hc in range(64):
                r = hc % 3
                pr.d("sp", lambda e, s_=s_, hc=hc, r=r: e.dma_start(
                    out=gvin[r][:, :],
                    in_=gv_scr[hc, :, 512 * s_:512 * (s_ + 1)]),
                    writes=([f"gvin{r}", "wslab0", "wslab1"]
                            if hc < 3 and s_ == 0 else [f"gvin{r}"]),
                    tag="gvld")
                pr.c("pool", lambda e, s_=s_, r=r: e.tensor_tensor(
                    qt1[:, :], gvin[r][:, :],
                    qs_tile[:, 512 * s_:512 * (s_ + 1)], ALU.mult),
                    reads=[f"gvin{r}", "qs_tile"], writes=["qt1"], hz=True)
                pr.c("dve", lambda e: ts(e, ki_b[:, :], qt1[:, :], 256.0,
                                         None, ALU.add),
                     reads=["qt1"], writes=["ki"], hz=True)
                pr.c("dve", lambda e: e.scalar_tensor_tensor(
                    qt2[:, :], ki_b[:, :], 256.0, qt1[:, :],
                    ALU.subtract, ALU.subtract),
                    reads=["ki", "qt1"], writes=["qt2"], hz=True)
                pr.c("dve", lambda e: e.scalar_tensor_tensor(
                    qt1[:, :], qt2[:, :], -0.5, ki_b[:, :],
                    ALU.is_le, ALU.add),
                    reads=["qt2", "ki"], writes=["qt1"], hz=True)
                pr.c("act", lambda e, hc=hc: e.activation(
                    kT_str[:, hc, :], qt1[:, :], AF.Copy, bias=-256.0,
                    scale=1.0),
                    reads=["qt1"],
                    writes=(["kT", "xq_res"] if hc == 0 and s_ == 0
                            else ["kT"]), hz=True)
            for qd in range(4):
                for hc in range(64):
                    r = hc % 3
                    pr.d("sp" if hc % 2 == 0 else "act",
                         lambda e, qd=qd, hc=hc, r=r:
                         e.dma_start_transpose(
                             moslab[r][:, 0:512],
                             mo_q[512 * qd:512 * (qd + 1),
                                  128 * hc:128 * (hc + 1)]),
                         writes=([f"moslab{r}", "tch", "wslab0", "wslab1"]
                                 if s_ == 0 and qd == 0 and hc < 3
                                 else [f"moslab{r}", "tch"]), tag="mold")
                    for tsub in range(4):
                        pr.c("pe", lambda e, hc=hc, tsub=tsub, r=r: e.matmul(
                            psA[:, 512 * tsub:512 * (tsub + 1)],
                            kT_str[:, hc, 128 * tsub:128 * (tsub + 1)],
                            moslab[r][:, 0:512],
                            start=(hc == 0), stop=(hc == 63)),
                            reads=[f"moslab{r}", "kT"], writes=[NQ[tsub]],
                            tag=f"s4bmm{qd}")
                for tsub in range(4):
                    ro = tsub % 2
                    pr.c("dve", lambda e, s_=s_, qd=qd, tsub=tsub, ro=ro:
                         e.scalar_tensor_tensor(
                             oev[ro][:, 512 * qd:512 * (qd + 1)],
                             psA[:, 512 * tsub:512 * (tsub + 1)],
                             so_tok[:, 4 * s_ + tsub:4 * s_ + tsub + 1],
                             outb_rep[:, 512 * qd:512 * (qd + 1)],
                             ALU.mult, ALU.add),
                         reads=[NQ[tsub], "so_tok", "outb_rep"],
                         writes=[f"oev{ro}q{qd}"])
                    if s_ == 0 and qd == 0 and tsub == 0 and "oevf" in taps:
                        t = tapout("oevf", (128, 512))
                        pr.d("sp", lambda e, t=t: e.dma_start(
                            out=t[:, :], in_=oev[0][:, 0:512]),
                            reads=["oev0q0"], writes=["tap_oevf"])
                    pr.d("pool", lambda e, s_=s_, qd=qd, tsub=tsub, ro=ro:
                         e.dma_start(
                             out=ffn_scr[512 * s_ + 128 * tsub:
                                         512 * s_ + 128 * (tsub + 1),
                                         512 * qd:512 * (qd + 1)],
                             in_=oev[ro][:, 512 * qd:512 * (qd + 1)]),
                         reads=[f"oev{ro}q{qd}"],
                         writes=[f"ffng{4 * s_ + tsub}q{qd}"], tag="oacc")

    if "s4b" in phases:
        s4b()

    def s5():
        for s_ in range(4):
            for ic in range(32):
                r = ic % 3
                pr.d("sp", lambda e, s_=s_, ic=ic, r=r: e.dma_start(
                    out=zin[r][:, :],
                    in_=z_scr[ic, :, 512 * s_:512 * (s_ + 1)]),
                    writes=[f"zin{r}"], tag="zld")
                pr.c("pe", lambda e, s_=s_, ic=ic: e.matmul(
                    psB[:, 512:1024], wh2iT[:, 128 * ic:128 * (ic + 1)],
                    yful_bf[0:16, 512 * s_:512 * (s_ + 1)],
                    start=True, stop=True),
                    reads=["wh2iT", "yful"], writes=["psB1"], tag="y1mm")
                pr.c("act", lambda e, r=r: e.activation(
                    zf_b[:, :], zin[r][:, :], AF.Silu, bias=cst[:, 2:3],
                    scale=1.0),
                    reads=[f"zin{r}", "cst"], writes=["zf"], hz=True)
                pr.c("dve", lambda e, ic=ic: e.tensor_tensor(
                    yi_str[:, ic, :], zf_b[:, :], psB[:, 512:1024],
                    ALU.mult),
                    reads=["zf", "psB1"],
                    writes=(["yi", "kT"] if ic == 0 else ["yi"]), hz=True)
            for qd in range(4):
                for ic in range(32):
                    r = ic % 3
                    pr.d("act" if ic % 2 == 0 else "sp",
                         lambda e, qd=qd, ic=ic, r=r:
                         e.dma_start_transpose(
                             woslab[r][:, 0:512],
                             wo_q[512 * qd:512 * (qd + 1),
                                  128 * ic:128 * (ic + 1)]),
                         writes=([f"woslab{r}", "tch", "wslab1"]
                                 if qd == 0 and ic < 3
                                 else [f"woslab{r}", "tch"]), tag="wold")
                    for tsub in range(4):
                        pr.c("pe", lambda e, ic=ic, tsub=tsub, r=r: e.matmul(
                            psA[:, 512 * tsub:512 * (tsub + 1)],
                            yi_str[:, ic, 128 * tsub:128 * (tsub + 1)],
                            woslab[r][:, 0:512],
                            start=(ic == 0), stop=(ic == 31)),
                            reads=[f"woslab{r}", "yi"], writes=[NQ[tsub]],
                            tag=f"s5mm{qd}")
                for tsub in range(4):
                    ro = tsub % 2
                    g = 4 * s_ + tsub
                    ob, obk = (qt1, "qt1") if ro == 0 else (gvin[0], "gvin0")
                    fb, fbk = (qt2, "qt2") if ro == 0 else (gvin[1], "gvin1")
                    rows = slice(512 * s_ + 128 * tsub,
                                 512 * s_ + 128 * (tsub + 1))
                    cols = slice(512 * qd, 512 * (qd + 1))
                    pr.d("sp", lambda e, ob=ob, rows=rows, cols=cols:
                         e.dma_start(out=ob[:, :], in_=out_sl[rows, cols]),
                         reads=[f"outg{g}q{qd}"], writes=[obk], tag="obld")
                    pr.d("act", lambda e, fb=fb, rows=rows, cols=cols:
                         e.dma_start(out=fb[:, :], in_=ffn_scr[rows, cols]),
                         reads=[f"ffng{g}q{qd}"], writes=[fbk], tag="fbld")
                    pr.c("dve", lambda e, qd=qd, tsub=tsub, ro=ro, ob=ob:
                         e.tensor_tensor(
                             oev[ro][:, 512 * qd:512 * (qd + 1)],
                             psA[:, 512 * tsub:512 * (tsub + 1)],
                             ob[:, :], ALU.add),
                         reads=[NQ[tsub], obk], writes=[f"oev{ro}q{qd}"])
                    pr.c("dve", lambda e, qd=qd, ro=ro, fb=fb:
                         e.tensor_tensor(
                             oev[ro][:, 512 * qd:512 * (qd + 1)],
                             oev[ro][:, 512 * qd:512 * (qd + 1)],
                             fb[:, :], ALU.add),
                         reads=[f"oev{ro}q{qd}", fbk],
                         writes=[f"oev{ro}q{qd}"], hz=True)
                    pr.d("pool", lambda e, qd=qd, ro=ro, rows=rows, cols=cols:
                         e.dma_start(out=out_sl[rows, cols],
                                     in_=oev[ro][:, 512 * qd:512 * (qd + 1)]),
                         reads=[f"oev{ro}q{qd}"],
                         writes=[f"outg{g}q{qd}"], tag="ost")

    if "s5" in phases:
        s5()

    if "xrT" in taps:
        t = tapout("xrT", (128, NDC, TH), BF16)
        pr.d("sp", lambda e, t=t: e.dma_start(out=t[:, :, :], in_=xrT[:, :, :]),
             reads=["xrT"], writes=["tap_xrT"])
    if "xq" in taps:
        t = tapout("xq", (NDC, 128, TCORE), BF16)
        pr.d("sp", lambda e, t=t: e.dma_start(out=t[:, :, :], in_=xq_scr[:, :, :]),
             reads=["xq_scr"], writes=["tap_xq"])
    if "sxinv" in taps:
        t = tapout("sxinv", (128, NTG))
        pr.d("sp", lambda e, t=t: e.dma_start(out=t[:, :], in_=sxi_tok[:, :]),
             reads=["sxi_tok"], writes=["tap_sxinv"])
    if "winq" in taps:
        t = tapout("winq", (WINP, D), BF16)
        pr.d("sp", lambda e, t=t: e.dma_start(out=t[:, :], in_=win_q[:, :]),
             reads=["win_q"], writes=["tap_winq"])
    if "mgq" in taps:
        t = tapout("mgq", (HID, D), BF16)
        pr.d("sp", lambda e, t=t: e.dma_start(out=t[:, :], in_=mg_q[:, :]),
             reads=["mg_q"], writes=["tap_mgq"])
    if "moq" in taps:
        t = tapout("moq", (D, HID), BF16)
        pr.d("sp", lambda e, t=t: e.dma_start(out=t[:, :], in_=mo_q[:, :]),
             reads=["mo_q"], writes=["tap_moq"])
    if "ffnscr" in taps:
        t = tapout("ffnscr", (TCORE, D))
        pr.d("sp", lambda e, t=t: e.dma_start(out=t[:, :], in_=ffn_scr[:, :]),
             reads=[f"ffng{gg}q{qq}" for gg in range(NTG)
                    for qq in range(4)],
             writes=["tap_ffnscr"])
    if "yl" in taps:
        t = tapout("yl", (16, TCORE))
        pr.d("sp", lambda e, t=t: e.dma_start(out=t[:, :], in_=y_loc[0:16, :]),
             reads=["y_loc"], writes=["tap_yl"])
        t2 = tapout("cum", (16, TCORE))
        pr.d("sp", lambda e, t2=t2: e.dma_start(out=t2[:, :],
                                                in_=cum_t[0:16, :]),
             reads=["cum"], writes=["tap_cum"])
        t3 = tapout("dtsp", (16, TCORE))
        pr.d("sp", lambda e, t3=t3: e.dma_start(out=t3[:, :],
                                                in_=dt_sp[0:16, :]),
             reads=["dt_sp"], writes=["tap_dtsp"])
        t4 = tapout("xbar", (16, TCORE))
        pr.d("sp", lambda e, t4=t4: e.dma_start(out=t4[:, :],
                                                in_=xbar[0:16, :]),
             reads=["xbar"], writes=["tap_xbar"])
        t5 = tapout("acol", (16, 1))
        pr.d("sp", lambda e, t5=t5: e.dma_start(out=t5[:, :], in_=A_col[:, :]),
             reads=["A_col"], writes=["tap_acol"])
        t6 = tapout("hin", (128, 2))
        pr.d("sp", lambda e, t6=t6: e.dma_start(out=t6[:, :], in_=hinA[:, :]),
             reads=["hin"], writes=["tap_hin"])
        t7 = tapout("ggtok", (128, NTG))
        pr.d("sp", lambda e, t7=t7: e.dma_start(out=t7[:, :],
                                                in_=gg_tok[:, :]),
             reads=["gg_tok"], writes=["tap_ggtok"])
    if "wht" in taps:
        t = tapout("wht", (16, DIN), BF16)
        pr.d("sp", lambda e, t=t: e.dma_start(out=t[:, :], in_=wh2iT[:, :]),
             reads=["wh2iT"], writes=["tap_wht"])
        t2 = tapout("yfd", (16, TCORE), BF16)
        pr.d("sp", lambda e, t2=t2: e.dma_start(out=t2[:, :],
                                                in_=yful_bf[0:16, :]),
             reads=["yful"], writes=["tap_yfd"])
    if "outbr" in taps:
        t = tapout("outbr", (128, D))
        pr.d("sp", lambda e, t=t: e.dma_start(out=t[:, :], in_=outb_rep[:, :]),
             reads=["outb_rep"], writes=["tap_outbr"])
        t2 = tapout("oev0", (128, D))
        pr.d("sp", lambda e, t2=t2: e.dma_start(out=t2[:, :], in_=oev[0][:, :]),
             reads=["oev0q0", "oev0q1", "oev0q2", "oev0q3"],
             writes=["tap_oev0"])
    if "ktd" in taps:
        t = tapout("ktd", (128, 64, 512), BF16)
        pr.d("sp", lambda e, t=t: e.dma_start(out=t[:, :, :],
                                              in_=kT_str[:, :, :]),
             reads=["kT"], writes=["tap_ktd"])
        t2 = tapout("qst", (128, TCORE))
        pr.d("sp", lambda e, t2=t2: e.dma_start(out=t2[:, :],
                                                in_=qs_tile[:, :]),
             reads=["qs_tile"], writes=["tap_qst"])
        t3 = tapout("sot", (128, NTG))
        pr.d("sp", lambda e, t3=t3: e.dma_start(out=t3[:, :],
                                                in_=so_tok[:, :]),
             reads=["so_tok"], writes=["tap_sot"])
    if "csb0" in taps:
        t = tapout("csb0", (128, TCORE))
        pr.d("sp", lambda e, t=t: e.dma_start(out=t[:, :], in_=Csb[0][:, :]),
             reads=["Csb0"], writes=["tap_csb0"])
        t2 = tapout("bsb0", (128, TCORE))
        pr.d("sp", lambda e, t2=t2: e.dma_start(out=t2[:, :],
                                                in_=Bsb[0][:, :]),
             reads=["Bsb0"], writes=["tap_bsb0"])
        t3 = tapout("p0", (128, TCORE))
        pr.d("sp", lambda e, t3=t3: e.dma_start(out=t3[:, :], in_=P0[:, :]),
             reads=["P0"], writes=["tap_p0"])
    if "hb0" in taps:
        t = tapout("hb0", (128, TCORE))
        pr.d("sp", lambda e, t=t: e.dma_start(out=t[:, :], in_=hb[0][:, :]),
             reads=["hb0"], writes=["tap_hb0"])
    if "gvd" in taps:
        t = tapout("gvd", (64, 128, TCORE))
        pr.d("sp", lambda e, t=t: e.dma_start(out=t[:, :, :],
                                              in_=gv_scr[:, :, :]),
             reads=["gv_scr"], writes=["tap_gvd"])
    if "zd" in taps:
        t = tapout("zd", (32, 128, TCORE), BF16)
        pr.d("sp", lambda e, t=t: e.dma_start(out=t[:, :, :],
                                              in_=z_scr[:, :, :]),
             reads=["z_scr"], writes=["tap_zd"])
    if "gammas" in taps:
        t = tapout("gammas", (128, 8))
        pr.d("sp", lambda e, t=t: e.dma_start(out=t[:, :], in_=gcols[:, :]),
             reads=["gcols"], writes=["tap_gammas"])
    if "pacc" in taps:
        t = tapout("pacc", (128, 4))
        pr.d("sp", lambda e, t=t: e.dma_start(out=t[:, :], in_=pacc[:, :]),
             reads=["pacc"], writes=["tap_pacc"])
        t2 = tapout("ag1", (8, 4))
        pr.d("sp", lambda e, t2=t2: e.dma_start(out=t2[:, :],
                                                in_=misc[0:8, 48:52]),
             reads=["ag1_sb"], writes=["tap_ag1"])
    if "mgsh" in taps:
        t = tapout("mgsh", (1024, D), BF16)
        pr.d("sp", lambda e, t=t: e.dma_start(out=t[:, :], in_=mg_shard[:, :]),
             reads=["mg_shard"], writes=["tap_mgsh"])

    with nc.Block() as block:
        pr.build(nc, block, ctx)
    return nc, tap_outs


_CACHE = {}


def kernel(**inputs):
    if "nc" not in _CACHE:
        _CACHE["nc"], _ = build_nc()
    nc = _CACHE["nc"]
    in_maps = host_inputs(inputs)
    res = run_bass_kernel_spmd(nc, in_maps, core_ids=list(range(NCORES)))
    out = np.concatenate([r["out_sl"] for r in res.results], axis=0)
    return np.ascontiguousarray(out.reshape(1, SEQ, D))


# version 14 (final)
# speedup vs baseline: 1.3390x; 1.1259x over previous
from contextlib import ExitStack

import numpy as np

import concourse.bass as bass
import concourse.mybir as mybir
from concourse.bass_utils import run_bass_kernel_spmd

F32 = mybir.dt.float32
BF16 = mybir.dt.bfloat16
I32 = mybir.dt.int32
AF = mybir.ActivationFunctionType
ALU = mybir.AluOpType
AX = mybir.AxisListType

NCORES = 8
SEQ, D = 16384, 2048
TCORE = SEQ // NCORES
HALO = 8
TH = TCORE + HALO
DIN, H, NST, DH = 4096, 16, 16, 256
HID = 8192
EPS_LN = 1e-5
EPS_RMS = 1.1920929e-07
NDC = D // 128
NTB = TCORE // 512
NTG = TCORE // 128
WINP = 8960
WIN_SH = WINP // NCORES
N_WCH = 69

_PHASES = ("s0", "s1", "s2", "s3", "s4a", "s4b", "s5")
ACCUM_ON = True

ENGS = ("pe", "act", "dve", "pool", "sp")
DMA_ROT = 8


class Op:
    __slots__ = ("eng", "kind", "emit", "reads", "writes", "deps", "inc_id",
                 "dma_idx", "tag", "hz")

    def __init__(self, eng, kind, emit, reads, writes, tag="", hz=False):
        self.eng, self.kind, self.emit = eng, kind, emit
        self.reads, self.writes, self.tag = tuple(reads), tuple(writes), tag
        self.deps = self.inc_id = self.dma_idx = None
        self.hz = hz


class Prog:

    def __init__(self):
        self.ops = []

    def c(self, eng, emit, reads=(), writes=(), tag="", hz=False):
        self.ops.append(Op(eng, "c", emit, reads, writes, tag, hz))

    def d(self, queue, emit, reads=(), writes=(), tag=""):
        self.ops.append(Op(queue, "d", emit, reads, writes, tag))

    def cc(self, emit, reads=(), writes=(), tag=""):
        self.ops.append(Op("pool", "cc", emit, reads, writes, tag))

    HZ_GAP = 10**9

    def build(self, nc, block, ctx):
        ops = self.ops
        pos = {}
        ecnt = {e: 0 for e in ENGS}
        for i, op in enumerate(ops):
            pos[i] = ecnt[op.eng]
            ecnt[op.eng] += 1
        last_w, readers, need_inc = {}, {}, set()
        for i, op in enumerate(ops):
            dep = set()
            for k in op.reads:
                if k in last_w:
                    dep.add(last_w[k])
            for k in op.writes:
                if k in last_w:
                    dep.add(last_w[k])
                dep.update(readers.get(k, ()))

            def keep(j, i=i, op=op):
                oj = ops[j]
                if oj.eng != op.eng or oj.kind != "c":
                    return True
                if oj.eng == "pe":
                    return False
                if oj.hz:
                    return True
                return pos[i] - pos[j] <= self.HZ_GAP
            real = {j for j in dep if j != i and keep(j)}
            op.deps = real
            need_inc.update(real)
            for k in op.reads:
                readers.setdefault(k, []).append(i)
            for k in op.writes:
                last_w[k] = i
                readers[k] = []
        eng_ctr = {e: 0 for e in ENGS}
        dma_ctr = {e: 0 for e in ENGS}
        cc_ctr = 0
        for i, op in enumerate(ops):
            if op.kind == "d":
                op.dma_idx = dma_ctr[op.eng]
                dma_ctr[op.eng] += 1
            elif op.kind == "cc":
                cc_ctr += 1
                op.inc_id = cc_ctr
            elif i in need_inc:
                eng_ctr[op.eng] += 1
                op.inc_id = eng_ctr[op.eng]
        esem = {e: ctx.enter_context(nc.semaphore(f"e_{e}")) for e in ENGS}
        dsem = {e: [ctx.enter_context(nc.semaphore(f"d_{e}{r}"))
                    for r in range(DMA_ROT)]
                for e in ENGS if dma_ctr[e]}
        ccsem = ctx.enter_context(nc.semaphore("ccs")) if cc_ctr else None

        def tgt(j):
            oj = ops[j]
            if oj.kind == "d":
                return (dsem[oj.eng][oj.dma_idx % DMA_ROT],
                        16 * (oj.dma_idx // DMA_ROT + 1))
            if oj.kind == "cc":
                return (ccsem, oj.inc_id)
            return (esem[oj.eng], oj.inc_id)

        per_eng = {e: [i for i, op in enumerate(ops) if op.eng == e]
                   for e in ENGS}

        def mk(ename):
            idxs = per_eng[ename]

            def emitfn(eng):
                floors = {}
                for i in idxs:
                    op = ops[i]
                    waits = {}
                    for j in sorted(op.deps):
                        sem, val = tgt(j)
                        if waits.get(sem, -1) < val:
                            waits[sem] = val
                    if op.kind == "d" and op.dma_idx >= DMA_ROT:
                        sem = dsem[ename][op.dma_idx % DMA_ROT]
                        val = 16 * (op.dma_idx // DMA_ROT)
                        if waits.get(sem, -1) < val:
                            waits[sem] = val
                    for sem, val in waits.items():
                        if floors.get(sem, 0) < val:
                            eng.wait_ge(sem, val)
                            floors[sem] = val
                    ins = op.emit(eng)
                    if op.kind == "d":
                        ins.then_inc(dsem[ename][op.dma_idx % DMA_ROT], 16)
                    elif op.kind == "cc":
                        ins.then_inc(ccsem, 1)
                    elif op.inc_id is not None:
                        ins.then_inc(esem[ename], 1)
                if ename == "sp":
                    for e, n in dma_ctr.items():
                        for r in range(DMA_ROT):
                            cnt = (n - r + DMA_ROT - 1) // DMA_ROT
                            if cnt > 0:
                                eng.wait_ge(dsem[e][r], 16 * cnt)
            return emitfn

        names = {"pe": "tensor", "act": "scalar", "dve": "vector",
                 "pool": "gpsimd", "sp": "sync"}
        for e in ENGS:
            if per_eng[e]:
                getattr(block, names[e])(mk(e))


def ts(e, out, in0, s1, s2=None, op0=ALU.mult, op1=ALU.bypass):
    return e.tensor_scalar(out, in0, s1, s2, op0, op1)


def host_inputs(inputs):
    x = np.ascontiguousarray(
        np.asarray(inputs["L0_attn_x"]).reshape(SEQ, D), dtype=np.float32)
    W_in = np.asarray(inputs["L0_attn_W_in"], dtype=np.float32)
    win_perm = np.zeros((WINP, D), np.float32)
    win_perm[0:2 * DIN] = W_in[0:2 * DIN]
    win_perm[8192:8208] = W_in[8192:8208]
    win_perm[8320:8576] = W_in[8208:8464]
    win_perm[8576:8832] = W_in[8464:8720]
    gate_w = np.asarray(inputs["L0_ffn_gate_w"], dtype=np.float32)
    val_w = np.asarray(inputs["L0_ffn_val_w"], dtype=np.float32)
    out_w = np.asarray(inputs["L0_ffn_out_w"], dtype=np.float32)
    wout = np.asarray(inputs["L0_attn_W_out"], dtype=np.float32)
    small = dict(
        w_norm=np.ascontiguousarray(
            np.asarray(inputs["L0_attn_w_norm"], np.float32).reshape(1, D)),
        dt_bias=np.ascontiguousarray(
            np.asarray(inputs["L0_attn_dt_bias"], np.float32).reshape(H, 1)),
        a_log=np.ascontiguousarray(
            np.asarray(inputs["L0_attn_A_log"], np.float32).reshape(H, 1)),
        w_conv=np.ascontiguousarray(
            np.asarray(inputs["L0_attn_w_conv"], np.float32).reshape(DIN, 4)),
        w_h2i=np.ascontiguousarray(
            np.asarray(inputs["L0_attn_W_h2i"], np.float32)),
        gate_b=np.ascontiguousarray(
            np.asarray(inputs["L0_ffn_gate_b"], np.float32).reshape(64, 128)),
        val_b=np.ascontiguousarray(
            np.asarray(inputs["L0_ffn_val_b"], np.float32).reshape(64, 128)),
        out_b=np.ascontiguousarray(
            np.asarray(inputs["L0_ffn_out_b"], np.float32).reshape(1, D)),
    )
    maps = []
    for k in range(NCORES):
        lo = TCORE * k - HALO
        xs = np.zeros((TH, D), np.float32)
        s0_ = max(0, lo)
        xs[s0_ - lo:] = x[s0_:TCORE * (k + 1)]
        m = dict(small)
        m["x_sl"] = xs
        m["halo_on"] = np.full((HALO, 1), 0.0 if k == 0 else 1.0, np.float32)
        cm = np.zeros((128, 8), np.float32)
        cm[:, :k] = 1.0
        m["cmask"] = cm
        m["win_sl"] = np.ascontiguousarray(win_perm[WIN_SH * k:WIN_SH * (k + 1)])
        m["gate_sl"] = np.ascontiguousarray(gate_w[1024 * k:1024 * (k + 1)])
        m["val_sl"] = np.ascontiguousarray(val_w[1024 * k:1024 * (k + 1)])
        m["outw_sl"] = np.ascontiguousarray(out_w[256 * k:256 * (k + 1)])
        m["wout_sl"] = np.ascontiguousarray(wout[256 * k:256 * (k + 1)])
        maps.append(m)
    return maps


def build_nc(taps=(), phases=_PHASES):
    taps = set(taps)
    nc = bass.Bass("TRN2", target_bir_lowering=False)
    pr = Prog()
    ctx = ExitStack()
    nc._kernel_ctx = ctx

    def din(name, shape, dt=F32):
        return nc.dram_tensor(name, list(shape), dt, kind="ExternalInput").ap()

    def dout(name, shape, dt=F32):
        return nc.dram_tensor(name, list(shape), dt, kind="ExternalOutput").ap()

    def dint(name, shape, dt=F32, shared=False):
        return nc.dram_tensor(name, list(shape), dt, kind="Internal",
                              addr_space=("Shared" if shared else "Local")).ap()

    x_sl = din("x_sl", (TH, D))
    halo_on = din("halo_on", (HALO, 1))
    win_sl = din("win_sl", (WIN_SH, D))
    gate_sl = din("gate_sl", (1024, D))
    val_sl = din("val_sl", (1024, D))
    outw_sl = din("outw_sl", (256, HID))
    wout_sl = din("wout_sl", (256, DIN))
    w_norm = din("w_norm", (1, D))
    dt_bias = din("dt_bias", (H, 1))
    a_log = din("a_log", (H, 1))
    cmask_in = din("cmask", (128, 8))
    w_conv = din("w_conv", (DIN, 4))
    w_h2i = din("w_h2i", (DIN, H))
    gate_b = din("gate_b", (64, 128))
    val_b = din("val_b", (64, 128))
    out_b = din("out_b", (1, D))
    out_sl = dout("out_sl", (TCORE, D))

    win_shard = dint("win_shard", (WIN_SH, D), BF16)
    win_q = dint("win_q", (WINP, D), BF16, shared=True)
    mg_shard = dint("mg_shard", (1024, D), BF16)
    mg_q = dint("mg_q", (HID, D), BF16, shared=True)
    mv_shard = dint("mv_shard", (1024, D), BF16)
    mv_q = dint("mv_q", (HID, D), BF16, shared=True)
    mo_shard = dint("mo_shard", (256, HID), BF16)
    mo_q = dint("mo_q", (D, HID), BF16, shared=True)
    wo_shard = dint("wo_shard", (256, DIN), BF16)
    wo_q = dint("wo_q", (D, DIN), BF16, shared=True)
    ag1_in = dint("ag1_in", (1, 4))
    ag1_out = dint("ag1_out", (NCORES, 4), F32, shared=True)
    ag3_in = dint("ag3_in", (1, 512))
    ag3_out = dint("ag3_out", (NCORES, 512), F32, shared=True)
    z_scr = dint("z_scr", (32, 128, TCORE), BF16)
    gv_scr = dint("gv_scr", (64, 128, TCORE))
    xq_scr = dint("xq_scr", (NDC, 128, TCORE), BF16)
    row_b = dint("row_b", (1, 4096))
    ffn_scr = dint("ffn_scr", (TCORE, D))
    attn_scr = dint("attn_scr", (TCORE, D))

    tap_outs = {}

    def tapout(name, shape, dt=F32):
        tap_outs[name] = dout("tap_" + name, shape, dt)
        return tap_outs[name]

    ARENA = 202 * 1024
    _arena = nc.alloc_sbuf_tensor("arena", [128, ARENA], mybir.dt.uint8)
    _abase = nc.lookup_mloc(_arena).addr

    def sb_at(name, shape, dt, off):
        assert off + int(np.prod(shape[1:])) * mybir.dt.size(dt) <= ARENA, (
            name, off)
        return nc.alloc_sbuf_tensor_at(name, list(shape), dt,
                                       offset=_abase + off)

    class Bump:
        def __init__(self, base):
            self.cur = base

        def take(self, name, shape, dt):
            per = int(np.prod(shape[1:])) * mybir.dt.size(dt)
            t = sb_at(name, shape, dt, self.cur)
            self.cur += (per + 31) & ~31
            return t

    PB = Bump(0)
    wn_rep = PB.take("wn_rep", [128, D], F32)
    P0 = PB.take("P0", [128, TCORE], F32)
    P1 = PB.take("P1", [128, TCORE], F32)
    ident_f = PB.take("ident_f", [128, 128], F32)
    ident_bf = PB.take("ident_bf", [128, 128], BF16)
    selT_lo = PB.take("selT_lo", [128, 16], F32)
    selT_hi = PB.take("selT_hi", [128, 16], F32)
    sel_loT = PB.take("sel_loT", [16, 128], F32)
    sel_hiT = PB.take("sel_hiT", [16, 128], F32)
    ones_col = PB.take("ones_col", [128, 1], F32)
    ones_col_bf = PB.take("ones_col_bf", [128, 1], BF16)
    ones_row = PB.take("ones_row", [1, 128], F32)
    xbar = PB.take("xbar", [16, TCORE], F32)
    y_loc = PB.take("y_loc", [16, TCORE], F32)
    yful_bf = PB.take("yful_bf", [16, TCORE], BF16)
    misc = PB.take("misc", [128, 64], F32)
    gcols = PB.take("gcols", [128, 8], F32)
    sx_col = PB.take("sx_col", [128, 1], F32)
    sxi_tok = PB.take("sxi_tok", [128, NTG], F32)
    so_tok = PB.take("so_tok", [128, NTG], F32)
    gg_tok = PB.take("gg_tok", [128, NTG], F32)
    dtb_col = PB.take("dtb_col", [16, 1], F32)
    A_col = PB.take("A_col", [16, 1], F32)
    S_colA = PB.take("S_colA", [128, 2], F32)
    D_colA = PB.take("D_colA", [128, 2], F32)
    hinA = PB.take("hinA", [128, 2], F32)
    agS = PB.take("agS", [128, 32], F32)
    wc_sb = PB.take("wc_sb", [128, 128], F32)
    cmask = PB.take("cmask_sb", [128, 8], F32)
    xbar_row = PB.take("xbar_row", [1, TCORE], F32)
    dgs = [PB.take(f"dg{j}", [128, 128], BF16) for j in range(4)]
    gb_sb = PB.take("gb_sb", [128, 64], F32)
    vb_sb = PB.take("vb_sb", [128, 64], F32)
    bload = PB.take("bload", [64, 128], F32)
    _SMALLROW_OFF = PB.cur
    smallrow = PB.take("smallrow", [1, D], F32)
    cst = PB.take("cst", [128, 8], F32)
    PERS_END = PB.cur
    assert PERS_END < 70 * 1024, PERS_END

    AB = PERS_END
    xrT = sb_at("xrT", [128, NDC, TH], BF16, AB)
    xq_res = sb_at("xq_res", [128, NDC, TCORE], BF16, AB)
    kT_str = sb_at("kT_str", [128, 64, 512], BF16, AB)
    yi_str = sb_at("yi_str", [128, 32, 512], BF16, AB)
    A_END = AB + ((NDC * TH * 2 + 31) & ~31)

    BB = A_END
    B2 = Bump(BB)
    wslab = [B2.take(f"wslab{i}", [128, NDC, 256], BF16) for i in range(2)]
    B_END = B2.cur
    S1B = Bump(BB)
    xg = [S1B.take(f"xg{i}", [128, D], F32) for i in range(2)]
    xng = [S1B.take(f"xng{i}", [128, D], F32) for i in range(2)]
    xr_bf = S1B.take("xr_bf", [128, D], BF16)
    kx_bf = S1B.take("kx_bf", [128, D], BF16)
    q1_f = S1B.take("q1_f", [128, D], F32)
    k0_i = S1B.take("k0_i", [128, D], I32)
    dp_f = S1B.take("dp_f", [128, D], F32)
    kq_st = S1B.take("kq_st", [128, NDC, 128], BF16)
    kf_f = q1_f
    S4B = Bump(BB)
    moslab = [S4B.take(f"moslab{i}", [128, 1024], BF16) for i in range(3)]
    gvin = [S4B.take(f"gvin{i}", [128, 512], F32) for i in range(3)]
    woslab = [S4B.take(f"woslab{i}", [128, 1024], BF16) for i in range(3)]
    zin = [S4B.take(f"zin{i}", [128, 512], BF16) for i in range(3)]
    oev = [S4B.take(f"oev{i}", [128, D], F32) for i in range(2)]
    outb_rep = S4B.take("outb_rep", [128, D], F32)
    qs_tile = S4B.take("qs_tile", [128, TCORE], F32)
    wh2iT = S4B.take("wh2iT", [16, DIN], BF16)
    ki_b = S4B.take("ki_b", [128, 512], I32)
    qt1 = S4B.take("qt1", [128, 512], F32)
    qt2 = S4B.take("qt2", [128, 512], F32)
    zf_b = S4B.take("zf_b", [128, 512], F32)

    CB = B_END
    C2 = Bump(CB)
    Bsb = [C2.take(f"Bsb{i}", [128, TCORE], F32) for i in range(2)]
    Csb = [C2.take(f"Csb{i}", [128, TCORE], F32) for i in range(2)]
    pxi_bf = C2.take("pxi_bf", [128, TH], BF16)
    xi_bf = C2.take("xi_bf", [128, TCORE], BF16)
    zev = [C2.take(f"zev{i}", [128, 1024], BF16) for i in range(2)]
    C_END = C2.cur
    S3B = Bump(AB)
    ub = [S3B.take(f"ub{i}", [128, TCORE], F32) for i in range(2)]
    dab = [S3B.take(f"dab{i}", [128, TCORE], F32) for i in range(2)]
    hb = [S3B.take(f"hb{i}", [128, TCORE], F32) for i in range(2)]
    htmp = S3B.take("htmp", [16, TCORE], F32)
    dt_sp = S3B.take("dt_sp", [16, TCORE], F32)
    assert S3B.cur <= A_END + 1, (S3B.cur, A_END)
    cum_t = sb_at("cum_t", [16, TCORE], F32, _SMALLROW_OFF)
    S3_END = max(S3B.cur, C_END)
    S4A = Bump(CB)
    gbuf = [S4A.take("gbuf0", [128, TCORE], F32)]
    gvbuf = [S4A.take("gvbuf0", [128, TCORE], F32)]
    macc = S4A.take("macc", [128, TCORE], F32)
    babs = S4A.take("babs", [128, TCORE], F32)
    sxinv_tile = S4A.take("sxinv_tile", [128, TCORE], F32)
    PR = Bump(BB)
    pw = [PR.take(f"pw{i}", [128, D], F32) for i in range(2)]
    pq = [PR.take(f"pq{i}", [128, D], BF16) for i in range(2)]
    pacc = PR.take("pacc", [128, 4], F32)
    ptmp = PR.take("ptmp", [128, D], F32)
    ptmp2 = PR.take("ptmp2", [128, D], F32)
    LIMIT = ARENA
    for v in (S3_END, S4A.cur, PR.cur, B_END, S1B.cur, S4B.cur):
        assert v < LIMIT, (v, LIMIT)

    psA = ctx.enter_context(nc.psum_tensor("psA", [128, 2048], F32))
    psB = ctx.enter_context(nc.psum_tensor("psB", [128, 1536], F32))
    psT = ctx.enter_context(nc.psum_tensor("psT", [128, 1024], BF16))
    KA = ["psA0", "psA1", "psA2", "psA3"]
    KB = ["psB0", "psB1", "psB2"]

    def bcast_row(srow, dst, dkey, skey, nblk=4):
        for blk in range(nblk):
            pr.c("pe", lambda e, blk=blk: e.matmul(
                psA[:, 512 * blk:512 * (blk + 1)], ones_row[:, :],
                srow[0:1, 512 * blk:512 * (blk + 1)], start=True, stop=True),
                reads=[skey, "onr"], writes=[KA[blk]])
        pr.c("dve", lambda e, nblk=nblk: e.tensor_copy(
            dst[:, 0:512 * nblk], psA[:, 0:512 * nblk]),
            reads=KA[:nblk], writes=[dkey])

    def consts():
        pr.c("pool", lambda e: e.memset(ident_f[:, :], 0.0), writes=["idf"], hz=True)
        pr.c("pool", lambda e: e.affine_select(
            out=ident_f[:, :], in_=ident_f[:, :], compare_op=ALU.not_equal,
            fill=1.0, base=0, pattern=[[-1, 128]], channel_multiplier=1),
            reads=["idf"], writes=["idf"], hz=True)
        pr.c("dve", lambda e: e.tensor_copy(ident_bf[:, :], ident_f[:, :]),
             reads=["idf"], writes=["idb"])
        for t, off, key in ((selT_lo, 0, "stlo"), (selT_hi, 128, "sthi")):
            pr.c("pool", lambda e, t=t: e.memset(t[:, :], 1.0), writes=[key], hz=True)
            pr.c("pool", lambda e, t=t, off=off: e.affine_select(
                out=t[:, :], in_=t[:, :], compare_op=ALU.is_ge, fill=0.0,
                base=off, pattern=[[-16, 16]], channel_multiplier=1),
                reads=[key], writes=[key], hz=True)
            pr.c("pool", lambda e, t=t, off=off: e.affine_select(
                out=t[:, :], in_=t[:, :], compare_op=ALU.is_ge, fill=0.0,
                base=15 - off, pattern=[[16, 16]], channel_multiplier=-1),
                reads=[key], writes=[key], hz=True)
        pr.c("pe", lambda e: e.transpose(psB[0:16, 0:128], selT_lo[:, :],
                                         ident_f[:, :]),
             reads=["stlo", "idf"], writes=["psB0"])
        pr.c("dve", lambda e: e.tensor_copy(sel_loT[:, :], psB[0:16, 0:128]),
             reads=["psB0"], writes=["slT"])
        pr.c("pe", lambda e: e.transpose(psB[0:16, 512:640], selT_hi[:, :],
                                         ident_f[:, :]),
             reads=["sthi", "idf"], writes=["psB1"])
        pr.c("dve", lambda e: e.tensor_copy(sel_hiT[:, :], psB[0:16, 512:640]),
             reads=["psB1"], writes=["shT"])
        pr.c("pool", lambda e: e.memset(ones_col[:, :], 1.0), writes=["onc"], hz=True)
        pr.c("pool", lambda e: e.memset(ones_col_bf[:, :], 1.0),
             writes=["onc"], hz=True)
        pr.c("pool", lambda e: e.memset(cst[:, 0:1], EPS_LN), writes=["cst"], hz=True)
        pr.c("pool", lambda e: e.memset(cst[:, 1:2], EPS_RMS), writes=["cst"], hz=True)
        pr.c("pool", lambda e: e.memset(cst[:, 2:3], 0.0), writes=["cst"], hz=True)
        pr.c("pool", lambda e: e.memset(cst[:, 4:5], 1.0), writes=["cst"],
             hz=True)
        pr.c("pool", lambda e: e.memset(ones_row[:, :], 1.0), writes=["onr"], hz=True)
        pr.d("sp", lambda e: e.dma_start(out=cst[0:HALO, 3:4],
                                         in_=halo_on[:, :]),
             reads=["cst"], writes=["cst"], tag="halo")
        pr.d("sp", lambda e: e.dma_start(out=smallrow[0:1, :],
                                         in_=w_norm[:, :]),
             writes=["wn_row"], tag="wn")
        bcast_row(smallrow, wn_rep, "wn_rep", "wn_row")

    consts()

    def s0():
        pr.c("dve", lambda e: e.memset(pacc[:, :], 0.0), writes=["pacc"], hz=True)
        for i in range(WIN_SH // 112):
            b = i % 2
            pr.d("sp", lambda e, i=i, b=b: e.dma_start(
                out=pw[b][0:112, :], in_=win_sl[112 * i:112 * (i + 1), :]),
                writes=[f"pw{b}"], tag="s0win")
            pr.c("dve", lambda e, b=b: e.tensor_copy(pq[b][0:112, :],
                                                     pw[b][0:112, :]),
                 reads=[f"pw{b}"], writes=[f"pq{b}"])
            pr.d("act", lambda e, i=i, b=b: e.dma_start(
                out=win_shard[112 * i:112 * (i + 1), :], in_=pq[b][0:112, :]),
                reads=[f"pq{b}"], writes=["win_shard"])
        for t in range(4):
            i, ch = divmod(t, 2)
            b = t % 2
            pr.d("sp", lambda e, i=i, ch=ch, b=b: e.dma_start(
                out=pw[b][:, :], in_=wout_sl[128 * i:128 * (i + 1),
                                             D * ch:D * (ch + 1)]),
                writes=[f"pw{b}"], tag="s0wo")
            pr.c("dve", lambda e, b=b: e.tensor_copy(pq[b][:, :], pw[b][:, :]),
                 reads=[f"pw{b}"], writes=[f"pq{b}"])
            pr.d("act", lambda e, i=i, ch=ch, b=b: e.dma_start(
                out=wo_shard[128 * i:128 * (i + 1), D * ch:D * (ch + 1)],
                in_=pq[b][:, :]),
                reads=[f"pq{b}"], writes=["wo_shard"])
        gsrc = []
        for i in range(8):
            gsrc.append((0, gate_sl[128 * i:128 * (i + 1), :]))
        for i in range(8):
            gsrc.append((1, val_sl[128 * i:128 * (i + 1), :]))
        for i in range(2):
            for ch in range(4):
                gsrc.append((2, outw_sl[128 * i:128 * (i + 1),
                                        D * ch:D * (ch + 1)]))
        for n, (col, srcap) in enumerate(gsrc):
            b = n % 2
            pr.d("sp", lambda e, b=b, srcap=srcap:
                 e.dma_start(out=pw[b][:, :], in_=srcap),
                 writes=[f"pw{b}"], tag="s0gam")
            pr.c("dve", lambda e, b=b: e.tensor_reduce(
                misc[:, 40:41], pw[b][:, :], AX.X, ALU.add,
                apply_absolute_value=True),
                reads=[f"pw{b}"], writes=["gred"], hz=True)
            pr.c("dve", lambda e, col=col: e.tensor_tensor(
                pacc[:, col:col + 1], pacc[:, col:col + 1],
                misc[:, 40:41], ALU.add),
                reads=["gred", "pacc"], writes=["pacc"])
        pr.c("pe", lambda e: e.matmul(psB[0:1, 0:4], ones_col[:, :],
                                      pacc[:, 0:4], start=True, stop=True),
             reads=["pacc", "onc"], writes=["psB0"])
        pr.c("dve", lambda e: e.tensor_copy(misc[0:1, 42:46], psB[0:1, 0:4]),
             reads=["psB0"], writes=["gpart"])
        pr.d("sp", lambda e: e.dma_start(out=ag1_in[:, 0:4],
                                         in_=misc[0:1, 42:46]),
             reads=["gpart"], writes=["ag1_in"])
        pr.cc(lambda e: e.collective_compute(
            "AllGather", ALU.bypass, replica_groups=[list(range(NCORES))],
            ins=[ag1_in], outs=[ag1_out]),
            reads=["ag1_in"], writes=["ag1_out"], tag="AG1")
        pr.d("sp", lambda e: e.dma_start(out=misc[0:8, 48:52],
                                         in_=ag1_out[:, 0:4]),
             reads=["ag1_out"], writes=["ag1_sb"])
        pr.c("pe", lambda e: e.matmul(psB[0:1, 512:516], ones_col[0:8, :],
                                      misc[0:8, 48:52], start=True, stop=True),
             reads=["ag1_sb", "onc"], writes=["psB1"])
        for j, cntj in enumerate((HID * D, HID * D, D * HID)):
            pr.c("dve", lambda e, j=j, cntj=cntj: ts(
                e, misc[0:1, 52 + j:53 + j], psB[0:1, 512 + j:513 + j],
                1.0 / cntj, 1e-5, ALU.mult, ALU.max),
                reads=["psB1"], writes=["gfin"])
        pr.c("pe", lambda e: e.matmul(psB[:, 1024:1027], ones_row[:, :],
                                      misc[0:1, 52:55], start=True, stop=True),
             reads=["gfin", "onr"], writes=["psB2"])
        pr.c("dve", lambda e: e.tensor_copy(gcols[:, 0:3], psB[:, 1024:1027]),
             reads=["psB2"], writes=["gcols"])
        pr.c("dve", lambda e: ts(e, gcols[:, 3:6], gcols[:, 0:3], 0.5),
             reads=["gcols"], writes=["gcols"])

        def quant(srcap, dstap, shkey, col, n):
            b = n % 2
            pr.d("sp", lambda e, b=b, srcap=srcap:
                 e.dma_start(out=pw[b][:, :], in_=srcap),
                 writes=[f"pw{b}"], tag="s0q")
            pr.c("dve", lambda e, b=b, col=col:
                 e.tensor_scalar(ptmp[:, :], pw[b][:, :],
                                 gcols[:, 3 + col:4 + col], None, ALU.is_ge),
                 reads=[f"pw{b}", "gcols"], writes=["ptmp"])
            pr.c("dve", lambda e, b=b, col=col:
                 e.tensor_scalar(ptmp2[:, :], pw[b][:, :], -1.0,
                                 gcols[:, 3 + col:4 + col], ALU.mult,
                                 ALU.is_ge),
                 reads=[f"pw{b}", "gcols"], writes=["ptmp2"])
            pr.c("dve", lambda e, b=b:
                 e.tensor_tensor(pq[b][:, :], ptmp[:, :], ptmp2[:, :],
                                 ALU.subtract),
                 reads=["ptmp", "ptmp2"], writes=[f"pq{b}"])
            pr.d("act", lambda e, b=b, dstap=dstap: e.dma_start(
                out=dstap, in_=pq[b][:, :]),
                reads=[f"pq{b}"], writes=[shkey])
        n = 0
        for i in range(8):
            quant(gate_sl[128 * i:128 * (i + 1), :],
                  mg_shard[128 * i:128 * (i + 1), :], "mg_shard", 0, n)
            n += 1
        for i in range(8):
            quant(val_sl[128 * i:128 * (i + 1), :],
                  mv_shard[128 * i:128 * (i + 1), :], "mv_shard", 1, n)
            n += 1
        for i in range(2):
            for ch in range(4):
                quant(outw_sl[128 * i:128 * (i + 1), D * ch:D * (ch + 1)],
                      mo_shard[128 * i:128 * (i + 1), D * ch:D * (ch + 1)],
                      "mo_shard", 2, n)
                n += 1

    if "s0" in phases:
        s0()

    S1KEYS_ = ["xg0", "xg1", "xng0", "xng1", "xr_bf", "kx_bf", "q1", "k0",
               "dp", "kq_st"]

    def s1():
        pr.c("pool", lambda e: e.memset(misc[0:1, 61:62], 0.0),
             writes=["pw0", "pw1", "pq0", "pq1", "pacc", "ptmp",
                     "ptmp2"] + S1KEYS_, hz=True)
        for g in range(NTG + 1):
            b = g % 2
            rows = HALO if g == 0 else 128
            r0 = 0 if g == 0 else HALO + 128 * (g - 1)
            xgk, xngk = f"xg{b}", f"xng{b}"
            prep_keys = (["pw0", "pw1", "pq0", "pq1", "ptmp", "ptmp2", "pacc"]
                         if g <= 1 and "s0" in phases else [])
            pr.d("sp", lambda e, b=b, rows=rows, r0=r0: e.dma_start(
                out=xg[b][0:rows, :], in_=x_sl[r0:r0 + rows, :]),
                reads=prep_keys, writes=[xgk], tag="s1x")
            for q in range(4):
                pr.c("dve", lambda e, b=b, rows=rows, q=q: e.bn_stats(
                    misc[0:rows, 12 + 6 * q:18 + 6 * q],
                    xg[b][0:rows, 512 * q:512 * (q + 1)]),
                    reads=[xgk], writes=["bst"], hz=True)
            pr.c("dve", lambda e, rows=rows: e.bn_aggr(
                misc[0:rows, 36:38], misc[0:rows, 12:36]),
                reads=["bst"], writes=["mv"], hz=True)
            pr.c("act", lambda e, rows=rows: e.activation(
                misc[0:rows, 0:1], misc[0:rows, 37:38], AF.Ln,
                bias=cst[0:rows, 0:1], scale=1.0),
                reads=["mv", "cst"], writes=["lnv"])
            pr.c("act", lambda e, rows=rows: e.activation(
                misc[0:rows, 1:2], misc[0:rows, 0:1], AF.Exp,
                bias=cst[0:rows, 2:3], scale=-0.5),
                reads=["lnv", "cst"], writes=["rs"])
            pr.c("dve", lambda e, b=b, rows=rows: e.tensor_scalar(
                xng[b][0:rows, :], xg[b][0:rows, :], misc[0:rows, 36:37],
                misc[0:rows, 1:2], ALU.subtract, ALU.mult),
                reads=[xgk, "mv", "rs"], writes=[xngk])
            if g == 1 and "st2" in taps:
                t3 = tapout("st2", (128, 16))
                pr.c("dve", lambda e, t3=t3: e.tensor_copy(
                    smallrow[0:1, 0:16] if False else gg_tok[:, 0:16],
                    misc[:, 0:16]), reads=["rr", "mv", "rs", "lnm", "ms"],
                    writes=["st2cap"])
                pr.c("dve", lambda e: e.tensor_copy(
                    so_tok[:, 0:2], misc[:, 36:38]),
                    reads=["mv"], writes=["st2cap2"])
                pr.d("sp", lambda e, t3=t3: e.dma_start(
                    out=t3[:, 0:14], in_=gg_tok[:, 0:14]),
                    reads=["st2cap"], writes=["tap_st2"])
                pr.d("sp", lambda e, t3=t3: e.dma_start(
                    out=t3[:, 14:16], in_=so_tok[:, 0:2]),
                    reads=["st2cap2"], writes=["tap_st2"])
            if g == 1 and "xn1" in taps:
                t = tapout("xn1", (128, D))
                pr.d("sp", lambda e, t=t, b=b: e.dma_start(
                    out=t[:, :], in_=xng[b][:, :]),
                    reads=[xngk], writes=["tap_xn1"])
                t2 = tapout("st1", (128, 64))
                pr.d("sp", lambda e, t2=t2: e.dma_start(
                    out=t2[:, :], in_=misc[:, :]),
                    reads=["rr", "mv", "rs"], writes=["tap_st1"])
            if g > 0:
                pr.c("pool", lambda e, b=b: e.tensor_tensor(
                    xg[b][:, :], xg[b][:, :], xng[b][:, :], ALU.add),
                    reads=[xgk, xngk], writes=[xgk])
                pr.d("act", lambda e, b=b, g=g: e.dma_start(
                    out=out_sl[128 * (g - 1):128 * g, :], in_=xg[b][:, :]),
                    reads=[xgk],
                    writes=[f"outg{g - 1}q{qq}" for qq in range(4)],
                    tag="s1out")
            pr.c("dve", lambda e, rows=rows: ts(
                e, misc[0:rows, 3:4], misc[0:rows, 37:38], 1.0, EPS_LN,
                ALU.mult, ALU.add), reads=["mv"], writes=["vpe"])
            pr.c("dve", lambda e, rows=rows: e.reciprocal(
                misc[0:rows, 4:5], misc[0:rows, 3:4]),
                reads=["vpe"], writes=["r1"], hz=True)
            pr.c("dve", lambda e, rows=rows: e.tensor_tensor(
                misc[0:rows, 5:6], misc[0:rows, 37:38], misc[0:rows, 4:5],
                ALU.mult), reads=["mv", "r1"], writes=["ms"])
            pr.c("act", lambda e, rows=rows: e.activation(
                misc[0:rows, 6:7], misc[0:rows, 5:6], AF.Ln,
                bias=cst[0:rows, 1:2], scale=1.0),
                reads=["ms", "cst"], writes=["lnm"])
            pr.c("act", lambda e, rows=rows: e.activation(
                misc[0:rows, 7:8], misc[0:rows, 6:7], AF.Exp,
                bias=cst[0:rows, 2:3], scale=-0.5),
                reads=["lnm", "cst"], writes=["rr"])
            pr.c("dve", lambda e, b=b, rows=rows: e.scalar_tensor_tensor(
                xr_bf[0:rows, :], xng[b][0:rows, :], misc[0:rows, 7:8],
                wn_rep[0:rows, :], ALU.mult, ALU.mult),
                reads=[xngk, "rr", "wn_rep"], writes=["xr_bf"])
            if g == 0:
                pr.c("dve", lambda e: ts(
                    e, xr_bf[0:HALO, :], xr_bf[0:HALO, :], cst[0:HALO, 3:4],
                    None, ALU.mult),
                    reads=["xr_bf", "cst"], writes=["xr_bf"])
            for half in range(2):
                for c8 in range(8):
                    c = half * 8 + c8
                    pr.c("pe", lambda e, c=c, c8=c8: e.transpose(
                        psT[:, 128 * c8:128 * (c8 + 1)],
                        xr_bf[:, 128 * c:128 * (c + 1)], ident_bf[:, :]),
                        reads=["xr_bf", "idb"], writes=["psT"], tag="s1tr")

                def ev(e, g=g, half=half):
                    c0 = half * 8
                    src = psT[:, :].rearrange("p (c t) -> p c t", c=8)
                    if g == 0:
                        return e.tensor_copy(xrT[:, c0:c0 + 8, 0:HALO],
                                             src[:, :, 0:HALO])
                    return e.tensor_copy(
                        xrT[:, c0:c0 + 8, HALO + 128 * (g - 1):
                            HALO + 128 * g], src)
                pr.c("dve", ev, reads=["psT"], writes=["xrT"])
            if g == 0:
                continue
            pr.c("dve", lambda e, b=b: e.reduce_max(
                misc[:, 8:9], xng[b][:, :], AX.X, apply_absolute_value=True),
                reads=[xngk], writes=["gx"], hz=True)
            pr.c("dve", lambda e: ts(e, misc[:, 9:10], misc[:, 8:9], 1e-5,
                                     None, ALU.max),
                 reads=["gx"], writes=["gxc"])
            pr.c("dve", lambda e: e.reciprocal(misc[:, 10:11], misc[:, 9:10]),
                 reads=["gxc"], writes=["rgx"], hz=True)
            pr.c("dve", lambda e: ts(e, sx_col[:, 0:1], misc[:, 10:11], 127.0),
                 reads=["rgx"], writes=["sxc"])
            pr.c("dve", lambda e, g=g: ts(
                e, sxi_tok[:, g - 1:g], misc[:, 9:10], 1.0 / 127.0),
                reads=["gxc"], writes=["sxi_tok"])
            pr.c("dve", lambda e, b=b: ts(e, q1_f[:, :], xng[b][:, :],
                                          sx_col[:, 0:1]),
                 reads=[xngk, "sxc"], writes=["q1"])
            pr.c("dve", lambda e: ts(e, k0_i[:, :], q1_f[:, :], 256.0, None,
                                     ALU.add),
                 reads=["q1"], writes=["k0"])
            pr.c("dve", lambda e: e.scalar_tensor_tensor(
                dp_f[:, :], k0_i[:, :], 256.0, q1_f[:, :], ALU.subtract,
                ALU.subtract), reads=["k0", "q1"], writes=["dp"])
            pr.c("dve", lambda e: e.scalar_tensor_tensor(
                kf_f[:, :], dp_f[:, :], -0.5, k0_i[:, :], ALU.is_le, ALU.add),
                reads=["dp", "k0", "q1"], writes=["q1"])
            pr.c("dve", lambda e: ts(e, kx_bf[:, :], kf_f[:, :], 256.0, None,
                                     ALU.subtract),
                 reads=["q1"], writes=["kx_bf"])
            if g == 1 and "q11" in taps:
                for nm_, buf_, key_ in (("q11", q1_f, "q1"), ("k01", k0_i, "k0"),
                                        ("dp1", dp_f, "dp"), ("kx1", kx_bf, "kx_bf"),
                                        ("xr1", xr_bf, "xr_bf")):
                    t9 = tapout(nm_, (128, D), F32 if nm_ != "kx1" and nm_ != "xr1" else BF16)
                    if nm_ == "k01":
                        t9 = tap_outs[nm_] = dout("tap_" + nm_ + "b", (128, D), I32)
                    pr.d("sp", lambda e, t9=t9, buf_=buf_: e.dma_start(
                        out=t9[:, :], in_=buf_[:, :]),
                        reads=[key_], writes=["tap_" + nm_])
            for half in range(2):
                for c8 in range(8):
                    c = half * 8 + c8
                    pr.c("pe", lambda e, c=c, c8=c8: e.transpose(
                        psT[:, 128 * c8:128 * (c8 + 1)],
                        kx_bf[:, 128 * c:128 * (c + 1)], ident_bf[:, :]),
                        reads=["kx_bf", "idb"], writes=["psT"], tag="s1trq")

                def evq(e, half=half):
                    c0 = half * 8
                    return e.tensor_copy(
                        kq_st[:, c0:c0 + 8, :],
                        psT[:, :].rearrange("p (c t) -> p c t", c=8))
                pr.c("dve", evq, reads=["psT"], writes=["kq_st"])
            pr.d("act", lambda e, g=g: e.dma_start(
                out=xq_scr.transpose([1, 0, 2])[:, :, 128 * (g - 1):128 * g],
                in_=kq_st[:, :, :]),
                reads=["kq_st"], writes=["xq_scr"], tag="s1xq")

    if "s1" in phases:
        s1()

    def s0_ags():
        prev = ["xrT", "kq_st"]
        for nm, i_, o_, ik, ok in (
                ("AGwin", win_shard, win_q, "win_shard", "win_q"),
                ("AGmg", mg_shard, mg_q, "mg_shard", "mg_q"),
                ("AGmv", mv_shard, mv_q, "mv_shard", "mv_q"),
                ("AGmo", mo_shard, mo_q, "mo_shard", "mo_q"),
                ("AGwo", wo_shard, wo_q, "wo_shard", "wo_q")):
            pr.cc(lambda e, i_=i_, o_=o_: e.collective_compute(
                "AllGather", ALU.bypass, replica_groups=[list(range(NCORES))],
                ins=[i_], outs=[o_]),
                reads=[ik] + list(prev), writes=[ok], tag=nm)
            prev = [ok]

    if "s0" in phases:
        s0_ags()

    NQ = {0: "psA0", 1: "psA1", 2: "psA2", 3: "psA3"}
    NB = {0: "psB0", 1: "psB1", 2: "psB2"}

    def wslab_load(b, srcdram, c0, ncols, wkey):
        for dc in range(NDC):
            q = "sp" if dc % 2 == 0 else "act"
            pr.d(q, lambda e, b=b, dc=dc, c0=c0, ncols=ncols, srcdram=srcdram:
                 e.dma_start_transpose(
                     wslab[b][:, dc, 0:ncols],
                     srcdram[c0:c0 + ncols, 128 * dc:128 * (dc + 1)]),
                 writes=[wkey, "tch"], reads=["s1fence"], tag="wslab")

    S1KEYS = ["xg0", "xg1", "xng0", "xng1", "xr_bf", "kx_bf", "q1", "k0",
              "dp", "kq_st"]

    def super_mms(c0, nch, wkey, b, toks_xi):
        for ci in range(nch):
            c = c0 + ci
            for tb in range(4):
                lo = (0 if toks_xi else 8) + 512 * tb
                for dc in range(NDC):
                    pr.c("pe", lambda e, b=b, ci=ci, dc=dc, tb=tb, lo=lo:
                         e.matmul(psA[:, 512 * tb:512 * (tb + 1)],
                                  wslab[b][:, dc, 128 * ci:128 * (ci + 1)],
                                  xrT[:, dc, lo:lo + 512],
                                  start=(dc == 0), stop=(dc == NDC - 1)),
                         reads=[wkey, "xrT"], writes=[NQ[tb]],
                         tag=f"mm_c{c}")
            if toks_xi:
                for dc in range(NDC):
                    pr.c("pe", lambda e, b=b, ci=ci, dc=dc:
                         e.matmul(psB[:, 0:8],
                                  wslab[b][:, dc, 128 * ci:128 * (ci + 1)],
                                  xrT[:, dc, 2048:2056],
                                  start=(dc == 0), stop=(dc == NDC - 1)),
                         reads=[wkey, "xrT"], writes=["psB0"],
                         tag=f"mmtail_c{c}")
            yield c

    def s2():
        pr.c("pool", lambda e: e.memset(misc[0:1, 63:64], 0.0),
             reads=[], writes=["s1fence"] + S1KEYS, hz=True)
        pr.d("sp", lambda e: e.dma_start(
            out=wc_sb[:, :].rearrange("p (c j) -> p c j", j=4),
            in_=w_conv.rearrange("(c p) j -> p c j", p=128)),
            writes=["wc"], tag="wc")
        pr.d("sp", lambda e: e.dma_start(out=dtb_col[:, :], in_=dt_bias[:, :]),
             writes=["dtb"], tag="dtb")
        pr.d("sp", lambda e: e.dma_start(out=A_col[:, :], in_=a_log[:, :]),
             writes=["A_log_sb"], tag="alog")
        pr.c("act", lambda e: e.activation(A_col[:, :], A_col[:, :], AF.Exp,
                                           bias=cst[0:16, 2:3], scale=1.0),
             reads=["A_log_sb", "cst"], writes=["A_log_sb"], hz=True)
        pr.c("dve", lambda e: ts(e, A_col[:, :], A_col[:, :], -1.0),
             reads=["A_log_sb"], writes=["A_col"], hz=True)
        pr.d("sp", lambda e: e.dma_start(out=cmask[:, :], in_=cmask_in[:, :]),
             writes=["cmask"], tag="cmaskl")

        for sup in range(16):
            b = sup % 2
            wkey = f"wslab{b}"
            wslab_load(b, win_q, (32 + 2 * sup) * 128, 256, wkey)
            for c in super_mms(32 + 2 * sup, 2, wkey, b, True):
                cc = c - 32
                head = cc // 2
                for tb in range(4):
                    pr.c("dve", lambda e, tb=tb: e.tensor_copy(
                        pxi_bf[:, 512 * tb:512 * (tb + 1)],
                        psA[:, 512 * tb:512 * (tb + 1)]),
                        reads=[NQ[tb]], writes=["pxi"])
                pr.c("dve", lambda e: e.tensor_copy(
                    pxi_bf[:, 2048:2056], psB[:, 0:8]),
                    reads=["psB0"], writes=["pxi"])
                if cc == 0 and "pxi0" in taps:
                    t = tapout("pxi0", (128, TH), BF16)
                    pr.d("sp", lambda e, t=t: e.dma_start(
                        out=t[:, :], in_=pxi_bf[:, :]),
                        reads=["pxi"], writes=["tap_pxi0"])
                for j in range(4):
                    pr.c("dve", lambda e, j=j, cc=cc: ts(
                        e, dgs[j][:, :], ident_f[:, :],
                        wc_sb[:, 4 * cc + j:4 * cc + j + 1]),
                        reads=["idf", "wc"], writes=[f"dg{j}"], hz=True)
                ctgt = [(psB, 0, "psB0"), (psB, 512, "psB1"),
                        (psB, 1024, "psB2"), (psA, 0, "psA0")]
                for tb in range(4):
                    tgt, off, key = ctgt[tb]
                    for j in range(4):
                        pr.c("pe", lambda e, tb=tb, j=j, tgt=tgt, off=off:
                             e.matmul(tgt[:, off:off + 512], dgs[j][:, :],
                                      pxi_bf[:, 512 * tb + 5 + j:
                                             512 * tb + 5 + j + 512],
                                      start=(j == 0), stop=(j == 3)),
                             reads=[f"dg{j}", "pxi"], writes=[key],
                             tag=f"conv{cc}")
                    pr.c("act", lambda e, tb=tb, tgt=tgt, off=off, key=key:
                         e.activation(xi_bf[:, 512 * tb:512 * (tb + 1)],
                                      tgt[:, off:off + 512], AF.Silu,
                                      bias=cst[:, 2:3], scale=1.0),
                         reads=[key, "cst"], writes=["xi_bf"], hz=True)
                if cc == 0 and "xi0" in taps:
                    t = tapout("xi0", (128, TCORE), BF16)
                    pr.d("sp", lambda e, t=t: e.dma_start(
                        out=t[:, :], in_=xi_bf[:, :]),
                        reads=["xi_bf"], writes=["tap_xi0"])
                if cc % 2 == 0:
                    pr.c("pool", lambda e: e.memset(xbar_row[0:1, :], 0.0),
                         writes=["xbar_row"], hz=True)
                for tb in range(4):
                    pr.c("pe", lambda e, tb=tb: e.matmul(
                        psB[0:1, 0:512], ones_col_bf[:, :],
                        xi_bf[:, 512 * tb:512 * (tb + 1)],
                        start=True, stop=True),
                        reads=["onc", "xi_bf"], writes=["psB0"],
                        tag="xbarmm")
                    pr.c("dve", lambda e, tb=tb: e.tensor_tensor(
                        xbar_row[0:1, 512 * tb:512 * (tb + 1)],
                        xbar_row[0:1, 512 * tb:512 * (tb + 1)],
                        psB[0:1, 0:512], ALU.add),
                        reads=["psB0", "xbar_row"], writes=["xbar_row"])
                if cc % 2 == 1:
                    pr.d("act", lambda e: e.dma_start(
                        out=row_b[0:1, 0:TCORE], in_=xbar_row[0:1, :]),
                        reads=["xbar_row", "row_b"], writes=["row_b"],
                        tag="xbb1")
                    pr.d("sp", lambda e, head=head: e.dma_start(
                        out=xbar[head:head + 1, :], in_=row_b[0:1, 0:TCORE]),
                        reads=["row_b"], writes=["xbar"], tag="xbb2")

    def s2dt():
        for sup, (c0, nch) in enumerate(((65, 2), (67, 2), (64, 1))):
            b = sup % 2
            wkey = f"wslab{b}"
            wslab_load(b, win_q, c0 * 128, 128 * nch, wkey)
            for c in super_mms(c0, nch, wkey, b, False):
                if c == 64:
                    for tb in range(4):
                        pr.c("dve", lambda e, tb=tb: e.tensor_copy(
                            dt_sp[0:16, 512 * tb:512 * (tb + 1)],
                            psA[0:16, 512 * tb:512 * (tb + 1)]),
                            reads=[NQ[tb]],
                            writes=["dt_sp", "xrT"])
                    if "dtraw" in taps:
                        t = tapout("dtraw", (16, TCORE))
                        pr.d("sp", lambda e, t=t: e.dma_start(
                            out=t[:, :], in_=dt_sp[0:16, :]),
                            reads=["dt_sp"], writes=["tap_dtraw"])
                else:
                    dst, key = ((Bsb[c - 65], f"Bsb{c - 65}") if c < 67 else
                                (Csb[c - 67], f"Csb{c - 67}"))
                    for tb in range(4):
                        pr.c("dve", lambda e, tb=tb, dst=dst, key=key:
                             e.tensor_copy(
                                 dst[:, 512 * tb:512 * (tb + 1)],
                                 psA[:, 512 * tb:512 * (tb + 1)]),
                             reads=[NQ[tb]], writes=[key])

    def s3a():
        pr.c("dve", lambda e: ts(e, dt_sp[0:16, :], dt_sp[0:16, :],
                                 dtb_col[:, 0:1], None, ALU.add),
             reads=["dt_sp", "dtb"], writes=["dt_sp"], hz=True)
        pr.c("act", lambda e: e.activation(htmp[0:16, :], dt_sp[0:16, :],
                                           AF.Abs, bias=cst[0:16, 2:3],
                                           scale=1.0),
             reads=["dt_sp", "cst"], writes=["htmp", "xrT"], hz=True)
        pr.c("act", lambda e: e.activation(htmp[0:16, :], htmp[0:16, :],
                                           AF.Exp, bias=cst[0:16, 2:3],
                                           scale=-1.0),
             reads=["htmp", "cst"], writes=["htmp"], hz=True)
        pr.c("act", lambda e: e.activation(htmp[0:16, :], htmp[0:16, :],
                                           AF.Ln, bias=cst[0:16, 4:5],
                                           scale=1.0),
             reads=["htmp", "cst"], writes=["htmp"], hz=True)
        pr.c("dve", lambda e: ts(e, dt_sp[0:16, :], dt_sp[0:16, :], 0.0,
                                 None, ALU.max),
             reads=["dt_sp"], writes=["dt_sp"], hz=True)
        pr.c("dve", lambda e: e.tensor_tensor(dt_sp[0:16, :], dt_sp[0:16, :],
                                              htmp[0:16, :], ALU.add),
             reads=["dt_sp", "htmp"], writes=["dt_sp"], hz=True)
        pr.c("dve", lambda e: ts(e, htmp[0:16, :], dt_sp[0:16, :],
                                 A_col[:, 0:1]),
             reads=["dt_sp", "A_col"], writes=["htmp"], hz=True)
        pr.c("dve", lambda e: e.tensor_tensor_scan(
            cum_t[0:16, :], htmp[0:16, :], htmp[0:16, :], 0.0,
            ALU.add, ALU.bypass),
            reads=["htmp"], writes=["cum", "smallrow"], hz=True)
        for bt, sel, selk in ((0, sel_loT, "slT"), (1, sel_hiT, "shT")):
            for tb in range(4):
                pr.c("pe", lambda e, tb=tb, sel=sel, selk=selk: e.matmul(
                    psA[:, 512 * tb:512 * (tb + 1)], sel[:, :],
                    htmp[0:16, 512 * tb:512 * (tb + 1)],
                    start=True, stop=True),
                    reads=[selk, "htmp"], writes=[NQ[tb]], tag="bca")
            pr.c("act", lambda e, bt=bt: e.activation(
                dab[bt][:, :], psA[:, :], AF.Exp, bias=cst[:, 2:3], scale=1.0),
                reads=["psA0", "psA1", "psA2", "psA3", "cst"],
                writes=[f"dab{bt}", "xrT"], hz=True)
        pr.c("dve", lambda e: e.scalar_tensor_tensor(
            htmp[0:16, :], xbar[0:16, :], 1.0 / DH, dt_sp[0:16, :],
            ALU.mult, ALU.mult),
            reads=["xbar", "dt_sp", "cum"], writes=["htmp"], hz=True)
        for bt, sel, selk in ((0, sel_loT, "slT"), (1, sel_hiT, "shT")):
            for tb in range(4):
                pr.c("pe", lambda e, tb=tb, sel=sel, selk=selk: e.matmul(
                    psA[:, 512 * tb:512 * (tb + 1)], sel[:, :],
                    htmp[0:16, 512 * tb:512 * (tb + 1)],
                    start=True, stop=True),
                    reads=[selk, "htmp"], writes=[NQ[tb]], tag="bcw")
            pr.c("dve", lambda e, bt=bt: e.tensor_tensor(
                ub[bt][:, :], Bsb[bt][:, :], psA[:, :], ALU.mult),
                reads=["psA0", "psA1", "psA2", "psA3", f"Bsb{bt}"],
                writes=[f"ub{bt}", "xrT"], hz=True)
        if "us3" in taps:
            t = tapout("us3", (128, TCORE))
            pr.d("sp", lambda e, t=t: e.dma_start(out=t[:, :],
                                                  in_=ub[0][:, :]),
                 reads=["ub0"], writes=["tap_us3"])
            t2 = tapout("das3", (128, TCORE))
            pr.d("sp", lambda e, t2=t2: e.dma_start(out=t2[:, :],
                                                    in_=dab[0][:, :]),
                 reads=["dab0"], writes=["tap_das3"])
        for bt in range(2):
            pr.c("dve", lambda e, bt=bt: e.tensor_tensor_scan(
                hb[bt][:, :], dab[bt][:, :], ub[bt][:, :], 0.0,
                ALU.mult, ALU.add),
                reads=[f"dab{bt}", f"ub{bt}"],
                writes=[f"hb{bt}", "xrT"], hz=True)
            pr.c("dve", lambda e, bt=bt: e.tensor_copy(
                S_colA[:, bt:bt + 1], hb[bt][:, TCORE - 1:TCORE]),
                reads=[f"hb{bt}"], writes=["S_col"], hz=True)
            pr.c("dve", lambda e, bt=bt: e.tensor_tensor(
                ub[bt][:, :], hb[bt][:, :], Csb[bt][:, :], ALU.mult),
                reads=[f"hb{bt}", f"Csb{bt}"], writes=[f"ub{bt}"], hz=True)

    def s3b():
        for tb in range(4):
            pr.c("pe", lambda e, tb=tb: e.matmul(
                psA[0:16, 512 * tb:512 * (tb + 1)], selT_lo[:, :],
                ub[0][:, 512 * tb:512 * (tb + 1)], start=True, stop=False),
                reads=["stlo", "ub0"], writes=[NQ[tb]], tag="ylo")
            pr.c("pe", lambda e, tb=tb: e.matmul(
                psA[0:16, 512 * tb:512 * (tb + 1)], selT_hi[:, :],
                ub[1][:, 512 * tb:512 * (tb + 1)], start=False, stop=True),
                reads=["sthi", "ub1"], writes=[NQ[tb]], tag="yhi")
        pr.c("dve", lambda e: e.tensor_copy(y_loc[0:16, :], psA[0:16, :]),
             reads=["psA0", "psA1", "psA2", "psA3"], writes=["y_loc"],
             hz=True)
        pr.c("act", lambda e: e.activation(htmp[0:16, :], cum_t[0:16, :],
                                           AF.Exp, bias=cst[0:16, 2:3],
                                           scale=1.0),
             reads=["cum", "cst"], writes=["htmp"], hz=True)
        for bt, sel, selk, Pt, Pk in ((0, sel_loT, "slT", P0, "P0"),
                                      (1, sel_hiT, "shT", P1, "P1")):
            for tb in range(4):
                pr.c("pe", lambda e, tb=tb, sel=sel, selk=selk: e.matmul(
                    psA[:, 512 * tb:512 * (tb + 1)], sel[:, :],
                    htmp[0:16, 512 * tb:512 * (tb + 1)],
                    start=True, stop=True),
                    reads=[selk, "htmp"], writes=[NQ[tb]], tag="bcE")
            pr.c("dve", lambda e, bt=bt, Pt=Pt, Pk=Pk: e.tensor_tensor(
                Pt[:, :], Csb[bt][:, :], psA[:, :], ALU.mult),
                reads=["psA0", "psA1", "psA2", "psA3", f"Csb{bt}"],
                writes=[Pk], hz=True)
            pr.c("dve", lambda e, bt=bt: e.tensor_copy(
                D_colA[:, bt:bt + 1], psA[:, TCORE - 1:TCORE]),
                reads=["psA3"], writes=["D_col"], hz=True)
        pr.d("sp", lambda e: e.dma_start(out=ag3_in[0:1, 0:256],
                                         in_=S_colA[:, :]),
             reads=["S_col"], writes=["ag3_in"], tag="ag3s")
        pr.d("sp", lambda e: e.dma_start(out=ag3_in[0:1, 256:512],
                                         in_=D_colA[:, :]),
             reads=["D_col"], writes=["ag3_in"], tag="ag3d")
        pr.cc(lambda e: e.collective_compute(
            "AllGather", ALU.bypass, replica_groups=[list(range(NCORES))],
            ins=[ag3_in], outs=[ag3_out]),
            reads=["ag3_in", "wo_q"], writes=["ag3_out"], tag="AG3")
        pr.d("sp", lambda e: e.dma_start(
            out=agS[:, 0:16].rearrange("p (r b) -> p r b", b=2),
            in_=ag3_out[:, 0:256].rearrange("r (p b) -> p r b", p=128)),
            reads=["ag3_out"], writes=["agS"], tag="ag3u1")
        pr.d("sp", lambda e: e.dma_start(
            out=agS[:, 16:32].rearrange("p (r b) -> p r b", b=2),
            in_=ag3_out[:, 256:512].rearrange("r (p b) -> p r b", p=128)),
            reads=["ag3_out"], writes=["agS"], tag="ag3u2")
        pr.c("dve", lambda e: e.memset(hinA[:, :], 0.0), writes=["hin"],
             hz=True)
        for r in range(NCORES):
            for bt in range(2):
                pr.c("dve", lambda e, r=r, bt=bt: e.scalar_tensor_tensor(
                    misc[:, 40 + bt:41 + bt], hinA[:, bt:bt + 1],
                    agS[:, 16 + 2 * r + bt:17 + 2 * r + bt],
                    agS[:, 2 * r + bt:1 + 2 * r + bt],
                    ALU.mult, ALU.add),
                    reads=["hin", "agS"], writes=["hin_t"], hz=True)
                pr.c("dve", lambda e, bt=bt: e.tensor_tensor(
                    misc[:, 42 + bt:43 + bt], misc[:, 40 + bt:41 + bt],
                    hinA[:, bt:bt + 1], ALU.subtract),
                    reads=["hin_t", "hin"], writes=["hin_d"], hz=True)
                pr.c("dve", lambda e, r=r, bt=bt: e.scalar_tensor_tensor(
                    hinA[:, bt:bt + 1], misc[:, 42 + bt:43 + bt],
                    cmask[:, r:r + 1], hinA[:, bt:bt + 1],
                    ALU.mult, ALU.add),
                    reads=["hin_d", "cmask", "hin"], writes=["hin"], hz=True)
        for bt, Pt, Pk in ((0, P0, "P0"), (1, P1, "P1")):
            pr.c("dve", lambda e, bt=bt, Pt=Pt, Pk=Pk: ts(
                e, dab[bt][:, :], Pt[:, :], hinA[:, bt:bt + 1]),
                reads=[Pk, "hin"], writes=[f"dab{bt}"], hz=True)
        for tb in range(4):
            pr.c("pe", lambda e, tb=tb: e.matmul(
                psA[0:16, 512 * tb:512 * (tb + 1)], selT_lo[:, :],
                dab[0][:, 512 * tb:512 * (tb + 1)], start=True, stop=False),
                reads=["stlo", "dab0"], writes=[NQ[tb]], tag="yc0")
            pr.c("pe", lambda e, tb=tb: e.matmul(
                psA[0:16, 512 * tb:512 * (tb + 1)], selT_hi[:, :],
                dab[1][:, 512 * tb:512 * (tb + 1)], start=False, stop=True),
                reads=["sthi", "dab1"], writes=[NQ[tb]], tag="yc1")
        pr.c("dve", lambda e: e.tensor_tensor(
            y_loc[0:16, :], y_loc[0:16, :], psA[0:16, :], ALU.add),
            reads=["psA0", "psA1", "psA2", "psA3", "y_loc"],
            writes=["y_loc"], hz=True)
        pr.c("dve", lambda e: e.tensor_copy(yful_bf[0:16, :], y_loc[0:16, :]),
             reads=["y_loc"], writes=["yful"], hz=True)

    def s2z():
        for sup in range(16):
            b = sup % 2
            wkey = f"wslab{b}"
            wslab_load(b, win_q, 2 * sup * 128, 256, wkey)
            for ci in range(2):
                c = 2 * sup + ci
                for tb in range(4):
                    lo = 8 + 512 * tb
                    for dc in range(NDC):
                        pr.c("pe", lambda e, b=b, ci=ci, dc=dc, tb=tb, lo=lo:
                             e.matmul(psA[:, 512 * tb:512 * (tb + 1)],
                                      wslab[b][:, dc, 128 * ci:128 * (ci + 1)],
                                      xrT[:, dc, lo:lo + 512],
                                      start=(dc == 0), stop=(dc == NDC - 1)),
                             reads=[wkey, "xrT"], writes=[NQ[tb]],
                             tag=f"zmm{c}")
                for half in range(2):
                    zk = f"zev{half}"
                    for q in range(2):
                        tb = 2 * half + q
                        pr.c("dve", lambda e, tb=tb, half=half, q=q:
                             e.tensor_copy(zev[half][:, 512 * q:512 * (q + 1)],
                                           psA[:, 512 * tb:512 * (tb + 1)]),
                             reads=[NQ[tb]], writes=[zk])
                    pr.d("act", lambda e, c=c, half=half, zk=zk: e.dma_start(
                        out=z_scr[c, :, 1024 * half:1024 * (half + 1)],
                        in_=zev[half][:, :]),
                        reads=[zk], writes=["z_scr"], tag="zspill")

    if "s2" in phases:
        s2()
        s2z()
        s2dt()
    if "s3" in phases:
        s3a()
        s3b()

    def s4a():
        for srcb, dstb, kk in ((gate_b, gb_sb, "gb"), (val_b, vb_sb, "vb")):
            pr.d("sp", lambda e, srcb=srcb: e.dma_start(out=bload[:, :],
                                                        in_=srcb[:, :]),
                 reads=["bload"], writes=["bload"], tag="bld")
            pr.c("pe", lambda e: e.transpose(psB[:, 0:64], bload[:, :],
                                             ident_f[0:64, 0:64]),
                 reads=["bload", "idf"], writes=["psB0"])
            pr.c("dve", lambda e, dstb=dstb, kk=kk: e.tensor_copy(
                dstb[:, :], psB[:, 0:64]),
                reads=["psB0"], writes=[kk], hz=True)
        pr.c("pe", lambda e: e.transpose(psB[0:16, 512:640], sxi_tok[:, :],
                                         ident_f[:, :]),
             reads=["sxi_tok", "idf"], writes=["psB1"])
        pr.c("dve", lambda e: e.tensor_copy(bload[0:16, 0:128],
                                            psB[0:16, 512:640]),
             reads=["psB1", "bload"], writes=["bload"], hz=True)
        pr.d("act", lambda e: e.dma_start(
            out=row_b.rearrange("a (g p) -> (a g) p", p=128)[0:NTG, :],
            in_=bload[0:16, 0:128]),
            reads=["bload", "row_b"], writes=["row_b"], tag="sxf")
        pr.d("sp", lambda e: e.dma_start(out=smallrow[0:1, 0:TCORE],
                                         in_=row_b[0:1, 0:TCORE]),
             reads=["row_b"], writes=["smallrow", "cum"], tag="sxl")
        pr.c("pool", lambda e: e.memset(misc[0:1, 62:63], 0.0),
             writes=["pxi", "xi_bf", "zev0", "zev1", "sxinv_tile"], hz=True)
        bcast_row(smallrow, sxinv_tile, "sxinv_tile", "smallrow")
        pr.c("pool", lambda e: e.memset(macc[:, :], 0.0),
             writes=["macc", "Csb0"], hz=True)
        for dc in range(NDC):
            pr.d("sp", lambda e, dc=dc: e.dma_start(
                out=xq_res[:, dc, :], in_=xq_scr[dc, :, :]),
                writes=(["xq_res", "xrT", "ub0", "ub1", "hb0", "hb1",
                         "dab0", "dab1", "htmp", "dt_sp"] if dc == 0
                        else ["xq_res"]), tag="xqload")
        for c in range(64):
            for mat, mq, gv in ((0, mg_q, "g"), (1, mv_q, "v")):
                b = (2 * c + mat) % 2
                wkey = f"wslab{b}"
                wslab_load(b, mq, 128 * c, 128, wkey)
                for tb in range(4):
                    for dc in range(NDC):
                        pr.c("pe", lambda e, b=b, dc=dc, tb=tb:
                             e.matmul(psA[:, 512 * tb:512 * (tb + 1)],
                                      wslab[b][:, dc, 0:128],
                                      xq_res[:, dc, 512 * tb:512 * (tb + 1)],
                                      start=(dc == 0), stop=(dc == NDC - 1)),
                             reads=[wkey, "xq_res"], writes=[NQ[tb]],
                             tag=f"s4mm{gv}{c}")
                bb = 0
                if mat == 0:
                    for tb in range(4):
                        pr.c("dve", lambda e, bb=bb, tb=tb:
                             e.scalar_tensor_tensor(
                                 gbuf[bb][:, 512 * tb:512 * (tb + 1)],
                                 psA[:, 512 * tb:512 * (tb + 1)],
                                 gcols[:, 0:1],
                                 sxinv_tile[:, 512 * tb:512 * (tb + 1)],
                                 ALU.mult, ALU.mult),
                             reads=[NQ[tb], "gcols", "sxinv_tile"],
                             writes=([f"gbuf{bb}", "Bsb0"] if tb == 0 else
                                     [f"gbuf{bb}"]))
                    pr.c("act", lambda e, bb=bb, c=c: e.activation(
                        gbuf[bb][:, :], gbuf[bb][:, :], AF.Silu,
                        bias=gb_sb[:, c:c + 1], scale=1.0),
                        reads=[f"gbuf{bb}", "gb"], writes=[f"gbuf{bb}"],
                        hz=True)
                else:
                    for tb in range(4):
                        pr.c("dve", lambda e, bb=bb, tb=tb:
                             e.scalar_tensor_tensor(
                                 gvbuf[bb][:, 512 * tb:512 * (tb + 1)],
                                 psA[:, 512 * tb:512 * (tb + 1)],
                                 gcols[:, 1:2],
                                 sxinv_tile[:, 512 * tb:512 * (tb + 1)],
                                 ALU.mult, ALU.mult),
                             reads=[NQ[tb], "gcols", "sxinv_tile"],
                             writes=([f"gvbuf{bb}", "Bsb1"] if tb == 0 else
                                     [f"gvbuf{bb}"]))
                    pr.c("dve", lambda e, bb=bb, c=c: e.scalar_tensor_tensor(
                        gvbuf[bb][:, :], gvbuf[bb][:, :], vb_sb[:, c:c + 1],
                        gbuf[bb][:, :], ALU.add, ALU.mult),
                        reads=[f"gvbuf{bb}", "vb", f"gbuf{bb}"],
                        writes=[f"gvbuf{bb}"], hz=True)
                    pr.c("act", lambda e, bb=bb: e.activation(
                        babs[:, :], gvbuf[bb][:, :], AF.Abs,
                        bias=cst[:, 2:3], scale=1.0),
                        reads=[f"gvbuf{bb}", "cst"],
                        writes=["babs", "Csb1"], hz=True)
                    pr.c("dve", lambda e: e.tensor_tensor(
                        macc[:, :], macc[:, :], babs[:, :], ALU.max),
                        reads=["babs", "macc"], writes=["macc"], hz=True)
                    pr.d("act", lambda e, bb=bb, c=c: e.dma_start(
                        out=gv_scr[c, :, :], in_=gvbuf[bb][:, :]),
                        reads=[f"gvbuf{bb}"], writes=["gv_scr"], tag="gvsp")
        for blk in range(NTG):
            pr.c("pe", lambda e, blk=blk: e.transpose(
                psA[:, 0:128], macc[:, 128 * blk:128 * (blk + 1)],
                ident_f[:, :]),
                reads=["macc", "idf"], writes=["psA0"], tag="ggtr")
            pr.c("dve", lambda e, blk=blk: e.reduce_max(
                gg_tok[:, blk:blk + 1], psA[:, 0:128], AX.X),
                reads=["psA0"], writes=["gg_tok"], hz=True)
        pr.c("dve", lambda e: ts(e, gg_tok[:, :], gg_tok[:, :], 1e-5, None,
                                 ALU.max),
             reads=["gg_tok"], writes=["gg_tok"], hz=True)
        pr.c("dve", lambda e: ts(e, so_tok[:, :], gg_tok[:, :],
                                 gcols[:, 2:3], 1.0 / 127.0,
                                 ALU.mult, ALU.mult),
             reads=["gg_tok", "gcols"], writes=["so_tok"], hz=True)
        pr.c("dve", lambda e: e.reciprocal(misc[:, 44:60], gg_tok[:, :]),
             reads=["gg_tok"], writes=["qsr"], hz=True)
        pr.c("dve", lambda e: ts(e, misc[:, 44:60], misc[:, 44:60], 127.0),
             reads=["qsr"], writes=["qsr"], hz=True)
        pr.c("pe", lambda e: e.transpose(psB[0:16, 512:640], misc[:, 44:60],
                                         ident_f[:, :]),
             reads=["qsr", "idf"], writes=["psB1"])
        pr.c("dve", lambda e: e.tensor_copy(bload[0:16, 0:128],
                                            psB[0:16, 512:640]),
             reads=["psB1", "bload"], writes=["bload"], hz=True)
        pr.d("act", lambda e: e.dma_start(
            out=row_b.rearrange("a (g p) -> (a g) p", p=128)[0:NTG, :],
            in_=bload[0:16, 0:128]),
            reads=["bload", "row_b"], writes=["row_b"], tag="qsf")
        pr.d("sp", lambda e: e.dma_start(out=smallrow[0:1, 0:TCORE],
                                         in_=row_b[0:1, 0:TCORE]),
             reads=["row_b", "sxinv_tile"], writes=["smallrow"], tag="qsl")

    if "s4a" in phases:
        s4a()

    def s4b():
        pr.c("pool", lambda e: e.memset(misc[0:1, 60:61], 0.0),
             writes=["gbuf0", "gvbuf0", "macc", "babs", "sxinv_tile",
                     "wslab0", "wslab1", "xq_res",
                     "outb_rep", "qs_tile", "wh2iT", "ki_b", "qt1", "qt2",
                     "zf", "kT", "yi",
                     "moslab0", "moslab1", "moslab2",
                     "woslab0", "woslab1", "woslab2",
                     "zin0", "zin1", "zin2", "gvin0", "gvin1", "gvin2"]
             + [f"oev{ro}q{qq}" for ro in range(2) for qq in range(4)],
             hz=True)
        bcast_row(smallrow, qs_tile, "qs_tile", "smallrow")
        pr.d("sp", lambda e: e.dma_start(out=smallrow[0:1, 0:D],
                                         in_=out_b[:, :]),
             reads=["smallrow", "qs_tile"], writes=["smallrow"], tag="obl")
        bcast_row(smallrow, outb_rep, "outb_rep", "smallrow")
        for icb in range(8):
            pr.d("sp", lambda e, icb=icb: e.dma_start(
                out=zf_b[0:16, 0:512].rearrange("p (c f) -> p c f", c=4),
                in_=w_h2i.rearrange("(c f) h -> h c f", f=128)[
                    :, 4 * icb:4 * icb + 4, :],
                allow_slow_non_contiguous=True),
                reads=["zf"], writes=["zf"], tag="wh2il")
            pr.c("dve", lambda e, icb=icb: e.tensor_copy(
                wh2iT[:, 512 * icb:512 * (icb + 1)], zf_b[0:16, 0:512]),
                reads=["zf"], writes=["wh2iT"], hz=True)
        for s_ in range(4):
            for hc in range(64):
                r = hc % 3
                pr.d("sp", lambda e, s_=s_, hc=hc, r=r: e.dma_start(
                    out=gvin[r][:, :],
                    in_=gv_scr[hc, :, 512 * s_:512 * (s_ + 1)]),
                    writes=([f"gvin{r}", "wslab0", "wslab1"]
                            if hc < 3 and s_ == 0 else [f"gvin{r}"]),
                    tag="gvld")
                pr.c("pool", lambda e, s_=s_, r=r: e.tensor_tensor(
                    qt1[:, :], gvin[r][:, :],
                    qs_tile[:, 512 * s_:512 * (s_ + 1)], ALU.mult),
                    reads=[f"gvin{r}", "qs_tile"], writes=["qt1"], hz=True)
                pr.c("dve", lambda e: ts(e, ki_b[:, :], qt1[:, :], 256.0,
                                         None, ALU.add),
                     reads=["qt1"], writes=["ki"], hz=True)
                pr.c("dve", lambda e: e.scalar_tensor_tensor(
                    qt2[:, :], ki_b[:, :], 256.0, qt1[:, :],
                    ALU.subtract, ALU.subtract),
                    reads=["ki", "qt1"], writes=["qt2"], hz=True)
                pr.c("dve", lambda e: e.scalar_tensor_tensor(
                    qt1[:, :], qt2[:, :], -0.5, ki_b[:, :],
                    ALU.is_le, ALU.add),
                    reads=["qt2", "ki"], writes=["qt1"], hz=True)
                pr.c("act", lambda e, hc=hc: e.activation(
                    kT_str[:, hc, :], qt1[:, :], AF.Copy, bias=-256.0,
                    scale=1.0),
                    reads=["qt1"],
                    writes=(["kT", "xq_res"] if hc == 0 and s_ == 0
                            else ["kT"]), hz=True)
            for qd in range(4):
                for hc in range(64):
                    r = hc % 3
                    pr.d("sp",
                         lambda e, qd=qd, hc=hc, r=r:
                         e.dma_start_transpose(
                             moslab[r][:, 0:512],
                             mo_q[512 * qd:512 * (qd + 1),
                                  128 * hc:128 * (hc + 1)]),
                         writes=([f"moslab{r}", "wslab0", "wslab1"]
                                 if s_ == 0 and qd == 0 and hc < 3
                                 else [f"moslab{r}"]), tag="mold")
                    for tsub in range(4):
                        pr.c("pe", lambda e, hc=hc, tsub=tsub, r=r: e.matmul(
                            psA[:, 512 * tsub:512 * (tsub + 1)],
                            kT_str[:, hc, 128 * tsub:128 * (tsub + 1)],
                            moslab[r][:, 0:512],
                            start=(hc == 0), stop=(hc == 63)),
                            reads=[f"moslab{r}", "kT"], writes=[NQ[tsub]],
                            tag=f"s4bmm{qd}")
                for tsub in range(4):
                    ro = tsub % 2
                    pr.c("dve", lambda e, s_=s_, qd=qd, tsub=tsub, ro=ro:
                         e.scalar_tensor_tensor(
                             oev[ro][:, 512 * qd:512 * (qd + 1)],
                             psA[:, 512 * tsub:512 * (tsub + 1)],
                             so_tok[:, 4 * s_ + tsub:4 * s_ + tsub + 1],
                             outb_rep[:, 512 * qd:512 * (qd + 1)],
                             ALU.mult, ALU.add),
                         reads=[NQ[tsub], "so_tok", "outb_rep"],
                         writes=[f"oev{ro}q{qd}"])
                    if s_ == 0 and qd == 0 and tsub == 0 and "oevf" in taps:
                        t = tapout("oevf", (128, 512))
                        pr.d("sp", lambda e, t=t: e.dma_start(
                            out=t[:, :], in_=oev[0][:, 0:512]),
                            reads=["oev0q0"], writes=["tap_oevf"])
                    pr.d("pool", lambda e, s_=s_, qd=qd, tsub=tsub, ro=ro:
                         e.dma_start(
                             out=ffn_scr[512 * s_ + 128 * tsub:
                                         512 * s_ + 128 * (tsub + 1),
                                         512 * qd:512 * (qd + 1)],
                             in_=oev[ro][:, 512 * qd:512 * (qd + 1)]),
                         reads=[f"oev{ro}q{qd}"],
                         writes=[f"ffng{4 * s_ + tsub}q{qd}"], tag="oacc")

    if "s4b" in phases:
        s4b()

    def s5():
        for s_ in range(4):
            for ic in range(32):
                r = ic % 3
                pr.d("sp", lambda e, s_=s_, ic=ic, r=r: e.dma_start(
                    out=zin[r][:, :],
                    in_=z_scr[ic, :, 512 * s_:512 * (s_ + 1)]),
                    writes=[f"zin{r}"], tag="zld")
                pr.c("pe", lambda e, s_=s_, ic=ic: e.matmul(
                    psB[:, 512:1024], wh2iT[:, 128 * ic:128 * (ic + 1)],
                    yful_bf[0:16, 512 * s_:512 * (s_ + 1)],
                    start=True, stop=True),
                    reads=["wh2iT", "yful"], writes=["psB1"], tag="y1mm")
                pr.c("act", lambda e, r=r: e.activation(
                    zf_b[:, :], zin[r][:, :], AF.Silu, bias=cst[:, 2:3],
                    scale=1.0),
                    reads=[f"zin{r}", "cst"], writes=["zf"], hz=True)
                pr.c("dve", lambda e, ic=ic: e.tensor_tensor(
                    yi_str[:, ic, :], zf_b[:, :], psB[:, 512:1024],
                    ALU.mult),
                    reads=["zf", "psB1"],
                    writes=(["yi", "kT"] if ic == 0 else ["yi"]), hz=True)
            for qd in range(4):
                for ic in range(32):
                    r = ic % 3
                    pr.d("sp",
                         lambda e, qd=qd, ic=ic, r=r:
                         e.dma_start_transpose(
                             woslab[r][:, 0:512],
                             wo_q[512 * qd:512 * (qd + 1),
                                  128 * ic:128 * (ic + 1)]),
                         writes=([f"woslab{r}", "wslab1"]
                                 if qd == 0 and ic < 3
                                 else [f"woslab{r}"]), tag="wold")
                    for tsub in range(4):
                        pr.c("pe", lambda e, ic=ic, tsub=tsub, r=r: e.matmul(
                            psA[:, 512 * tsub:512 * (tsub + 1)],
                            yi_str[:, ic, 128 * tsub:128 * (tsub + 1)],
                            woslab[r][:, 0:512],
                            start=(ic == 0), stop=(ic == 31)),
                            reads=[f"woslab{r}", "yi"], writes=[NQ[tsub]],
                            tag=f"s5mm{qd}")
                for tsub in range(4):
                    ro = tsub % 2
                    g = 4 * s_ + tsub
                    ob, obk = (qt1, "qt1") if ro == 0 else (gvin[0], "gvin0")
                    fb, fbk = (qt2, "qt2") if ro == 0 else (gvin[1], "gvin1")
                    rows = slice(512 * s_ + 128 * tsub,
                                 512 * s_ + 128 * (tsub + 1))
                    cols = slice(512 * qd, 512 * (qd + 1))
                    pr.d("sp", lambda e, ob=ob, rows=rows, cols=cols:
                         e.dma_start(out=ob[:, :], in_=out_sl[rows, cols]),
                         reads=[f"outg{g}q{qd}"], writes=[obk], tag="obld")
                    pr.d("act", lambda e, fb=fb, rows=rows, cols=cols:
                         e.dma_start(out=fb[:, :], in_=ffn_scr[rows, cols]),
                         reads=[f"ffng{g}q{qd}"], writes=[fbk], tag="fbld")
                    pr.c("dve", lambda e, qd=qd, tsub=tsub, ro=ro, ob=ob:
                         e.tensor_tensor(
                             oev[ro][:, 512 * qd:512 * (qd + 1)],
                             psA[:, 512 * tsub:512 * (tsub + 1)],
                             ob[:, :], ALU.add),
                         reads=[NQ[tsub], obk], writes=[f"oev{ro}q{qd}"])
                    pr.c("dve", lambda e, qd=qd, ro=ro, fb=fb:
                         e.tensor_tensor(
                             oev[ro][:, 512 * qd:512 * (qd + 1)],
                             oev[ro][:, 512 * qd:512 * (qd + 1)],
                             fb[:, :], ALU.add),
                         reads=[f"oev{ro}q{qd}", fbk],
                         writes=[f"oev{ro}q{qd}"], hz=True)
                    pr.d("pool", lambda e, qd=qd, ro=ro, rows=rows, cols=cols:
                         e.dma_start(out=out_sl[rows, cols],
                                     in_=oev[ro][:, 512 * qd:512 * (qd + 1)]),
                         reads=[f"oev{ro}q{qd}"],
                         writes=[f"outg{g}q{qd}"], tag="ost")

    if "s5" in phases:
        s5()

    if "xrT" in taps:
        t = tapout("xrT", (128, NDC, TH), BF16)
        pr.d("sp", lambda e, t=t: e.dma_start(out=t[:, :, :], in_=xrT[:, :, :]),
             reads=["xrT"], writes=["tap_xrT"])
    if "xq" in taps:
        t = tapout("xq", (NDC, 128, TCORE), BF16)
        pr.d("sp", lambda e, t=t: e.dma_start(out=t[:, :, :], in_=xq_scr[:, :, :]),
             reads=["xq_scr"], writes=["tap_xq"])
    if "sxinv" in taps:
        t = tapout("sxinv", (128, NTG))
        pr.d("sp", lambda e, t=t: e.dma_start(out=t[:, :], in_=sxi_tok[:, :]),
             reads=["sxi_tok"], writes=["tap_sxinv"])
    if "winq" in taps:
        t = tapout("winq", (WINP, D), BF16)
        pr.d("sp", lambda e, t=t: e.dma_start(out=t[:, :], in_=win_q[:, :]),
             reads=["win_q"], writes=["tap_winq"])
    if "mgq" in taps:
        t = tapout("mgq", (HID, D), BF16)
        pr.d("sp", lambda e, t=t: e.dma_start(out=t[:, :], in_=mg_q[:, :]),
             reads=["mg_q"], writes=["tap_mgq"])
    if "moq" in taps:
        t = tapout("moq", (D, HID), BF16)
        pr.d("sp", lambda e, t=t: e.dma_start(out=t[:, :], in_=mo_q[:, :]),
             reads=["mo_q"], writes=["tap_moq"])
    if "ffnscr" in taps:
        t = tapout("ffnscr", (TCORE, D))
        pr.d("sp", lambda e, t=t: e.dma_start(out=t[:, :], in_=ffn_scr[:, :]),
             reads=[f"ffng{gg}q{qq}" for gg in range(NTG)
                    for qq in range(4)],
             writes=["tap_ffnscr"])
    if "yl" in taps:
        t = tapout("yl", (16, TCORE))
        pr.d("sp", lambda e, t=t: e.dma_start(out=t[:, :], in_=y_loc[0:16, :]),
             reads=["y_loc"], writes=["tap_yl"])
        t2 = tapout("cum", (16, TCORE))
        pr.d("sp", lambda e, t2=t2: e.dma_start(out=t2[:, :],
                                                in_=cum_t[0:16, :]),
             reads=["cum"], writes=["tap_cum"])
        t3 = tapout("dtsp", (16, TCORE))
        pr.d("sp", lambda e, t3=t3: e.dma_start(out=t3[:, :],
                                                in_=dt_sp[0:16, :]),
             reads=["dt_sp"], writes=["tap_dtsp"])
        t4 = tapout("xbar", (16, TCORE))
        pr.d("sp", lambda e, t4=t4: e.dma_start(out=t4[:, :],
                                                in_=xbar[0:16, :]),
             reads=["xbar"], writes=["tap_xbar"])
        t5 = tapout("acol", (16, 1))
        pr.d("sp", lambda e, t5=t5: e.dma_start(out=t5[:, :], in_=A_col[:, :]),
             reads=["A_col"], writes=["tap_acol"])
        t6 = tapout("hin", (128, 2))
        pr.d("sp", lambda e, t6=t6: e.dma_start(out=t6[:, :], in_=hinA[:, :]),
             reads=["hin"], writes=["tap_hin"])
        t7 = tapout("ggtok", (128, NTG))
        pr.d("sp", lambda e, t7=t7: e.dma_start(out=t7[:, :],
                                                in_=gg_tok[:, :]),
             reads=["gg_tok"], writes=["tap_ggtok"])
    if "wht" in taps:
        t = tapout("wht", (16, DIN), BF16)
        pr.d("sp", lambda e, t=t: e.dma_start(out=t[:, :], in_=wh2iT[:, :]),
             reads=["wh2iT"], writes=["tap_wht"])
        t2 = tapout("yfd", (16, TCORE), BF16)
        pr.d("sp", lambda e, t2=t2: e.dma_start(out=t2[:, :],
                                                in_=yful_bf[0:16, :]),
             reads=["yful"], writes=["tap_yfd"])
    if "outbr" in taps:
        t = tapout("outbr", (128, D))
        pr.d("sp", lambda e, t=t: e.dma_start(out=t[:, :], in_=outb_rep[:, :]),
             reads=["outb_rep"], writes=["tap_outbr"])
        t2 = tapout("oev0", (128, D))
        pr.d("sp", lambda e, t2=t2: e.dma_start(out=t2[:, :], in_=oev[0][:, :]),
             reads=["oev0q0", "oev0q1", "oev0q2", "oev0q3"],
             writes=["tap_oev0"])
    if "ktd" in taps:
        t = tapout("ktd", (128, 64, 512), BF16)
        pr.d("sp", lambda e, t=t: e.dma_start(out=t[:, :, :],
                                              in_=kT_str[:, :, :]),
             reads=["kT"], writes=["tap_ktd"])
        t2 = tapout("qst", (128, TCORE))
        pr.d("sp", lambda e, t2=t2: e.dma_start(out=t2[:, :],
                                                in_=qs_tile[:, :]),
             reads=["qs_tile"], writes=["tap_qst"])
        t3 = tapout("sot", (128, NTG))
        pr.d("sp", lambda e, t3=t3: e.dma_start(out=t3[:, :],
                                                in_=so_tok[:, :]),
             reads=["so_tok"], writes=["tap_sot"])
    if "csb0" in taps:
        t = tapout("csb0", (128, TCORE))
        pr.d("sp", lambda e, t=t: e.dma_start(out=t[:, :], in_=Csb[0][:, :]),
             reads=["Csb0"], writes=["tap_csb0"])
        t2 = tapout("bsb0", (128, TCORE))
        pr.d("sp", lambda e, t2=t2: e.dma_start(out=t2[:, :],
                                                in_=Bsb[0][:, :]),
             reads=["Bsb0"], writes=["tap_bsb0"])
        t3 = tapout("p0", (128, TCORE))
        pr.d("sp", lambda e, t3=t3: e.dma_start(out=t3[:, :], in_=P0[:, :]),
             reads=["P0"], writes=["tap_p0"])
    if "hb0" in taps:
        t = tapout("hb0", (128, TCORE))
        pr.d("sp", lambda e, t=t: e.dma_start(out=t[:, :], in_=hb[0][:, :]),
             reads=["hb0"], writes=["tap_hb0"])
    if "gvd" in taps:
        t = tapout("gvd", (64, 128, TCORE))
        pr.d("sp", lambda e, t=t: e.dma_start(out=t[:, :, :],
                                              in_=gv_scr[:, :, :]),
             reads=["gv_scr"], writes=["tap_gvd"])
    if "zd" in taps:
        t = tapout("zd", (32, 128, TCORE), BF16)
        pr.d("sp", lambda e, t=t: e.dma_start(out=t[:, :, :],
                                              in_=z_scr[:, :, :]),
             reads=["z_scr"], writes=["tap_zd"])
    if "gammas" in taps:
        t = tapout("gammas", (128, 8))
        pr.d("sp", lambda e, t=t: e.dma_start(out=t[:, :], in_=gcols[:, :]),
             reads=["gcols"], writes=["tap_gammas"])
    if "pacc" in taps:
        t = tapout("pacc", (128, 4))
        pr.d("sp", lambda e, t=t: e.dma_start(out=t[:, :], in_=pacc[:, :]),
             reads=["pacc"], writes=["tap_pacc"])
        t2 = tapout("ag1", (8, 4))
        pr.d("sp", lambda e, t2=t2: e.dma_start(out=t2[:, :],
                                                in_=misc[0:8, 48:52]),
             reads=["ag1_sb"], writes=["tap_ag1"])
    if "mgsh" in taps:
        t = tapout("mgsh", (1024, D), BF16)
        pr.d("sp", lambda e, t=t: e.dma_start(out=t[:, :], in_=mg_shard[:, :]),
             reads=["mg_shard"], writes=["tap_mgsh"])

    with nc.Block() as block:
        pr.build(nc, block, ctx)
    return nc, tap_outs


_CACHE = {}


def kernel(**inputs):
    if "nc" not in _CACHE:
        _CACHE["nc"], _ = build_nc()
    nc = _CACHE["nc"]
    in_maps = host_inputs(inputs)
    res = run_bass_kernel_spmd(nc, in_maps, core_ids=list(range(NCORES)))
    out = np.concatenate([r["out_sl"] for r in res.results], axis=0)
    return np.ascontiguousarray(out.reshape(1, SEQ, D))
